# version 05
# baseline (speedup 1.0000x reference)
import contextlib
import numpy as np
import concourse.bass as bass
import concourse.mybir as mybir

F32 = mybir.dt.float32
BF16 = mybir.dt.bfloat16
I32 = mybir.dt.int32
AF = mybir.ActivationFunctionType
ALU = mybir.AluOpType
AX = mybir.AxisListType

ENGINES = ("sync", "tensor", "vector", "scalar", "gpsimd")


class Sem:
    __slots__ = ("name", "count", "h")

    def __init__(self, name, h):
        self.name = name
        self.count = 0
        self.h = h


class Prog:

    def __init__(self, nc, es):
        self.nc = nc
        self.es = es
        self.ops = []

    def sem(self, name):
        return Sem(name, self.es.enter_context(self.nc.semaphore(name)))

    def sbuf(self, name, shape, dtype):
        return self.es.enter_context(self.nc.sbuf_tensor(name, shape, dtype))

    def psum(self, name, shape, dtype=F32):
        return self.es.enter_context(self.nc.psum_tensor(name, shape, dtype))

    def op(self, engine, fn, waits=(), inc=None, incs=None):
        assert engine in ENGINES, engine
        for s, v in waits:
            assert v <= s.count, (s.name, v, s.count, "wait on future sem value")
        ilist = list(incs) if incs else ([] if inc is None else [inc])
        for s, n in ilist:
            s.count += n
        self.ops.append((engine, tuple(waits), fn, ilist))

    def dma(self, engine, out, in_, waits=(), inc=None):
        def fn(eng, out=out, in_=in_):
            return eng.dma_start(out=out, in_=in_)
        self.op(engine, fn, waits=waits, inc=inc)

    def dma_ch(self, engine, out, in_, sem, waits=()):
        def fn(eng, out=out, in_=in_):
            return eng.dma_start(out=out, in_=in_)
        self.op(engine, fn, waits=waits, inc=(sem, 16))

    def build(self):
        with self.nc.Block() as block:
            for name in ENGINES:
                ops = [o for o in self.ops if o[0] == name]
                if not ops:
                    continue

                def make(ops):
                    def emit(eng):
                        for _, waits, fn, ilist in ops:
                            for s, v in waits:
                                eng.wait_ge(s.h, v)
                            ins = fn(eng)
                            for s, n in ilist:
                                ins = ins.then_inc(s.h, n)
                    return emit

                getattr(block, name)(make(ops))


def ceil_div(a, b):
    return -(-a // b)



RSQRT_HD = 0.08838834764831845
EPS_LN = 1e-5
EPS_RMS = 1.1920929e-07


class L1:
    def __init__(self, T=1024, TT=512, D=2048, H=8192, NH=16, NKV=4):
        self.__dict__.update(T=T, TT=TT, D=D, H=H, NH=NH, NKV=NKV)
        self.CD, self.CH, self.NT = D // 128, H // 128, T // TT
        self.O3 = (NH + 2 * NKV) * 128

    def build(self):
        T, TT, D, H, NH, NKV = self.T, self.TT, self.D, self.H, self.NH, self.NKV
        CD, O3 = self.CD, self.O3
        nc = bass.Bass("TRN2", target_bir_lowering=False, debug=False,
                       detect_race_conditions=False)

        dr = {}
        dr["x_fm"] = nc.dram_tensor("x_fm", [D, T], F32, kind="ExternalInput").ap()
        dr["wgT"] = nc.dram_tensor("wgT", [D, H], F32, kind="ExternalInput").ap()
        dr["wuT"] = nc.dram_tensor("wuT", [D, H], F32, kind="ExternalInput").ap()
        dr["wdT"] = nc.dram_tensor("wdT", [H, D], F32, kind="ExternalInput").ap()
        dr["wqkvT"] = nc.dram_tensor("wqkvT", [D, O3], F32, kind="ExternalInput").ap()
        dr["cos_t"] = nc.dram_tensor("cos_t", [128, T], F32, kind="ExternalInput").ap()
        dr["sin_t"] = nc.dram_tensor("sin_t", [128, T], F32, kind="ExternalInput").ap()
        dr["rotc_in"] = nc.dram_tensor("rotc_in", [128, 128], F32, kind="ExternalInput").ap()
        dr["gain_in"] = nc.dram_tensor("gain_in", [1, NH], F32, kind="ExternalInput").ap()
        dr["eye_in"] = nc.dram_tensor("eye_in", [128, 128], F32, kind="ExternalInput").ap()
        dr["sel16_in"] = nc.dram_tensor("sel16_in", [16, 16 * 128], BF16, kind="ExternalInput").ap()
        dr["selc_in"] = nc.dram_tensor("selc_in", [128, 2 * CD - 1], BF16, kind="ExternalInput").ap()

        dr["x1_out"] = nc.dram_tensor("x1_out", [D, T], F32, kind="ExternalOutput").ap()
        dr["q_out"] = nc.dram_tensor("q_out", [NH, 128, T], F32, kind="ExternalOutput").ap()
        dr["k_out"] = nc.dram_tensor("k_out", [NKV, 128, T], F32, kind="ExternalOutput").ap()
        dr["v_out"] = nc.dram_tensor("v_out", [NKV, T, 128], F32, kind="ExternalOutput").ap()
        if getattr(self, "debug_weff", False):
            dr["dbg_weff"] = nc.dram_tensor("dbg_weff", [self.O3 // 256, 128, 16 * 256], F32,
                                            kind="ExternalOutput").ap()
            dr["dbg_wb"] = nc.dram_tensor("dbg_wb", [self.O3 // 256, 128, 16 * 256], F32,
                                          kind="ExternalOutput").ap()

        with contextlib.ExitStack() as es:
            p = Prog(nc, es)
            self._body(nc, p, dr)
            p.build()
        return nc

    def _body(self, nc, p, dr):
        T, TT, D, H, NH, NKV = self.T, self.TT, self.D, self.H, self.NH, self.NKV
        CD, CH, NT, O3 = self.CD, self.CH, self.NT, self.O3
        SLAB = 256
        NSLAB = O3 // SLAB

        xT = p.sbuf("xT", [128, CD * T], F32)
        h1 = p.sbuf("h1", [128, CD * TT], F32)
        act = p.sbuf("act", [128, 2 * TT], F32)
        y_acc = p.sbuf("y_acc", [128, CD * TT], F32)
        wg_s = p.sbuf("wg_s", [128, 2 * D], F32)
        wu_s = p.sbuf("wu_s", [128, 2 * D], F32)
        wd_s = p.sbuf("wd_s", [128, 2 * D], F32)
        lnss = p.sbuf("lnss", [128, TT], F32)
        asc = p.sbuf("asc", [128, TT], F32)
        wst = p.sbuf("wst", [128, 2 * D], F32)
        h1b = h1.bitcast(BF16)
        actb = act.bitcast(BF16)
        wgb = wg_s.bitcast(BF16)
        wub = wu_s.bitcast(BF16)
        wdb = wd_s.bitcast(BF16)
        xTb = xT.bitcast(BF16)

        sq = p.sbuf("sq", [128, TT], F32)
        cos2 = wst[:, 0:T]
        sin2 = wst[:, T:2 * T]
        rotc = wst[:, 2 * T:2 * T + 128]
        eye_s = wst[:, 2 * T + 128:2 * T + 256]
        stats = p.sbuf("stats", [128, 2 * TT], F32)
        consts = p.sbuf("consts", [128, 144], F32)
        gain_s = p.sbuf("gain_s", [1, NH], F32)
        selcol = p.sbuf("selcol", [128, 2 * CD - 1], BF16)

        wbb = y_acc[:, 0:CD * SLAB // 2].bitcast(BF16)
        wab = y_acc[:, 2048:4096].bitcast(BF16)
        rb = y_acc[:, 4096:4224].bitcast(BF16)
        g1 = y_acc[:, 4224:4480]
        g2 = y_acc[:, 4480:4736]
        qraw = y_acc[:, 4736:5248]
        ro = y_acc[:, 5248:5760]
        sclb = y_acc[0:16, 6784:6912].bitcast(BF16)
        inv_bc = y_acc[:, 6912:7168]
        sel16 = y_acc[0:16, 5760:6784].bitcast(BF16)

        ps = [p.psum(f"ps{i}", [128, 512], F32) for i in range(8)]
        PS_GATE, PS_UP, PS_D0, PS_D1, PS_S0, PS_S1, PS_B0, PS_B1 = range(8)

        s_in = p.sem("s_in")
        s_gu = [p.sem("s_gu0"), p.sem("s_gu1")]
        s_wd = [p.sem("s_wd0"), p.sem("s_wd1")]
        s_st = [p.sem("s_st0"), p.sem("s_st1")]
        s_x1 = p.sem("s_x1")
        s_out = p.sem("s_out")
        s_dbg = p.sem("s_dbg")
        mm = p.sem("mms")
        vv = p.sem("vvs")
        aa = p.sem("aas")

        self._sqpar = 0
        self._sq_marks = {}
        self._free = {}
        self._weff_dump = {}

        p.dma_ch("sync", xT[:, :].rearrange("p (c t) -> p c t", c=CD),
                 dr["x_fm"].rearrange("(c p) t -> p c t", p=128), s_in)
        p.dma_ch("sync", gain_s[:, :], dr["gain_in"], s_in)
        p.dma_ch("sync", selcol[:, :], dr["selc_in"], s_in)
        lds0 = s_in.count
        p.op("vector", lambda eng: eng.memset(consts[:, 0:1], 1.0), inc=(vv, 1))
        p.op("vector", lambda eng: eng.memset(consts[0:1, 1:129], 1.0), inc=(vv, 1))
        consts_ready = vv.count
        ones_col = consts[:, 0:1]
        ones_row = consts[0:1, 1:129]

        A = stats[0:1, 0:TT]
        B = stats[0:1, TT:2 * TT]
        C = stats[0:1, 0:TT]
        NSC = lnss[0:1, 0:TT]

        def rstd_from_B(extra_eps, width):
            p.op("vector", lambda eng: eng.tensor_scalar_add(
                B[:, 0:width], B[:, 0:width], extra_eps), inc=(vv, 1))
            p.op("vector", lambda eng: eng.reciprocal(
                C[:, 0:width], B[:, 0:width]), inc=(vv, 1))
            p.op("scalar", lambda eng: eng.activation(
                C[:, 0:width], C[:, 0:width], AF.Sqrt),
                waits=[(vv, vv.count)], inc=(aa, 1))
            p.op("scalar", lambda eng: eng.activation(
                NSC[:, 0:width], C[:, 0:width], AF.Square), inc=(aa, 1))
            p.op("vector", lambda eng: eng.tensor_mul(
                NSC[:, 0:width], NSC[:, 0:width], B[:, 0:width]),
                waits=[(aa, aa.count)], inc=(vv, 1))
            p.op("vector", lambda eng: eng.tensor_scalar(
                NSC[:, 0:width], NSC[:, 0:width], -0.5, 1.5, ALU.mult, ALU.add),
                inc=(vv, 1))
            p.op("vector", lambda eng: eng.tensor_mul(
                C[:, 0:width], C[:, 0:width], NSC[:, 0:width]), inc=(vv, 1))
            return vv.count

        def ln(src_view, dst_view, eps, wait0=(), n_chunks=CD, width=TT, split=None):
            ndim = n_chunks * 128
            w0 = list(wait0) + [(vv, consts_ready)]
            for c in range(n_chunks):
                wsq = [(mm, self._sq_marks[0])] if 0 in self._sq_marks else []
                p.op("scalar", lambda eng, c=c: eng.activation(
                    sq[:, 0:width], src_view(c), AF.Square),
                    waits=wsq + (w0 if c == 0 else []), inc=(aa, 1))
                p.op("tensor", lambda eng, c=c: eng.matmul(
                    ps[PS_S0][0:1, 0:width], ones_col, src_view(c),
                    start=(c == 0), stop=(c == n_chunks - 1)),
                    waits=(w0 + self._free.get(PS_S0, [])) if c == 0 else (),
                    inc=(mm, 1))
                p.op("tensor", lambda eng, c=c: eng.matmul(
                    ps[PS_S1][0:1, 0:width], ones_col, sq[:, 0:width],
                    start=(c == 0), stop=(c == n_chunks - 1)),
                    waits=[(aa, aa.count)] + (self._free.get(PS_S1, []) if c == 0 else []),
                    inc=(mm, 1))
                self._sq_marks[0] = mm.count
            stats_mm = mm.count
            w = [(mm, stats_mm)]
            p.op("vector", lambda eng: eng.tensor_scalar_mul(
                A[:, 0:width], ps[PS_S0][0:1, 0:width], 1.0 / ndim),
                waits=w, inc=(vv, 1))
            self._free[PS_S0] = [(vv, vv.count)]
            p.op("vector", lambda eng: eng.tensor_scalar_mul(
                B[:, 0:width], ps[PS_S1][0:1, 0:width], 1.0 / ndim),
                waits=w, inc=(vv, 1))
            self._free[PS_S1] = [(vv, vv.count)]
            mean_done = vv.count
            p.op("tensor", lambda eng: eng.matmul(
                ps[PS_B0][:, 0:width], ones_row, A[:, 0:width],
                start=True, stop=True),
                waits=[(vv, mean_done)] + self._free.get(PS_B0, []), inc=(mm, 1))
            mbc = mm.count
            p.op("scalar", lambda eng: eng.activation(
                NSC[:, 0:width], A[:, 0:width], AF.Square),
                waits=[(vv, mean_done)], inc=(aa, 1))
            p.op("vector", lambda eng: eng.tensor_sub(
                B[:, 0:width], B[:, 0:width], NSC[:, 0:width]),
                waits=[(aa, aa.count), (mm, mbc)], inc=(vv, 1))
            rstd_done = rstd_from_B(eps, width)
            p.op("tensor", lambda eng: eng.matmul(
                ps[PS_B1][:, 0:width], ones_row, C[:, 0:width],
                start=True, stop=True),
                waits=[(vv, rstd_done)] + self._free.get(PS_B1, []), inc=(mm, 1))
            bca = mm.count
            extra_w = []
            for c in range(n_chunks):
                p.op("vector", lambda eng, c=c: eng.tensor_sub(
                    dst_view(c), src_view(c), ps[PS_B0][:, 0:width]),
                    waits=([(mm, bca)] if c == 0 else []) + extra_w, inc=(vv, 1))
                p.op("vector", lambda eng, c=c: eng.tensor_mul(
                    dst_view(c), dst_view(c), ps[PS_B1][:, 0:width]), inc=(vv, 1))
                if split is not None:
                    extra_w, _ = split(c, vv.count)
            self._free[PS_B0] = [(vv, vv.count)]
            self._free[PS_B1] = [(vv, vv.count)]
            return vv.count, bca

        def xv(tile):
            return lambda c: xT[:, c * T + tile * TT: c * T + (tile + 1) * TT]

        def h1v(c):
            return h1[:, c * TT:(c + 1) * TT]

        gu_marks = {}
        wd_marks = {}
        gate_free = aa.count
        up_free = vv.count
        dn_free = {}
        x1dma = []
        s_wl = [p.sem("s_wl0"), p.sem("s_wl1")]
        self._slot_free = {}
        self._slot_seq = 0
        self._asc_free = vv.count
        self._act_free = {}

        def stage_load(dram_slab, pair_view, gseq_key, marks, reuse_waits):
            slot = self._slot_seq % 2
            self._slot_seq += 1
            sem = s_wl[slot]
            stage = wst[:, slot * D:(slot + 1) * D]
            p.dma_ch("sync", stage.rearrange("p (c n) -> p c n", c=CD), dram_slab,
                     sem, waits=self._slot_free.get(slot, []) + list(reuse_waits))
            ld = (sem, sem.count)
            p.op("scalar", lambda eng, stage=stage, pv=pair_view: eng.copy(
                pv[:, 0:D], stage), waits=[ld], inc=(aa, 1))
            him = aa.count
            p.op("vector", lambda eng, stage=stage, pv=pair_view: eng.tensor_sub(
                pv[:, D:2 * D], stage, pv[:, 0:D]),
                waits=[(aa, him)], inc=(vv, 1))
            lom = vv.count
            self._slot_free[slot] = [(vv, lom)]
            marks[gseq_key] = (him, lom)

        gu_sp = {}
        wd_sp = {}

        def sched_gu(tile, h):
            gseq = tile * CH + h
            buf = gseq % 2
            w = [(mm, gu_marks[gseq - 2])] if gseq - 2 in gu_marks else []
            stage_load(dr["wgT"][:, h * 128:(h + 1) * 128].rearrange("(c p) n -> p c n", p=128),
                       wgb[:, buf * 2 * D:(buf + 1) * 2 * D], ("g", gseq), gu_sp, w)
            stage_load(dr["wuT"][:, h * 128:(h + 1) * 128].rearrange("(c p) n -> p c n", p=128),
                       wub[:, buf * 2 * D:(buf + 1) * 2 * D], ("u", gseq), gu_sp, [])

        def sched_wd(tile, g):
            gseq = tile * (CH // 2) + g
            w = [(mm, wd_marks[gseq - 1])] if gseq - 1 in wd_marks else []
            for i in range(2):
                h = 2 * g + i
                stage_load(dr["wdT"][h * 128:(h + 1) * 128, :].rearrange("p (c n) -> p c n", c=CD),
                           wdb[:, i * 2 * D:(i + 1) * 2 * D], ("d", gseq, i), wd_sp,
                           w if i == 0 else [])

        def split_chunk(c, src_ap, hi_ap, lo_ap, ready_mark):
            p.op("scalar", lambda eng: eng.copy(hi_ap, src_ap),
                 waits=[(vv, ready_mark)], inc=(aa, 1))
            him = aa.count
            p.op("vector", lambda eng: eng.tensor_sub(lo_ap, src_ap, hi_ap),
                 waits=[(aa, him)], inc=(vv, 1))
            return [(aa, him), (vv, vv.count)], vv.count

        def h1h(c):
            return h1b[:, c * TT:(c + 1) * TT]

        def h1l(c):
            return h1b[:, CD * TT + c * TT:CD * TT + (c + 1) * TT]

        for tile in range(NT):
            lnw = []
            pair_marks = []

            def dstv(c):
                return lnss[:, 0:TT]

            mark, _ = ln(xv(tile), dstv, EPS_LN,
                         wait0=([(s_in, lds0)] if tile == 0 else []),
                         split=lambda c, rdy: split_chunk(
                             c, lnss[:, 0:TT], h1h(c), h1l(c), rdy))
            h1_ready = vv.count
            sched_gu(tile, 0)
            sched_gu(tile, 1)
            sched_wd(tile, 0)
            for h in range(CH):
                gseq = tile * CH + h
                buf = gseq % 2
                gi = h % 2
                g = h // 2
                wgv = wgb[:, buf * 2 * D:(buf + 1) * 2 * D]
                wuv = wub[:, buf * 2 * D:(buf + 1) * 2 * D]

                def passes(wv, k):
                    return [(wv[:, k * 128:(k + 1) * 128], h1h(k)),
                            (wv[:, k * 128:(k + 1) * 128], h1l(k)),
                            (wv[:, D + k * 128:D + (k + 1) * 128], h1h(k))]

                np_ = 3 * CD
                i_ = 0
                for k in range(CD):
                    for lhs, rhs in passes(wgv, k):
                        p.op("tensor", lambda eng, lhs=lhs, rhs=rhs, i_=i_: eng.matmul(
                            ps[PS_GATE][:, 0:TT], lhs, rhs,
                            start=(i_ == 0), stop=(i_ == np_ - 1)),
                            waits=[(vv, gu_sp[("g", gseq)][1]), (vv, h1_ready),
                                   (aa, gate_free)] if i_ == 0 else (),
                            inc=(mm, 1) if i_ == np_ - 1 else None)
                        i_ += 1
                gate_mm = mm.count
                i_ = 0
                for k in range(CD):
                    for lhs, rhs in passes(wuv, k):
                        p.op("tensor", lambda eng, lhs=lhs, rhs=rhs, i_=i_: eng.matmul(
                            ps[PS_UP][:, 0:TT], lhs, rhs,
                            start=(i_ == 0), stop=(i_ == np_ - 1)),
                            waits=[(vv, gu_sp[("u", gseq)][1]), (vv, up_free)]
                            if i_ == 0 else (),
                            inc=(mm, 1) if i_ == np_ - 1 else None)
                        i_ += 1
                gu_marks[gseq] = mm.count
                up_mm = mm.count
                if h + 2 < CH:
                    sched_gu(tile, h + 2)
                p.op("scalar", lambda eng: eng.activation(
                    asc[:, :], ps[PS_GATE][:, 0:TT], AF.Silu),
                    waits=[(mm, gate_mm), (vv, self._asc_free)], inc=(aa, 1))
                gate_free = aa.count
                p.op("vector", lambda eng: eng.tensor_mul(
                    asc[:, :], asc[:, :], ps[PS_UP][:, 0:TT]),
                    waits=[(mm, up_mm), (aa, aa.count)], inc=(vv, 1))
                up_free = vv.count
                ach = actb[:, gi * TT:(gi + 1) * TT]
                acl = actb[:, 2 * TT + gi * TT:3 * TT + gi * TT]
                p.op("scalar", lambda eng, ach=ach: eng.copy(ach, asc[:, :]),
                     waits=[(vv, up_free)] + self._act_free.get(gi, []), inc=(aa, 1))
                p.op("vector", lambda eng, ach=ach, acl=acl: eng.tensor_sub(
                    acl, asc[:, :], ach), waits=[(aa, aa.count)], inc=(vv, 1))
                self._asc_free = vv.count
                act_ready = vv.count
                if gi == 1:
                    gsq = tile * (CH // 2) + g
                    for o in range(CD):
                        bank = PS_D0 if o % 2 == 0 else PS_D1
                        wdn = list([(vv, dn_free[o - 2])] if o - 2 in dn_free else [])
                        if o == 0:
                            wdn += [(vv, act_ready),
                                    (vv, wd_sp[("d", gsq, 0)][1]),
                                    (vv, wd_sp[("d", gsq, 1)][1])]
                        i_ = 0
                        for i in range(2):
                            wdv = wdb[:, i * 2 * D:(i + 1) * 2 * D]
                            a_h = actb[:, i * TT:(i + 1) * TT]
                            a_l = actb[:, 2 * TT + i * TT:3 * TT + i * TT]
                            for lhs, rhs in [(wdv[:, o * 128:(o + 1) * 128], a_h),
                                             (wdv[:, o * 128:(o + 1) * 128], a_l),
                                             (wdv[:, D + o * 128:D + (o + 1) * 128], a_h)]:
                                p.op("tensor", lambda eng, lhs=lhs, rhs=rhs, i_=i_, bank=bank:
                                     eng.matmul(ps[bank][:, 0:TT], lhs, rhs,
                                                start=(i_ == 0), stop=(i_ == 5)),
                                     waits=wdn if i_ == 0 else (),
                                     inc=(mm, 1) if i_ == 5 else None)
                                i_ += 1
                        dn_mm = mm.count
                        if g == 0:
                            p.op("vector", lambda eng, o=o, bank=bank: eng.tensor_copy(
                                y_acc[:, o * TT:(o + 1) * TT], ps[bank][:, 0:TT]),
                                waits=[(mm, dn_mm)], inc=(vv, 1))
                        else:
                            p.op("vector", lambda eng, o=o, bank=bank: eng.tensor_add(
                                y_acc[:, o * TT:(o + 1) * TT],
                                y_acc[:, o * TT:(o + 1) * TT], ps[bank][:, 0:TT]),
                                waits=[(mm, dn_mm)], inc=(vv, 1))
                        dn_free[o] = vv.count
                        self._free[bank] = [(vv, vv.count)]
                    wd_marks[gsq] = mm.count
                    self._act_free = {0: [(mm, mm.count)], 1: [(mm, mm.count)]}
                    if g + 1 < CH // 2:
                        sched_wd(tile, g + 1)
            for o in range(CD):
                p.op("vector", lambda eng, o=o, tile=tile: eng.scalar_tensor_tensor(
                    xv(tile)(o), y_acc[:, o * TT:(o + 1) * TT], 0.5,
                    xv(tile)(o), ALU.mult, ALU.add), inc=(vv, 1))
            x1_done = vv.count
            p.dma_ch("gpsimd",
                     dr["x1_out"].rearrange("(c p) t -> p c t", p=128)[:, :, tile * TT:(tile + 1) * TT],
                     xT[:, :].rearrange("p (c t) -> p c t", c=CD)[:, :, tile * TT:(tile + 1) * TT],
                     s_x1, waits=[(vv, x1_done)])
            x1dma.append(s_x1.count)

        def xh(c, t):
            return xTb[:, 2 * (c * T + t * TT): 2 * (c * T + t * TT) + TT]

        def xl(c, t):
            return xTb[:, 2 * (c * T + t * TT) + TT: 2 * (c * T + t * TT) + 2 * TT]

        ha_ready = []
        for tile in range(NT):
            mark, _ = ln(xv(tile), lambda c: lnss[:, 0:TT], EPS_LN,
                         wait0=[(s_x1, x1dma[-1])],
                         split=(lambda tile: lambda c, rdy: split_chunk(
                             c, lnss[:, 0:TT], xh(c, tile), xl(c, tile), rdy))(tile))
            ha_ready.append(vv.count)

        stage_bufs = (wg_s, wu_s)
        weff_bufs = (wdb[:, 0:CD * SLAB], h1b[:, 0:CD * SLAB])
        stage_ld = {}
        stage_read = {}
        weff_done = {}
        wb_read = {}

        def load_slab(s):
            w = []
            if s - 2 in stage_read:
                w.append((aa, stage_read[s - 2]))
            if s == 0:
                w.append((vv, ha_ready[-1]))
            sem = s_st[s % 2]
            p.dma_ch("sync",
                     stage_bufs[s % 2][:, 0:CD * SLAB].rearrange("p (c n) -> p c n", c=CD),
                     dr["wqkvT"][:, s * SLAB:(s + 1) * SLAB].rearrange("(c p) n -> p c n", p=128),
                     sem, waits=w)
            stage_ld[s] = (sem, sem.count)

        p.dma_ch("sync", sel16, dr["sel16_in"], s_in,
                 waits=[(s_x1, x1dma[-1])])
        wstw = self._slot_free.get(0, []) + self._slot_free.get(1, [])
        p.dma_ch("sync", cos2, dr["cos_t"], s_in, waits=wstw)
        p.dma_ch("sync", sin2, dr["sin_t"], s_in)
        p.dma_ch("sync", rotc, dr["rotc_in"], s_in)
        p.dma_ch("sync", eye_s, dr["eye_in"], s_in)
        sel16_ld = s_in.count
        load_slab(0)
        load_slab(1)

        for s in range(NSLAB):
            par = s % 2
            stage = stage_bufs[par]
            weff = weff_bufs[par]
            w = [stage_ld[s]]
            if s - 1 in wb_read:
                w.append((vv, wb_read[s - 1]))
            if s == 0:
                w.append((s_x1, x1dma[-1]))
            p.op("scalar", lambda eng, stage=stage: eng.copy(
                wbb[:, :], stage[:, 0:CD * SLAB]), waits=w, inc=(aa, 1))
            p.op("scalar", lambda eng, stage=stage: eng.activation(
                wab[:, :], stage[:, 0:CD * SLAB], AF.Abs), inc=(aa, 1))
            stage_read[s] = aa.count
            if s + 2 < NSLAB:
                load_slab(s + 2)
            if getattr(self, "debug_weff", False):
                p.op("vector", lambda eng, stage=stage: eng.tensor_copy(
                    y_acc[:, 7936:8192], stage[:, 0:256]), waits=[(aa, aa.count)], inc=(vv, 1))
                p.dma_ch("gpsimd", dr["dbg_wb"][s][:, 0:256], y_acc[:, 7936:8192],
                         s_dbg, waits=[(vv, vv.count)])
                self._weff_dump[s * 100 + 50] = s_dbg.count
            wab_ready = vv.count
            for c in range(CD):
                p.op("tensor", lambda eng, c=c: eng.matmul(
                    ps[PS_S0][0:16, 0:SLAB], selcol[:, (CD - 1 - c):(2 * CD - 1 - c)],
                    wab[:, c * SLAB:(c + 1) * SLAB],
                    start=(c == 0), stop=(c == CD - 1)),
                    waits=([(aa, stage_read[s])] + self._free.get(PS_S0, []))
                    if c == 0 else (), inc=(mm, 1))
            gsum_mm = mm.count
            p.op("vector", lambda eng: eng.tensor_scalar_mul(
                sclb, ps[PS_S0][0:16, 0:SLAB], 1.0 / 128.0),
                waits=[(mm, gsum_mm)], inc=(vv, 1))
            self._free[PS_S0] = [(vv, vv.count)]
            p.op("vector", lambda eng: eng.tensor_scalar_max(
                sclb, sclb, 1e-8), inc=(vv, 1))
            scl_done = vv.count
            for c in range(CD):
                p.op("tensor", lambda eng, c=c: eng.matmul(
                    ps[PS_B1][:, 0:SLAB], sel16[:, c * 128:(c + 1) * 128],
                    sclb, start=True, stop=True),
                    waits=([(vv, scl_done), (s_in, sel16_ld)] if c == 0 else [])
                    + self._free.get(PS_B1, []), inc=(mm, 1))
                bc = mm.count
                p.op("vector", lambda eng: eng.reciprocal(
                    inv_bc, ps[PS_B1][:, 0:SLAB]), waits=[(mm, bc)], inc=(vv, 1))
                p.op("vector", lambda eng, c=c: eng.tensor_mul(
                    rb, wbb[:, c * SLAB:(c + 1) * SLAB], inv_bc),
                    inc=(vv, 1))
                p.op("vector", lambda eng: eng.tensor_single_scalar(
                    g1, rb, 0.5, ALU.is_gt), inc=(vv, 1))
                p.op("vector", lambda eng: eng.tensor_single_scalar(
                    g2, rb, -0.5, ALU.is_lt), inc=(vv, 1))
                p.op("vector", lambda eng: eng.tensor_sub(g1, g1, g2), inc=(vv, 1))
                p.op("vector", lambda eng: eng.tensor_mul(
                    g2, g1, ps[PS_B1][:, 0:SLAB]), inc=(vv, 1))
                self._free[PS_B1] = [(vv, vv.count)]
                p.op("vector", lambda eng, c=c: eng.tensor_sub(
                    rb, g2, wbb[:, c * SLAB:(c + 1) * SLAB]), inc=(vv, 1))
                wwf = []
                if c == 0 and s - 2 in weff_done:
                    wwf.append((mm, weff_done[s - 2]))
                    if self._weff_dump:
                        wwf.append((s_dbg, max(self._weff_dump.values())))
                p.op("vector", lambda eng, c=c, weff=weff: eng.tensor_add(
                    weff[:, c * SLAB:(c + 1) * SLAB],
                    wbb[:, c * SLAB:(c + 1) * SLAB], rb),
                    waits=wwf, inc=(vv, 1))
            wb_read[s] = vv.count
            weff_ready = vv.count
            if getattr(self, "debug_weff", False):
                p.dma_ch("gpsimd", dr["dbg_weff"][s], weff[:, 0:CD * SLAB],
                         s_dbg, waits=[(vv, weff_ready)])
                self._weff_dump[s] = s_dbg.count

            for tile in range(NT):
                for oc in range(2):
                    bank = PS_GATE if oc == 0 else PS_UP
                    og = s * SLAB // 128 + oc
                    i_ = 0
                    for k in range(CD):
                        wt = weff[:, k * SLAB + oc * 128: k * SLAB + oc * 128 + 128]
                        for rhs in (xh(k, tile), xl(k, tile)):
                            p.op("tensor",
                                 lambda eng, wt=wt, rhs=rhs, bank=bank, i_=i_:
                                 eng.matmul(
                                     ps[bank][:, 0:TT], wt, rhs,
                                     start=(i_ == 0), stop=(i_ == 2 * CD - 1)),
                                 waits=[(vv, weff_ready), (vv, vv.count), (aa, aa.count),
                                        (vv, ha_ready[tile])]
                                 if i_ == 0 else (), inc=(mm, 1) if i_ == 2 * CD - 1 else None)
                            i_ += 1
                    qkv_mm = mm.count
                    self._head(p, dr, og, tile, bank, qkv_mm, locals())
            weff_done[s] = mm.count

        p.op("gpsimd", lambda eng: eng.engine_nop(), waits=[(s_out, s_out.count), (s_x1, s_x1.count)])

    def _head(self, p, dr, og, tile, bank, qkv_mm, env):
        T, TT, NH, NKV = self.T, self.TT, self.NH, self.NKV
        ps = env["ps"]
        PS_D0, PS_D1, PS_S1, PS_B1 = 2, 3, 5, 7
        s_out, mm, vv, aa = (env[k] for k in ("s_out", "mm", "vv", "aa"))
        qraw, ro, sq = env["qraw"], env["ro"], env["sq"]
        cos2, sin2, rotc = env["cos2"], env["sin2"], env["rotc"]
        gain_s, eye_s = env["gain_s"], env["eye_s"]
        ones_col, ones_row = env["ones_col"], env["ones_row"]
        A, B, C = env["A"], env["B"], env["C"]
        s_in, lds0 = env["s_in"], env["lds0"]
        cosv = cos2[:, tile * TT:(tile + 1) * TT]
        sinv = sin2[:, tile * TT:(tile + 1) * TT]
        rstd_from_B = env["rstd_from_B"]

        p.op("vector", lambda eng: eng.tensor_copy(qraw, ps[bank][:, 0:TT]),
             waits=[(mm, qkv_mm)], inc=(vv, 1))
        evac = vv.count

        if og < NH:
            h, kind = og, "q"
        elif og < NH + NKV:
            h, kind = og - NH, "k"
        else:
            h, kind = og - NH - NKV, "v"

        if kind == "v":
            for st in range(TT // 128):
                p.op("tensor", lambda eng, st=st: eng.transpose(
                    ps[PS_D0][:, st * 128:(st + 1) * 128],
                    qraw[:, st * 128:(st + 1) * 128], eye_s[:, :]),
                    waits=([(vv, evac)] + self._free.get(PS_D0, []))
                    if st == 0 else (),
                    inc=(mm, 1) if st == TT // 128 - 1 else None)
            tmark = mm.count
            p.op("vector", lambda eng: eng.tensor_copy(ro, ps[PS_D0][:, 0:TT]),
                 waits=[(mm, tmark)], inc=(vv, 1))
            self._free[PS_D0] = [(vv, vv.count)]
            rmark = vv.count
            for st in range(TT // 128):
                p.dma_ch("gpsimd",
                         dr["v_out"][h][tile * TT + st * 128: tile * TT + (st + 1) * 128, :],
                         ro[:, st * 128:(st + 1) * 128],
                         s_out, waits=[(vv, rmark)] if st == 0 else ())
            return

        wsq = [(mm, self._sq_marks[0])] if 0 in self._sq_marks else []
        p.op("scalar", lambda eng: eng.activation(
            sq[:, 0:TT], qraw, AF.Square),
            waits=wsq + [(vv, evac)], inc=(aa, 1))
        p.op("tensor", lambda eng: eng.matmul(
            ps[PS_S1][0:1, 0:TT], ones_col, sq[:, 0:TT],
            start=True, stop=True),
            waits=[(aa, aa.count)] + self._free.get(PS_S1, []), inc=(mm, 1))
        self._sq_marks[0] = mm.count
        smm = mm.count
        p.op("tensor", lambda eng: eng.matmul(
            ps[PS_D1][:, 0:TT], rotc[:, :], qraw, start=True, stop=True),
            waits=[(s_in, lds0), (vv, evac)] + self._free.get(PS_D1, []), inc=(mm, 1))
        rxm = mm.count
        p.op("vector", lambda eng: eng.tensor_scalar_mul(
            B, ps[PS_S1][0:1, 0:TT], 1.0 / 128.0),
            waits=[(mm, smm)], inc=(vv, 1))
        self._free[PS_S1] = [(vv, vv.count)]
        rstd_from_B(EPS_RMS, TT)
        if kind == "q":
            p.op("vector", lambda eng: eng.tensor_scalar_mul(
                C, C, RSQRT_HD), inc=(vv, 1))
            p.op("vector", lambda eng, h=h: eng.tensor_scalar_mul(
                C, C, gain_s[0:1, h:h + 1]), inc=(vv, 1))
        rstd_done = vv.count
        p.op("tensor", lambda eng: eng.matmul(
            ps[PS_B1][:, 0:TT], ones_row, C, start=True, stop=True),
            waits=[(vv, rstd_done)] + self._free.get(PS_B1, []), inc=(mm, 1))
        bmark = mm.count
        p.op("vector", lambda eng: eng.tensor_mul(ro, qraw, cosv), inc=(vv, 1))
        p.op("vector", lambda eng: eng.tensor_mul(qraw, ps[PS_D1][:, 0:TT], sinv),
             waits=[(mm, rxm)], inc=(vv, 1))
        self._free[PS_D1] = [(vv, vv.count)]
        p.op("vector", lambda eng: eng.tensor_add(ro, ro, qraw), inc=(vv, 1))
        p.op("vector", lambda eng: eng.tensor_mul(ro, ro, ps[PS_B1][:, 0:TT]),
             waits=[(mm, bmark)], inc=(vv, 1))
        self._free[PS_B1] = [(vv, vv.count)]
        omark = vv.count
        out = dr["q_out"] if kind == "q" else dr["k_out"]
        p.dma_ch("gpsimd", out[h][:, tile * TT:(tile + 1) * TT], ro,
                 s_out, waits=[(vv, omark)])



class L2:

    def __init__(self, S=8192, NHL=2):
        self.S = S
        self.NHL = NHL

    def build(self):
        S, NHL = self.S, self.NHL
        NQB = S // 512
        NC = S // 128
        nc = bass.Bass("TRN2", target_bir_lowering=False, debug=False,
                       detect_race_conditions=False)
        dr = {}
        dr["q2"] = nc.dram_tensor("q2", [NHL, 128, S], F32, kind="ExternalInput").ap()
        dr["k2"] = nc.dram_tensor("k2", [128, S], F32, kind="ExternalInput").ap()
        dr["v2"] = nc.dram_tensor("v2", [S, 128], F32, kind="ExternalInput").ap()
        dr["y2"] = nc.dram_tensor("y2", [NHL, 128, S], F32, kind="ExternalOutput").ap()

        with contextlib.ExitStack() as es:
            p = Prog(nc, es)
            q2 = p.sbuf("q2s", [128, NHL * S], F32)
            k2 = p.sbuf("k2s", [128, S], F32)
            v2 = p.sbuf("v2s", [128, S], F32)
            q2b, k2b, v2b = q2.bitcast(BF16), k2.bitcast(BF16), v2.bitcast(BF16)
            ee = p.sbuf("ee", [128, 2 * 512], F32)
            eeb = p.sbuf("eeb", [128, 2 * 1024], BF16)
            bc = p.sbuf("bc", [128, 512], F32)
            av = p.sbuf("av", [128, 2 * 512], F32)
            rr = p.sbuf("rr", [1, 512], F32)
            consts = p.sbuf("c2", [128, 134], F32)
            bnc = p.sbuf("bnc", [128, 512], F32)
            ps = [p.psum(f"ps{i}", [128, 512], F32) for i in range(8)]
            PS_AV0, PS_AV1, PS_R0, PS_R1, PS_S0, PS_S1, PS_B = range(7)

            s_in = p.sem("s_in")
            s_out = p.sem("s_out")
            mm = p.sem("mms")
            vv = p.sem("vvs")
            aa = p.sem("aas")
            gg = p.sem("ggs")

            p.dma_ch("sync", q2[:, :].rearrange("p (h s) -> p h s", h=NHL),
                     dr["q2"].rearrange("h p s -> p h s"), s_in)
            p.dma_ch("sync", k2[:, :], dr["k2"], s_in)
            p.dma_ch("sync", v2[:, :].rearrange("p (c n) -> p c n", c=NC),
                     dr["v2"].rearrange("(c p) n -> p c n", p=128), s_in)
            lds = s_in.count
            p.op("vector", lambda eng: eng.memset(consts[:, 0:1], 1.0), inc=(vv, 1))
            p.op("vector", lambda eng: eng.memset(consts[0:1, 1:129], 1.0), inc=(vv, 1))
            p.op("vector", lambda eng: eng.memset(
                consts[:, 129:130].bitcast(BF16), 1.0), inc=(vv, 1))
            cready = vv.count
            ones_col = consts[:, 0:1]
            ones_row = consts[0:1, 1:129]
            ones_b = consts[:, 129:130].bitcast(BF16)[:, 0:1]

            def ksp(c):
                return (k2b[:, 2 * c * 128:2 * c * 128 + 128],
                        k2b[:, 2 * c * 128 + 128:2 * c * 128 + 256])

            def vsp(c):
                return (v2b[:, 2 * c * 128:2 * c * 128 + 128],
                        v2b[:, 2 * c * 128 + 128:2 * c * 128 + 256])

            def qsp(h, qb):
                o = 2 * (h * self.S + qb * 512)
                return (q2b[:, o:o + 512], q2b[:, o + 512:o + 1024])

            def split_inplace(f32view, hi, lo, w0):
                w = f32view.shape[1]
                p.op("vector", lambda eng, w=w: eng.tensor_copy(bnc[:, 0:w], f32view),
                     waits=w0 + [(aa, aa.count)], inc=(vv, 1))
                p.op("scalar", lambda eng, w=w: eng.copy(hi, bnc[:, 0:w]),
                     waits=[(vv, vv.count)], inc=(aa, 1))
                p.op("vector", lambda eng, w=w: eng.tensor_sub(lo, bnc[:, 0:w], hi),
                     waits=[(aa, aa.count)], inc=(vv, 1))

            for c in range(NC):
                kh, kl = ksp(c)
                split_inplace(k2[:, c * 128:(c + 1) * 128], kh, kl,
                              [(s_in, lds)] if c == 0 else [])
                vh, vl = vsp(c)
                split_inplace(v2[:, c * 128:(c + 1) * 128], vh, vl, [])
            for h in range(NHL):
                for qb in range(NQB):
                    qh, ql = qsp(h, qb)
                    split_inplace(q2[:, h * self.S + qb * 512:h * self.S + (qb + 1) * 512],
                                  qh, ql, [])
            splits_done = vv.count

            ee_free = {}
            sc_free = {}
            grp_free = {}
            for gi, (qb, h) in enumerate([(qb, h) for qb in range(NQB) for h in range(NHL)]):
                gpar = gi % 2
                AVB = PS_AV0 if gpar == 0 else PS_AV1
                RB = PS_R0 if gpar == 0 else PS_R1
                njc = 4 * qb + 4
                qh, ql = qsp(h, qb)
                for j in range(njc):
                    sb = PS_S0 if j % 2 == 0 else PS_S1
                    kh, kl = ksp(j)
                    vh, vl = vsp(j)
                    w0 = [(vv, splits_done)] if gi == 0 and j < 2 else []
                    i_ = 0
                    for lhs, rhs in ((kh, qh), (kh, ql), (kl, qh)):
                        p.op("tensor", lambda eng, lhs=lhs, rhs=rhs, sb=sb, i_=i_: eng.matmul(
                            ps[sb][:, 0:512], lhs, rhs,
                            start=(i_ == 0), stop=(i_ == 2)),
                            waits=(w0 + sc_free.get(sb, [])) if i_ == 0 else (),
                            inc=(mm, 1) if i_ == 2 else None)
                        i_ += 1
                    smm = mm.count
                    par = j % 2
                    ev = ee[:, par * 512:(par + 1) * 512]
                    eh = eeb[:, par * 1024:par * 1024 + 512]
                    el = eeb[:, par * 1024 + 512:par * 1024 + 1024]
                    p.op("scalar", lambda eng, ev=ev, sb=sb: eng.activation(
                        ev, ps[sb][:, 0:512], AF.Exp),
                        waits=[(mm, smm)] + ee_free.get(par, []), inc=(aa, 1))
                    emark = [(aa, aa.count)]
                    d = j - 4 * qb
                    if d >= 0:
                        p.op("gpsimd", lambda eng, ev=ev, d=d: eng.affine_select(
                            ev, ev, [[1, 512]], ALU.is_ge, 0.0,
                            base=-128 * d, channel_multiplier=-1),
                            waits=emark, inc=(gg, 1))
                        emark = [(gg, gg.count)]
                    p.op("scalar", lambda eng, eh=eh, ev=ev: eng.copy(eh, ev),
                         waits=emark, inc=(aa, 1))
                    p.op("vector", lambda eng, el=el, ev=ev, eh=eh: eng.tensor_sub(
                        el, ev, eh), waits=[(aa, aa.count)], inc=(vv, 1))
                    esp = [(vv, vv.count)]
                    p.op("tensor", lambda eng, eh=eh, j=j, njc=njc, RB=RB: eng.matmul(
                        ps[RB][0:1, 0:512], ones_b, eh,
                        start=(j == 0), stop=False),
                        waits=esp + (grp_free.get(gpar, []) if j == 0 else []),
                        inc=(mm, 1))
                    p.op("tensor", lambda eng, el=el, j=j, njc=njc, RB=RB: eng.matmul(
                        ps[RB][0:1, 0:512], ones_b, el,
                        start=False, stop=(j == njc - 1)), inc=(mm, 1))
                    i_ = 0
                    for lhs, rhs in ((vh, eh), (vh, el), (vl, eh)):
                        p.op("tensor", lambda eng, lhs=lhs, rhs=rhs, j=j, njc=njc,
                             AVB=AVB, i_=i_: eng.matmul(
                                 ps[AVB][:, 0:512], lhs, rhs,
                                 start=(j == 0 and i_ == 0),
                                 stop=(j == njc - 1 and i_ == 2)), inc=(mm, 1))
                        i_ += 1
                    sc_free[sb] = [(aa, aa.count)]
                    ee_free[par] = [(mm, mm.count)]
                avm = mm.count
                avv = av[:, gpar * 512:(gpar + 1) * 512]
                p.op("vector", lambda eng, RB=RB: eng.reciprocal(
                    rr[:, :], ps[RB][0:1, 0:512]),
                    waits=[(mm, avm)] + grp_free.get(1 - gpar, [])[0:0], inc=(vv, 1))
                p.op("tensor", lambda eng: eng.matmul(
                    ps[PS_B][:, 0:512], ones_row, rr[:, :], start=True, stop=True),
                    waits=[(vv, vv.count)], inc=(mm, 1))
                p.op("vector", lambda eng: eng.tensor_copy(bc[:, :], ps[PS_B][:, 0:512]),
                     waits=[(mm, mm.count)], inc=(vv, 1))
                p.op("vector", lambda eng, avv=avv, AVB=AVB: eng.tensor_mul(
                    avv, ps[AVB][:, 0:512], bc[:, :]), inc=(vv, 1))
                nmark = vv.count
                p.dma_ch("gpsimd", dr["y2"][h][:, qb * 512:(qb + 1) * 512], avv,
                         s_out, waits=[(vv, nmark)])
                grp_free[gpar] = [(vv, nmark), (s_out, s_out.count)]
            p.op("gpsimd", lambda eng: eng.engine_nop(), waits=[(s_out, s_out.count)])
            p.build()
        return nc



class L3(L1):

    def build(self):
        T, TT, D, H = self.T, self.TT, self.D, self.H
        CD = self.CD
        nc = bass.Bass("TRN2", target_bir_lowering=False, debug=False,
                       detect_race_conditions=False)
        dr = {}
        dr["x_fm"] = nc.dram_tensor("x_fm", [D, T], F32, kind="ExternalInput").ap()
        dr["y_fm"] = nc.dram_tensor("y_fm", [D, T], F32, kind="ExternalInput").ap()
        dr["wgT"] = nc.dram_tensor("wgT", [D, H], F32, kind="ExternalInput").ap()
        dr["wuT"] = nc.dram_tensor("wuT", [D, H], F32, kind="ExternalInput").ap()
        dr["wdT"] = nc.dram_tensor("wdT", [H, D], F32, kind="ExternalInput").ap()
        dr["wpT"] = nc.dram_tensor("wpT", [D, D], F32, kind="ExternalInput").ap()
        dr["sel16_in"] = nc.dram_tensor("sel16_in", [16, 16 * 128], BF16, kind="ExternalInput").ap()
        dr["selc_in"] = nc.dram_tensor("selc_in", [128, 2 * CD - 1], BF16, kind="ExternalInput").ap()
        dr["x3_out"] = nc.dram_tensor("x3_out", [D, T], F32, kind="ExternalOutput").ap()
        if getattr(self, "dbg_weff3", False):
            dr["dbg_w"] = nc.dram_tensor("dbg_w", [D // 256, 128, 16 * 256], BF16,
                                         kind="ExternalOutput").ap()
        with contextlib.ExitStack() as es:
            p = Prog(nc, es)
            self._body3(nc, p, dr)
            p.build()
        return nc

    def _body3(self, nc, p, dr):
        T, TT, D, H = self.T, self.TT, self.D, self.H
        CD, CH, NT = self.CD, self.CH, self.NT
        SLAB = 256
        NSLAB = D // SLAB

        xT = p.sbuf("xT", [128, CD * T], F32)
        h1 = p.sbuf("h1", [128, CD * TT], F32)
        act = p.sbuf("act", [128, 2 * TT], F32)
        y_acc = p.sbuf("y_acc", [128, CD * TT], F32)
        wg_s = p.sbuf("wg_s", [128, 2 * D], F32)
        wu_s = p.sbuf("wu_s", [128, 2 * D], F32)
        wd_s = p.sbuf("wd_s", [128, 2 * D], F32)
        wst = p.sbuf("wst", [128, 2 * D], F32)
        lnss = p.sbuf("lnss", [128, TT], F32)
        asc = p.sbuf("asc", [128, TT], F32)
        sq = p.sbuf("sq", [128, TT], F32)
        stats = p.sbuf("stats", [128, 2 * TT], F32)
        consts = p.sbuf("consts", [128, 144], F32)
        selcol = p.sbuf("selcol", [128, 2 * CD - 1], BF16)
        h1b = h1.bitcast(BF16)
        actb = act.bitcast(BF16)
        wgb = wg_s.bitcast(BF16)
        wub = wu_s.bitcast(BF16)
        wdb = wd_s.bitcast(BF16)
        weff_bufs = (wdb[:, 0:CD * SLAB], wdb[:, CD * SLAB:2 * CD * SLAB])
        wbb = y_acc[:, 4096:6144].bitcast(BF16)
        wab = y_acc[:, 6144:8192].bitcast(BF16)
        rb = y_acc[:, 0:128].bitcast(BF16)
        g1 = y_acc[:, 128:384]
        g2 = y_acc[:, 384:640]
        sclb = y_acc[0:16, 640:768].bitcast(BF16)
        inv_bc = y_acc[:, 768:1024]
        sel16 = y_acc[0:16, 1024:2048].bitcast(BF16)

        ps = [p.psum(f"ps{i}", [128, 512], F32) for i in range(8)]
        PS_GATE, PS_UP, PS_D0, PS_D1, PS_S0, PS_S1, PS_B0, PS_B1 = range(8)

        s_in = p.sem("s_in")
        s_st = [p.sem("s_st0"), p.sem("s_st1")]
        s_y = p.sem("s_y")
        s_x1 = p.sem("s_x1")
        mm = p.sem("mms")
        vv = p.sem("vvs")
        aa = p.sem("aas")
        s_wl = [p.sem("s_wl0"), p.sem("s_wl1")]

        self._sq_marks = {}
        self._free = {}
        self._slot_free = {}
        self._slot_seq = 0
        self._asc_free = 0
        self._act_free = {}

        p.dma_ch("sync", xT[:, :].rearrange("p (c t) -> p c t", c=CD),
                 dr["x_fm"].rearrange("(c p) t -> p c t", p=128), s_in)
        p.dma_ch("sync", selcol[:, :], dr["selc_in"], s_in)
        lds0 = s_in.count
        p.op("vector", lambda eng: eng.memset(consts[:, 0:1], 1.0), inc=(vv, 1))
        p.op("vector", lambda eng: eng.memset(consts[0:1, 1:129], 1.0), inc=(vv, 1))
        consts_ready = vv.count
        ones_col = consts[:, 0:1]
        ones_row = consts[0:1, 1:129]
        A = stats[0:1, 0:TT]
        B = stats[0:1, TT:2 * TT]
        C = stats[0:1, 0:TT]
        NSC = lnss[0:1, 0:TT]

        def rstd_from_B(extra_eps, width):
            p.op("vector", lambda eng: eng.tensor_scalar_add(
                B[:, 0:width], B[:, 0:width], extra_eps), inc=(vv, 1))
            p.op("vector", lambda eng: eng.reciprocal(
                C[:, 0:width], B[:, 0:width]), inc=(vv, 1))
            p.op("scalar", lambda eng: eng.activation(
                C[:, 0:width], C[:, 0:width], AF.Sqrt),
                waits=[(vv, vv.count)], inc=(aa, 1))
            p.op("scalar", lambda eng: eng.activation(
                NSC[:, 0:width], C[:, 0:width], AF.Square), inc=(aa, 1))
            p.op("vector", lambda eng: eng.tensor_mul(
                NSC[:, 0:width], NSC[:, 0:width], B[:, 0:width]),
                waits=[(aa, aa.count)], inc=(vv, 1))
            p.op("vector", lambda eng: eng.tensor_scalar(
                NSC[:, 0:width], NSC[:, 0:width], -0.5, 1.5, ALU.mult, ALU.add),
                inc=(vv, 1))
            p.op("vector", lambda eng: eng.tensor_mul(
                C[:, 0:width], C[:, 0:width], NSC[:, 0:width]), inc=(vv, 1))
            return vv.count

        def ln(src_view, dst_view, eps, wait0=(), n_chunks=CD, width=TT, split=None):
            w0 = list(wait0) + [(vv, consts_ready)]
            ndim = n_chunks * 128
            for c in range(n_chunks):
                wsq = [(mm, self._sq_marks[0])] if 0 in self._sq_marks else []
                p.op("scalar", lambda eng, c=c: eng.activation(
                    sq[:, 0:width], src_view(c), AF.Square),
                    waits=wsq + (w0 if c == 0 else []), inc=(aa, 1))
                p.op("tensor", lambda eng, c=c: eng.matmul(
                    ps[PS_S0][0:1, 0:width], ones_col, src_view(c),
                    start=(c == 0), stop=(c == n_chunks - 1)),
                    waits=(w0 + self._free.get(PS_S0, [])) if c == 0 else (),
                    inc=(mm, 1))
                p.op("tensor", lambda eng, c=c: eng.matmul(
                    ps[PS_S1][0:1, 0:width], ones_col, sq[:, 0:width],
                    start=(c == 0), stop=(c == n_chunks - 1)),
                    waits=[(aa, aa.count)] + (self._free.get(PS_S1, []) if c == 0 else []),
                    inc=(mm, 1))
                self._sq_marks[0] = mm.count
            stats_mm = mm.count
            w = [(mm, stats_mm)]
            p.op("vector", lambda eng: eng.tensor_scalar_mul(
                A[:, 0:width], ps[PS_S0][0:1, 0:width], 1.0 / ndim),
                waits=w, inc=(vv, 1))
            self._free[PS_S0] = [(vv, vv.count)]
            p.op("vector", lambda eng: eng.tensor_scalar_mul(
                B[:, 0:width], ps[PS_S1][0:1, 0:width], 1.0 / ndim),
                waits=w, inc=(vv, 1))
            self._free[PS_S1] = [(vv, vv.count)]
            mean_done = vv.count
            p.op("tensor", lambda eng: eng.matmul(
                ps[PS_B0][:, 0:width], ones_row, A[:, 0:width],
                start=True, stop=True),
                waits=[(vv, mean_done)] + self._free.get(PS_B0, []), inc=(mm, 1))
            mbc = mm.count
            p.op("scalar", lambda eng: eng.activation(
                NSC[:, 0:width], A[:, 0:width], AF.Square),
                waits=[(vv, mean_done)], inc=(aa, 1))
            p.op("vector", lambda eng: eng.tensor_sub(
                B[:, 0:width], B[:, 0:width], NSC[:, 0:width]),
                waits=[(aa, aa.count), (mm, mbc)], inc=(vv, 1))
            rstd_done = rstd_from_B(eps, width)
            p.op("tensor", lambda eng: eng.matmul(
                ps[PS_B1][:, 0:width], ones_row, C[:, 0:width],
                start=True, stop=True),
                waits=[(vv, rstd_done)] + self._free.get(PS_B1, []), inc=(mm, 1))
            bca = mm.count
            extra_w = []
            for c in range(n_chunks):
                p.op("vector", lambda eng, c=c: eng.tensor_sub(
                    dst_view(c), src_view(c), ps[PS_B0][:, 0:width]),
                    waits=([(mm, bca)] if c == 0 else []) + extra_w, inc=(vv, 1))
                p.op("vector", lambda eng, c=c: eng.tensor_mul(
                    dst_view(c), dst_view(c), ps[PS_B1][:, 0:width]), inc=(vv, 1))
                if split is not None:
                    extra_w, _ = split(c, vv.count)
            self._free[PS_B0] = [(vv, vv.count)]
            self._free[PS_B1] = [(vv, vv.count)]
            return vv.count, bca

        def xv(tile):
            return lambda c: xT[:, c * T + tile * TT: c * T + (tile + 1) * TT]

        def h1v(c):
            return h1[:, c * TT:(c + 1) * TT]

        def split_chunk(c, src_ap, hi_ap, lo_ap, ready_mark):
            p.op("scalar", lambda eng: eng.copy(hi_ap, src_ap),
                 waits=[(vv, ready_mark)], inc=(aa, 1))
            him = aa.count
            p.op("vector", lambda eng: eng.tensor_sub(lo_ap, src_ap, hi_ap),
                 waits=[(aa, him)], inc=(vv, 1))
            return [(aa, him), (vv, vv.count)], vv.count

        def h1h(c):
            return h1b[:, c * TT:(c + 1) * TT]

        def h1l(c):
            return h1b[:, CD * TT + c * TT:CD * TT + (c + 1) * TT]

        def ynh(c):
            return h1b[:, 2 * c * TT:(2 * c + 1) * TT]

        def ynl(c):
            return h1b[:, (2 * c + 1) * TT:(2 * c + 2) * TT]

        def stage_load(dram_slab, pair_view, key, marks, reuse_waits):
            slot = self._slot_seq % 2
            self._slot_seq += 1
            sem = s_wl[slot]
            stage = wst[:, slot * D:(slot + 1) * D]
            p.dma_ch("sync", stage.rearrange("p (c n) -> p c n", c=CD), dram_slab,
                     sem, waits=self._slot_free.get(slot, []) + list(reuse_waits))
            ld = (sem, sem.count)
            p.op("scalar", lambda eng, stage=stage, pv=pair_view: eng.copy(
                pv[:, 0:D], stage), waits=[ld], inc=(aa, 1))
            him = aa.count
            p.op("vector", lambda eng, stage=stage, pv=pair_view: eng.tensor_sub(
                pv[:, D:2 * D], stage, pv[:, 0:D]),
                waits=[(aa, him)], inc=(vv, 1))
            self._slot_free[slot] = [(vv, vv.count)]
            marks[key] = (him, vv.count)

        gu_marks = {}
        wd_marks = {}
        gu_sp = {}
        wd_sp = {}
        gate_free = aa.count
        up_free = vv.count
        dn_free = {}
        x1dma = []

        def sched_gu(tile, h):
            gseq = tile * CH + h
            buf = gseq % 2
            w = [(mm, gu_marks[gseq - 2])] if gseq - 2 in gu_marks else []
            if h < 2 and ("proj", tile) in gu_marks:
                w += [(mm, gu_marks[("proj", tile)])]
            stage_load(dr["wgT"][:, h * 128:(h + 1) * 128].rearrange("(c p) n -> p c n", p=128),
                       wgb[:, buf * 2 * D:(buf + 1) * 2 * D], ("g", gseq), gu_sp, w)
            stage_load(dr["wuT"][:, h * 128:(h + 1) * 128].rearrange("(c p) n -> p c n", p=128),
                       wub[:, buf * 2 * D:(buf + 1) * 2 * D], ("u", gseq), gu_sp, [])

        def sched_wd(tile, g):
            gseq = tile * (CH // 2) + g
            w = [(mm, wd_marks[gseq - 1])] if gseq - 1 in wd_marks else []
            if g == 0 and ("proj", tile) in gu_marks:
                w += [(mm, gu_marks[("proj", tile)])]
            for i in range(2):
                h = 2 * g + i
                stage_load(dr["wdT"][h * 128:(h + 1) * 128, :].rearrange("p (c n) -> p c n", c=CD),
                           wdb[:, i * 2 * D:(i + 1) * 2 * D], ("d", gseq, i), wd_sp,
                           w if i == 0 else [])

        stage_bufs = (wg_s, wu_s)
        for tile in range(NT):
            ylw = [(vv, up_free)] if tile else [(vv, consts_ready)]
            p.dma_ch("sync", h1[:, :].rearrange("p (c t) -> p c t", c=CD),
                     dr["y_fm"].rearrange("(c p) t -> p c t", p=128)[:, :, tile * TT:(tile + 1) * TT],
                     s_y, waits=ylw)
            yld = s_y.count
            for c in range(CD):
                wsq = [(mm, self._sq_marks[0])] if 0 in self._sq_marks else []
                p.op("scalar", lambda eng, c=c: eng.activation(
                    sq[:, 0:TT], h1v(c), AF.Square),
                    waits=wsq + ([(s_y, yld)] if c == 0 else []), inc=(aa, 1))
                p.op("tensor", lambda eng, c=c: eng.matmul(
                    ps[PS_S1][0:1, 0:TT], ones_col, sq[:, 0:TT],
                    start=(c == 0), stop=(c == CD - 1)),
                    waits=[(aa, aa.count)] + (self._free.get(PS_S1, []) if c == 0 else []),
                    inc=(mm, 1))
                self._sq_marks[0] = mm.count
            smm = mm.count
            p.op("vector", lambda eng: eng.tensor_scalar_mul(
                B, ps[PS_S1][0:1, 0:TT], 1.0 / D), waits=[(mm, smm)], inc=(vv, 1))
            self._free[PS_S1] = [(vv, vv.count)]
            rstd_done = rstd_from_B(EPS_RMS, TT)
            p.op("tensor", lambda eng: eng.matmul(
                ps[PS_B1][:, 0:TT], ones_row, C, start=True, stop=True),
                waits=[(vv, rstd_done)] + self._free.get(PS_B1, []), inc=(mm, 1))
            bca = mm.count
            extra_w = []
            for c in range(CD):
                p.op("vector", lambda eng, c=c: eng.tensor_mul(
                    lnss[:, 0:TT], h1v(c), ps[PS_B1][:, 0:TT]),
                    waits=([(mm, bca)] if c == 0 else []) + extra_w, inc=(vv, 1))
                extra_w, _ = split_chunk(c, lnss[:, 0:TT], ynh(c), ynl(c), vv.count)
            self._free[PS_B1] = [(vv, vv.count)]
            yn_ready = vv.count

            stage_ld = {}
            stage_read = {}
            weff_done = {}
            wb_read = {}

            def load_slab(s, tile=tile, stage_ld=stage_ld, stage_read=stage_read):
                w = []
                if s - 2 in stage_read:
                    w.append((aa, stage_read[s - 2]))
                elif tile == 0 and s < 2:
                    w.append((s_in, lds0))
                sem = s_st[s % 2]
                p.dma_ch("sync",
                         stage_bufs[s % 2][:, 0:CD * SLAB].rearrange("p (c n) -> p c n", c=CD),
                         dr["wpT"][:, s * SLAB:(s + 1) * SLAB].rearrange("(c p) n -> p c n", p=128),
                         sem, waits=w)
                stage_ld[s] = (sem, sem.count)

            p.dma_ch("sync", sel16, dr["sel16_in"], s_in,
                     waits=[(vv, yn_ready)])
            self._sel16_ld = s_in.count
            load_slab(0)
            load_slab(1)
            for s in range(NSLAB):
                par = s % 2
                stage = stage_bufs[par]
                weff = weff_bufs[par]
                w = [stage_ld[s]]
                if s - 1 in wb_read:
                    w.append((vv, wb_read[s - 1]))
                if s == 0:
                    w.append((vv, yn_ready))
                if getattr(self, "dbg_weff3", False) and s_y.count:
                    w.append((s_y, s_y.count))
                p.op("scalar", lambda eng, stage=stage: eng.copy(
                    wbb[:, :], stage[:, 0:CD * SLAB]), waits=w, inc=(aa, 1))
                p.op("scalar", lambda eng, stage=stage: eng.activation(
                    wab[:, :], stage[:, 0:CD * SLAB], AF.Abs), inc=(aa, 1))
                stage_read[s] = aa.count
                if s + 2 < NSLAB:
                    load_slab(s + 2)
                for c in range(CD):
                    p.op("tensor", lambda eng, c=c: eng.matmul(
                        ps[PS_S0][0:16, 0:SLAB], selcol[:, (CD - 1 - c):(2 * CD - 1 - c)],
                        wab[:, c * SLAB:(c + 1) * SLAB],
                        start=(c == 0), stop=(c == CD - 1)),
                        waits=([(aa, stage_read[s])] + self._free.get(PS_S0, []))
                        if c == 0 else (), inc=(mm, 1))
                gsum_mm = mm.count
                p.op("vector", lambda eng: eng.tensor_scalar_mul(
                    sclb, ps[PS_S0][0:16, 0:SLAB], 1.0 / 128.0),
                    waits=[(mm, gsum_mm)], inc=(vv, 1))
                self._free[PS_S0] = [(vv, vv.count)]
                p.op("vector", lambda eng: eng.tensor_scalar_max(
                    sclb, sclb, 1e-8), inc=(vv, 1))
                scl_done = vv.count
                for c in range(CD):
                    p.op("tensor", lambda eng, c=c: eng.matmul(
                        ps[PS_B1][:, 0:SLAB], sel16[:, c * 128:(c + 1) * 128],
                        sclb, start=True, stop=True),
                        waits=([(vv, scl_done), (s_in, self._sel16_ld)] if c == 0 else [])
                        + self._free.get(PS_B1, []), inc=(mm, 1))
                    bc = mm.count
                    p.op("vector", lambda eng: eng.reciprocal(
                        inv_bc, ps[PS_B1][:, 0:SLAB]), waits=[(mm, bc)], inc=(vv, 1))
                    p.op("vector", lambda eng, c=c: eng.tensor_mul(
                        rb, wbb[:, c * SLAB:(c + 1) * SLAB], inv_bc), inc=(vv, 1))
                    p.op("vector", lambda eng: eng.tensor_single_scalar(
                        g1, rb, 0.5, ALU.is_gt), inc=(vv, 1))
                    p.op("vector", lambda eng: eng.tensor_single_scalar(
                        g2, rb, -0.5, ALU.is_lt), inc=(vv, 1))
                    p.op("vector", lambda eng: eng.tensor_sub(g1, g1, g2), inc=(vv, 1))
                    p.op("vector", lambda eng: eng.tensor_mul(
                        g2, g1, ps[PS_B1][:, 0:SLAB]), inc=(vv, 1))
                    self._free[PS_B1] = [(vv, vv.count)]
                    p.op("vector", lambda eng, c=c: eng.tensor_sub(
                        rb, g2, wbb[:, c * SLAB:(c + 1) * SLAB]), inc=(vv, 1))
                    wwf = [(mm, weff_done[s - 2])] if (c == 0 and s - 2 in weff_done) else []
                    p.op("vector", lambda eng, c=c, weff=weff: eng.tensor_add(
                        weff[:, c * SLAB:(c + 1) * SLAB],
                        wbb[:, c * SLAB:(c + 1) * SLAB], rb),
                        waits=wwf, inc=(vv, 1))
                wb_read[s] = vv.count
                weff_ready = vv.count
                if getattr(self, "dbg_weff3", False) and tile == 0:
                    which = getattr(self, "dbg_which", "weff")
                    if which == "inv":
                        p.dma_ch("gpsimd", dr["dbg_w"][s][:, 0:512],
                                 inv_bc.bitcast(BF16), s_y, waits=[(vv, weff_ready)])
                    else:
                        src_dbg = {"weff": weff[:, 0:CD * SLAB], "wbb": wbb[:, :],
                                   "wab": wab[:, :]}[which]
                        p.dma_ch("gpsimd", dr["dbg_w"][s], src_dbg,
                                 s_y, waits=[(vv, weff_ready)])
                for oc in range(2):
                    bank = PS_GATE if oc == 0 else PS_UP
                    og = s * 2 + oc
                    i_ = 0
                    for k in range(CD):
                        wt = weff[:, k * SLAB + oc * 128: k * SLAB + oc * 128 + 128]
                        for rhs in (ynh(k), ynl(k)):
                            p.op("tensor",
                                 lambda eng, wt=wt, rhs=rhs, bank=bank, i_=i_:
                                 eng.matmul(ps[bank][:, 0:TT], wt, rhs,
                                            start=(i_ == 0), stop=(i_ == 2 * CD - 1)),
                                 waits=[(vv, weff_ready), (vv, vv.count),
                                        (aa, gate_free)] if i_ == 0 else (),
                                 inc=(mm, 1) if i_ == 2 * CD - 1 else None)
                            i_ += 1
                    pmm = mm.count
                    p.op("vector", lambda eng, og=og, tile=tile, bank=bank:
                         eng.tensor_add(xv(tile)(og), xv(tile)(og), ps[bank][:, 0:TT]),
                         waits=[(mm, pmm)], inc=(vv, 1))
                weff_done[s] = mm.count
            gu_marks[("proj", tile)] = mm.count

            if getattr(self, "dbg_yn", False):
                p.dma_ch("gpsimd",
                         dr["x3_out"].rearrange("(c p) t -> p c t", p=128)[:, :, tile * TT:(tile + 1) * TT],
                         h1[:, :].rearrange("p (c t) -> p c t", c=CD),
                         s_x1, waits=[(vv, yn_ready)])
                x1dma.append(s_x1.count)
                up_free = vv.count
                continue
            if getattr(self, "dbg_x2", False):
                p.dma_ch("gpsimd",
                         dr["x3_out"].rearrange("(c p) t -> p c t", p=128)[:, :, tile * TT:(tile + 1) * TT],
                         xT[:, :].rearrange("p (c t) -> p c t", c=CD)[:, :, tile * TT:(tile + 1) * TT],
                         s_x1, waits=[(vv, vv.count)])
                x1dma.append(s_x1.count)
                up_free = vv.count
                continue
            mark, _ = ln(xv(tile), lambda c: lnss[:, 0:TT], EPS_LN,
                         split=lambda c, rdy: split_chunk(
                             c, lnss[:, 0:TT], h1h(c), h1l(c), rdy))
            h1_ready = vv.count
            sched_gu(tile, 0)
            sched_gu(tile, 1)
            sched_wd(tile, 0)
            for h in range(CH):
                gseq = tile * CH + h
                buf = gseq % 2
                gi = h % 2
                g = h // 2
                wgv = wgb[:, buf * 2 * D:(buf + 1) * 2 * D]
                wuv = wub[:, buf * 2 * D:(buf + 1) * 2 * D]

                def passes(wv, k):
                    return [(wv[:, k * 128:(k + 1) * 128], h1h(k)),
                            (wv[:, k * 128:(k + 1) * 128], h1l(k)),
                            (wv[:, D + k * 128:D + (k + 1) * 128], h1h(k))]

                np_ = 3 * CD
                i_ = 0
                for k in range(CD):
                    for lhs, rhs in passes(wgv, k):
                        p.op("tensor", lambda eng, lhs=lhs, rhs=rhs, i_=i_: eng.matmul(
                            ps[PS_GATE][:, 0:TT], lhs, rhs,
                            start=(i_ == 0), stop=(i_ == np_ - 1)),
                            waits=[(vv, gu_sp[("g", gseq)][1]), (vv, h1_ready),
                                   (aa, gate_free)] if i_ == 0 else (),
                            inc=(mm, 1) if i_ == np_ - 1 else None)
                        i_ += 1
                gate_mm = mm.count
                i_ = 0
                for k in range(CD):
                    for lhs, rhs in passes(wuv, k):
                        p.op("tensor", lambda eng, lhs=lhs, rhs=rhs, i_=i_: eng.matmul(
                            ps[PS_UP][:, 0:TT], lhs, rhs,
                            start=(i_ == 0), stop=(i_ == np_ - 1)),
                            waits=[(vv, gu_sp[("u", gseq)][1]), (vv, up_free)]
                            if i_ == 0 else (),
                            inc=(mm, 1) if i_ == np_ - 1 else None)
                        i_ += 1
                gu_marks[gseq] = mm.count
                up_mm = mm.count
                if h + 2 < CH:
                    sched_gu(tile, h + 2)
                p.op("scalar", lambda eng: eng.activation(
                    asc[:, :], ps[PS_GATE][:, 0:TT], AF.Silu),
                    waits=[(mm, gate_mm), (vv, self._asc_free)], inc=(aa, 1))
                gate_free = aa.count
                p.op("vector", lambda eng: eng.tensor_mul(
                    asc[:, :], asc[:, :], ps[PS_UP][:, 0:TT]),
                    waits=[(mm, up_mm), (aa, aa.count)], inc=(vv, 1))
                up_free = vv.count
                ach = actb[:, gi * TT:(gi + 1) * TT]
                acl = actb[:, 2 * TT + gi * TT:3 * TT + gi * TT]
                p.op("scalar", lambda eng, ach=ach: eng.copy(ach, asc[:, :]),
                     waits=[(vv, up_free)] + self._act_free.get(gi, []), inc=(aa, 1))
                p.op("vector", lambda eng, ach=ach, acl=acl: eng.tensor_sub(
                    acl, asc[:, :], ach), waits=[(aa, aa.count)], inc=(vv, 1))
                self._asc_free = vv.count
                act_ready = vv.count
                if gi == 1:
                    gsq = tile * (CH // 2) + g
                    for o in range(CD):
                        bank = PS_D0 if o % 2 == 0 else PS_D1
                        wdn = list([(vv, dn_free[o - 2])] if o - 2 in dn_free else [])
                        if o == 0:
                            wdn += [(vv, act_ready),
                                    (vv, wd_sp[("d", gsq, 0)][1]),
                                    (vv, wd_sp[("d", gsq, 1)][1])]
                        i_ = 0
                        for i in range(2):
                            wdv = wdb[:, i * 2 * D:(i + 1) * 2 * D]
                            a_h = actb[:, i * TT:(i + 1) * TT]
                            a_l = actb[:, 2 * TT + i * TT:3 * TT + i * TT]
                            for lhs, rhs in [(wdv[:, o * 128:(o + 1) * 128], a_h),
                                             (wdv[:, o * 128:(o + 1) * 128], a_l),
                                             (wdv[:, D + o * 128:D + (o + 1) * 128], a_h)]:
                                p.op("tensor", lambda eng, lhs=lhs, rhs=rhs, i_=i_, bank=bank:
                                     eng.matmul(ps[bank][:, 0:TT], lhs, rhs,
                                                start=(i_ == 0), stop=(i_ == 5)),
                                     waits=wdn if i_ == 0 else (),
                                     inc=(mm, 1) if i_ == 5 else None)
                                i_ += 1
                        dn_mm = mm.count
                        if g == 0:
                            p.op("vector", lambda eng, o=o, bank=bank: eng.tensor_copy(
                                y_acc[:, o * TT:(o + 1) * TT], ps[bank][:, 0:TT]),
                                waits=[(mm, dn_mm)], inc=(vv, 1))
                        else:
                            p.op("vector", lambda eng, o=o, bank=bank: eng.tensor_add(
                                y_acc[:, o * TT:(o + 1) * TT],
                                y_acc[:, o * TT:(o + 1) * TT], ps[bank][:, 0:TT]),
                                waits=[(mm, dn_mm)], inc=(vv, 1))
                        dn_free[o] = vv.count
                        self._free[bank] = [(vv, vv.count)]
                    wd_marks[gsq] = mm.count
                    self._act_free = {0: [(mm, mm.count)], 1: [(mm, mm.count)]}
                    if g + 1 < CH // 2:
                        sched_wd(tile, g + 1)
            for o in range(CD):
                p.op("vector", lambda eng, o=o, tile=tile: eng.scalar_tensor_tensor(
                    xv(tile)(o), y_acc[:, o * TT:(o + 1) * TT], 0.5,
                    xv(tile)(o), ALU.mult, ALU.add), inc=(vv, 1))
            x3_done = vv.count
            p.dma_ch("gpsimd",
                     dr["x3_out"].rearrange("(c p) t -> p c t", p=128)[:, :, tile * TT:(tile + 1) * TT],
                     xT[:, :].rearrange("p (c t) -> p c t", c=CD)[:, :, tile * TT:(tile + 1) * TT],
                     s_x1, waits=[(vv, x3_done)])
            x1dma.append(s_x1.count)
        p.op("gpsimd", lambda eng: eng.engine_nop(), waits=[(s_x1, s_x1.count)])



NCORE = 8
SEQ = 8192
DM = 2048
TCORE = SEQ // NCORE


def _consts():
    eye = np.eye(128, dtype=np.float32)
    import ml_dtypes
    sel16 = np.zeros((16, 16 * 128), dtype=ml_dtypes.bfloat16)
    for c in range(16):
        sel16[c, c * 128:(c + 1) * 128] = 1.0
    CD = DM // 128
    selc = np.zeros((128, 2 * CD - 1), dtype=ml_dtypes.bfloat16)
    for pp in range(128):
        selc[pp, CD - 1] = 1.0
    R = np.zeros((128, 128), dtype=np.float32)
    for i in range(64):
        R[i, 64 + i] = 1.0
        R[64 + i, i] = -1.0
    rotc = np.ascontiguousarray(R.T)
    return eye, sel16, selc, rotc


_ROPE_CACHE = {}


def _rope_tables_full():
    if "t" not in _ROPE_CACHE:
        import jax
        import jax.numpy as jnp
        cpu = jax.devices("cpu")[0]
        with jax.default_device(cpu):
            inv = 1.0 / 10000.0 ** (jnp.arange(0, 128, 2, dtype=jnp.float32) / 128)
            t = jnp.arange(SEQ, dtype=jnp.float32)
            fr = jnp.outer(t, inv)
            c = np.asarray(jnp.cos(fr)).T.astype(np.float32)
            s = np.asarray(jnp.sin(fr)).T.astype(np.float32)
        _ROPE_CACHE["t"] = (c, s)
    return _ROPE_CACHE["t"]


def _rope_tables(pos):
    c, s = _rope_tables_full()
    cc = np.ascontiguousarray(c[:, pos])
    ss = np.ascontiguousarray(s[:, pos])
    return np.concatenate([cc, cc]), np.concatenate([ss, ss])


def kernel(L0_attn_x, L0_attn_w_qkv, L0_attn_w_proj, L0_attn_q_gain,
           L0_ffn_w_gate, L0_ffn_w_up, L0_ffn_w_down):
    from concourse.bass_utils import run_bass_kernel_spmd

    x = np.ascontiguousarray(np.asarray(L0_attn_x, dtype=np.float32)[0].T)
    wgT = np.ascontiguousarray(np.asarray(L0_ffn_w_gate, np.float32).T)
    wuT = np.ascontiguousarray(np.asarray(L0_ffn_w_up, np.float32).T)
    wdT = np.ascontiguousarray(np.asarray(L0_ffn_w_down, np.float32).T)
    wqkvT = np.ascontiguousarray(np.asarray(L0_attn_w_qkv, np.float32).T)
    wpT = np.ascontiguousarray(np.asarray(L0_attn_w_proj, np.float32).T)
    gain = np.asarray(L0_attn_q_gain, np.float32)[None, :]
    eye, sel16, selc, rotc = _consts()

    im1 = []
    for c in range(NCORE):
        pos = np.arange(c * TCORE, (c + 1) * TCORE)
        cos, sin = _rope_tables(pos)
        im1.append({
            "x_fm": np.ascontiguousarray(x[:, c * TCORE:(c + 1) * TCORE]),
            "wgT": wgT, "wuT": wuT, "wdT": wdT, "wqkvT": wqkvT,
            "cos_t": cos, "sin_t": sin, "gain_in": gain,
            "eye_in": eye, "sel16_in": sel16, "selc_in": selc, "rotc_in": rotc,
        })
    nc1 = L1().build()
    r1 = run_bass_kernel_spmd(nc1, im1, core_ids=list(range(NCORE))).results

    q_all = np.concatenate([r["q_out"] for r in r1], axis=2)
    k_all = np.concatenate([r["k_out"] for r in r1], axis=2)
    v_all = np.concatenate([r["v_out"] for r in r1], axis=1)

    im2 = []
    for c in range(NCORE):
        kv = c // 2
        im2.append({
            "q2": np.ascontiguousarray(q_all[2 * c:2 * c + 2]),
            "k2": np.ascontiguousarray(k_all[kv]),
            "v2": np.ascontiguousarray(v_all[kv]),
        })
    nc2 = L2().build()
    r2 = run_bass_kernel_spmd(nc2, im2, core_ids=list(range(NCORE))).results
    y_all = np.concatenate([r["y2"] for r in r2], axis=0)
    y_fm = y_all.reshape(DM, SEQ)

    im3 = []
    for c in range(NCORE):
        im3.append({
            "x_fm": r1[c]["x1_out"],
            "y_fm": np.ascontiguousarray(y_fm[:, c * TCORE:(c + 1) * TCORE]),
            "wgT": wgT, "wuT": wuT, "wdT": wdT, "wpT": wpT,
            "sel16_in": sel16, "selc_in": selc,
        })
    nc3 = L3().build()
    r3 = run_bass_kernel_spmd(nc3, im3, core_ids=list(range(NCORE))).results
    out = np.concatenate([r["x3_out"].T for r in r3], axis=0)
    return np.ascontiguousarray(out[None, :, :])


# version 06
# speedup vs baseline: 1.2101x; 1.2101x over previous
import contextlib
import numpy as np
import concourse.bass as bass
import concourse.mybir as mybir

F32 = mybir.dt.float32
BF16 = mybir.dt.bfloat16
I32 = mybir.dt.int32
AF = mybir.ActivationFunctionType
ALU = mybir.AluOpType
AX = mybir.AxisListType

ENGINES = ("sync", "tensor", "vector", "scalar", "gpsimd")


class Sem:
    __slots__ = ("name", "count", "h")

    def __init__(self, name, h):
        self.name = name
        self.count = 0
        self.h = h


class Prog:

    def __init__(self, nc, es):
        self.nc = nc
        self.es = es
        self.ops = []

    def sem(self, name):
        return Sem(name, self.es.enter_context(self.nc.semaphore(name)))

    def sbuf(self, name, shape, dtype):
        return self.es.enter_context(self.nc.sbuf_tensor(name, shape, dtype))

    def psum(self, name, shape, dtype=F32):
        return self.es.enter_context(self.nc.psum_tensor(name, shape, dtype))

    def op(self, engine, fn, waits=(), inc=None, incs=None):
        assert engine in ENGINES, engine
        for s, v in waits:
            assert v <= s.count, (s.name, v, s.count, "wait on future sem value")
        ilist = list(incs) if incs else ([] if inc is None else [inc])
        for s, n in ilist:
            s.count += n
        self.ops.append((engine, tuple(waits), fn, ilist))

    def dma(self, engine, out, in_, waits=(), inc=None):
        def fn(eng, out=out, in_=in_):
            return eng.dma_start(out=out, in_=in_)
        self.op(engine, fn, waits=waits, inc=inc)

    def dma_ch(self, engine, out, in_, sem, waits=()):
        def fn(eng, out=out, in_=in_):
            return eng.dma_start(out=out, in_=in_)
        self.op(engine, fn, waits=waits, inc=(sem, 16))

    def build(self):
        with self.nc.Block() as block:
            for name in ENGINES:
                ops = [o for o in self.ops if o[0] == name]
                if not ops:
                    continue

                def make(ops):
                    def emit(eng):
                        for _, waits, fn, ilist in ops:
                            for s, v in waits:
                                eng.wait_ge(s.h, v)
                            ins = fn(eng)
                            for s, n in ilist:
                                ins = ins.then_inc(s.h, n)
                    return emit

                getattr(block, name)(make(ops))


def ceil_div(a, b):
    return -(-a // b)



RSQRT_HD = 0.08838834764831845
EPS_LN = 1e-5
EPS_RMS = 1.1920929e-07


class L1:
    def __init__(self, T=1024, TT=512, D=2048, H=8192, NH=16, NKV=4):
        self.__dict__.update(T=T, TT=TT, D=D, H=H, NH=NH, NKV=NKV)
        self.CD, self.CH, self.NT = D // 128, H // 128, T // TT
        self.O3 = (NH + 2 * NKV) * 128

    def build(self):
        T, TT, D, H, NH, NKV = self.T, self.TT, self.D, self.H, self.NH, self.NKV
        CD, O3 = self.CD, self.O3
        nc = bass.Bass("TRN2", target_bir_lowering=False, debug=False,
                       detect_race_conditions=False)

        dr = {}
        dr["x_fm"] = nc.dram_tensor("x_fm", [D, T], F32, kind="ExternalInput").ap()
        dr["wgT"] = nc.dram_tensor("wgT", [D, H], F32, kind="ExternalInput").ap()
        dr["wuT"] = nc.dram_tensor("wuT", [D, H], F32, kind="ExternalInput").ap()
        dr["wdT"] = nc.dram_tensor("wdT", [H, D], F32, kind="ExternalInput").ap()
        dr["wqkvT"] = nc.dram_tensor("wqkvT", [D, O3], F32, kind="ExternalInput").ap()
        dr["cos_t"] = nc.dram_tensor("cos_t", [128, T], F32, kind="ExternalInput").ap()
        dr["sin_t"] = nc.dram_tensor("sin_t", [128, T], F32, kind="ExternalInput").ap()
        dr["rotc_in"] = nc.dram_tensor("rotc_in", [128, 128], F32, kind="ExternalInput").ap()
        dr["gain_in"] = nc.dram_tensor("gain_in", [1, NH], F32, kind="ExternalInput").ap()
        dr["eye_in"] = nc.dram_tensor("eye_in", [128, 128], F32, kind="ExternalInput").ap()
        dr["sel16_in"] = nc.dram_tensor("sel16_in", [16, 16 * 128], BF16, kind="ExternalInput").ap()
        dr["selc_in"] = nc.dram_tensor("selc_in", [128, 2 * CD - 1], BF16, kind="ExternalInput").ap()

        dr["x1_out"] = nc.dram_tensor("x1_out", [D, T], F32, kind="ExternalOutput").ap()
        dr["q_out"] = nc.dram_tensor("q_out", [NH, 128, T], F32, kind="ExternalOutput").ap()
        dr["k_out"] = nc.dram_tensor("k_out", [NKV, 128, T], F32, kind="ExternalOutput").ap()
        dr["v_out"] = nc.dram_tensor("v_out", [NKV, T, 128], F32, kind="ExternalOutput").ap()
        if getattr(self, "debug_weff", False):
            dr["dbg_weff"] = nc.dram_tensor("dbg_weff", [self.O3 // 256, 128, 16 * 256], F32,
                                            kind="ExternalOutput").ap()
            dr["dbg_wb"] = nc.dram_tensor("dbg_wb", [self.O3 // 256, 128, 16 * 256], F32,
                                          kind="ExternalOutput").ap()

        with contextlib.ExitStack() as es:
            p = Prog(nc, es)
            self._body(nc, p, dr)
            p.build()
        return nc

    def _body(self, nc, p, dr):
        T, TT, D, H, NH, NKV = self.T, self.TT, self.D, self.H, self.NH, self.NKV
        CD, CH, NT, O3 = self.CD, self.CH, self.NT, self.O3
        SLAB = 256
        NSLAB = O3 // SLAB

        xT = p.sbuf("xT", [128, CD * T], F32)
        h1 = p.sbuf("h1", [128, CD * T], F32)
        act = p.sbuf("act", [128, 2 * T], F32)
        wg_s = p.sbuf("wg_s", [128, D], F32)
        wu_s = p.sbuf("wu_s", [128, D], F32)
        wd_s = p.sbuf("wd_s", [128, 2 * D], F32)
        lnss = p.sbuf("lnss", [128, TT], F32)
        asc = p.sbuf("asc", [128, TT], F32)
        wst = p.sbuf("wst", [128, 2 * D], F32)
        h1b = h1.bitcast(BF16)
        actb = act.bitcast(BF16)
        wgb = wg_s.bitcast(BF16)
        wub = wu_s.bitcast(BF16)
        wdb = wd_s.bitcast(BF16)
        xTb = xT.bitcast(BF16)

        sq = p.sbuf("sq", [128, TT], F32)
        cos2 = wst[:, 0:T]
        sin2 = wst[:, T:2 * T]
        rotc = wst[:, 2 * T:2 * T + 128]
        eye_s = wst[:, 2 * T + 128:2 * T + 256]
        stats = p.sbuf("stats", [128, 2 * TT], F32)
        consts = p.sbuf("consts", [128, 144], F32)
        gain_s = p.sbuf("gain_s", [1, NH], F32)
        selcol = p.sbuf("selcol", [128, 2 * CD - 1], BF16)

        wbb = h1[:, 0:CD * SLAB // 2].bitcast(BF16)
        wab = h1[:, 2048:4096].bitcast(BF16)
        rb = h1[:, 4096:4224].bitcast(BF16)
        g1 = h1[:, 4224:4480]
        g2 = h1[:, 4480:4736]
        qraw = h1[:, 4736:5248]
        ro = h1[:, 5248:5760]
        sclb = h1[0:16, 6784:6912].bitcast(BF16)
        inv_bc = h1[:, 6912:7168]
        sel16 = h1[0:16, 5760:6784].bitcast(BF16)

        ps = [p.psum(f"ps{i}", [128, 512], F32) for i in range(8)]
        PS_GATE, PS_UP, PS_D0, PS_D1, PS_S0, PS_S1, PS_B0, PS_B1 = range(8)

        s_in = p.sem("s_in")
        s_gu = [p.sem("s_gu0"), p.sem("s_gu1")]
        s_wd = [p.sem("s_wd0"), p.sem("s_wd1")]
        s_st = [p.sem("s_st0"), p.sem("s_st1")]
        s_x1 = p.sem("s_x1")
        s_out = p.sem("s_out")
        s_dbg = p.sem("s_dbg")
        mm = p.sem("mms")
        vv = p.sem("vvs")
        aa = p.sem("aas")

        self._sqpar = 0
        self._sq_marks = {}
        self._free = {}
        self._weff_dump = {}

        p.dma_ch("sync", xT[:, :].rearrange("p (c t) -> p c t", c=CD),
                 dr["x_fm"].rearrange("(c p) t -> p c t", p=128), s_in)
        p.dma_ch("sync", gain_s[:, :], dr["gain_in"], s_in)
        p.dma_ch("sync", selcol[:, :], dr["selc_in"], s_in)
        lds0 = s_in.count
        p.op("vector", lambda eng: eng.memset(consts[:, 0:1], 1.0), inc=(vv, 1))
        p.op("vector", lambda eng: eng.memset(consts[0:1, 1:129], 1.0), inc=(vv, 1))
        consts_ready = vv.count
        ones_col = consts[:, 0:1]
        ones_row = consts[0:1, 1:129]

        A = stats[0:1, 0:TT]
        B = stats[0:1, TT:2 * TT]
        C = stats[0:1, 0:TT]
        NSC = lnss[0:1, 0:TT]

        def rstd_from_B(extra_eps, width):
            p.op("vector", lambda eng: eng.tensor_scalar_add(
                B[:, 0:width], B[:, 0:width], extra_eps), inc=(vv, 1))
            p.op("vector", lambda eng: eng.reciprocal(
                C[:, 0:width], B[:, 0:width]), inc=(vv, 1))
            p.op("scalar", lambda eng: eng.activation(
                C[:, 0:width], C[:, 0:width], AF.Sqrt),
                waits=[(vv, vv.count)], inc=(aa, 1))
            p.op("scalar", lambda eng: eng.activation(
                NSC[:, 0:width], C[:, 0:width], AF.Square), inc=(aa, 1))
            p.op("vector", lambda eng: eng.tensor_mul(
                NSC[:, 0:width], NSC[:, 0:width], B[:, 0:width]),
                waits=[(aa, aa.count)], inc=(vv, 1))
            p.op("vector", lambda eng: eng.tensor_scalar(
                NSC[:, 0:width], NSC[:, 0:width], -0.5, 1.5, ALU.mult, ALU.add),
                inc=(vv, 1))
            p.op("vector", lambda eng: eng.tensor_mul(
                C[:, 0:width], C[:, 0:width], NSC[:, 0:width]), inc=(vv, 1))
            return vv.count

        def ln(src_view, dst_view, eps, wait0=(), n_chunks=CD, width=TT, split=None):
            ndim = n_chunks * 128
            w0 = list(wait0) + [(vv, consts_ready)]
            for c in range(n_chunks):
                wsq = [(mm, self._sq_marks[0])] if 0 in self._sq_marks else []
                p.op("scalar", lambda eng, c=c: eng.activation(
                    sq[:, 0:width], src_view(c), AF.Square),
                    waits=wsq + (w0 if c == 0 else []), inc=(aa, 1))
                p.op("tensor", lambda eng, c=c: eng.matmul(
                    ps[PS_S0][0:1, 0:width], ones_col, src_view(c),
                    start=(c == 0), stop=(c == n_chunks - 1)),
                    waits=(w0 + self._free.get(PS_S0, [])) if c == 0 else (),
                    inc=(mm, 1))
                p.op("tensor", lambda eng, c=c: eng.matmul(
                    ps[PS_S1][0:1, 0:width], ones_col, sq[:, 0:width],
                    start=(c == 0), stop=(c == n_chunks - 1)),
                    waits=[(aa, aa.count)] + (self._free.get(PS_S1, []) if c == 0 else []),
                    inc=(mm, 1))
                self._sq_marks[0] = mm.count
            stats_mm = mm.count
            w = [(mm, stats_mm)]
            p.op("vector", lambda eng: eng.tensor_scalar_mul(
                A[:, 0:width], ps[PS_S0][0:1, 0:width], 1.0 / ndim),
                waits=w, inc=(vv, 1))
            self._free[PS_S0] = [(vv, vv.count)]
            p.op("vector", lambda eng: eng.tensor_scalar_mul(
                B[:, 0:width], ps[PS_S1][0:1, 0:width], 1.0 / ndim),
                waits=w, inc=(vv, 1))
            self._free[PS_S1] = [(vv, vv.count)]
            mean_done = vv.count
            p.op("tensor", lambda eng: eng.matmul(
                ps[PS_B0][:, 0:width], ones_row, A[:, 0:width],
                start=True, stop=True),
                waits=[(vv, mean_done)] + self._free.get(PS_B0, []), inc=(mm, 1))
            mbc = mm.count
            p.op("scalar", lambda eng: eng.activation(
                NSC[:, 0:width], A[:, 0:width], AF.Square),
                waits=[(vv, mean_done)], inc=(aa, 1))
            p.op("vector", lambda eng: eng.tensor_sub(
                B[:, 0:width], B[:, 0:width], NSC[:, 0:width]),
                waits=[(aa, aa.count), (mm, mbc)], inc=(vv, 1))
            rstd_done = rstd_from_B(eps, width)
            p.op("tensor", lambda eng: eng.matmul(
                ps[PS_B1][:, 0:width], ones_row, C[:, 0:width],
                start=True, stop=True),
                waits=[(vv, rstd_done)] + self._free.get(PS_B1, []), inc=(mm, 1))
            bca = mm.count
            extra_w = []
            for c in range(n_chunks):
                p.op("vector", lambda eng, c=c: eng.tensor_sub(
                    dst_view(c), src_view(c), ps[PS_B0][:, 0:width]),
                    waits=([(mm, bca)] if c == 0 else []) + extra_w, inc=(vv, 1))
                p.op("vector", lambda eng, c=c: eng.tensor_mul(
                    dst_view(c), dst_view(c), ps[PS_B1][:, 0:width]), inc=(vv, 1))
                if split is not None:
                    extra_w, _ = split(c, vv.count)
            self._free[PS_B0] = [(vv, vv.count)]
            self._free[PS_B1] = [(vv, vv.count)]
            return vv.count, bca

        def xv(tile):
            return lambda c: xT[:, c * T + tile * TT: c * T + (tile + 1) * TT]

        gu_marks = {}
        wd_marks = {}
        x1dma = []
        s_wl = [p.sem("s_wl0"), p.sem("s_wl1")]
        self._slot_free = {}
        self._slot_seq = 0

        def stage_load(dram_slab, pair_view, gseq_key, marks, split_waits):
            slot = self._slot_seq % 2
            self._slot_seq += 1
            sem = s_wl[slot]
            stage = wst[:, slot * D:(slot + 1) * D]
            p.dma_ch("sync", stage.rearrange("p (c n) -> p c n", c=CD), dram_slab,
                     sem, waits=self._slot_free.get(slot, []))
            ld = (sem, sem.count)
            p.op("scalar", lambda eng, stage=stage, pv=pair_view: eng.copy(
                pv[:, 0:D], stage), waits=[ld] + list(split_waits), inc=(aa, 1))
            him = aa.count
            p.op("vector", lambda eng, stage=stage, pv=pair_view: eng.tensor_sub(
                pv[:, D:2 * D], stage, pv[:, 0:D]),
                waits=[(aa, him)], inc=(vv, 1))
            lom = vv.count
            self._slot_free[slot] = [(vv, lom)]
            marks[gseq_key] = (him, lom)

        gu_sp = {}
        wd_sp = {}

        def sched_gu(h, wgw, wuw):
            stage_load(dr["wgT"][:, h * 128:(h + 1) * 128].rearrange("(c p) n -> p c n", p=128),
                       wgb[:, 0:2 * D], ("g", h), gu_sp, wgw)
            stage_load(dr["wuT"][:, h * 128:(h + 1) * 128].rearrange("(c p) n -> p c n", p=128),
                       wub[:, 0:2 * D], ("u", h), gu_sp, wuw)

        def sched_wd(g, ww):
            for i in range(2):
                h = 2 * g + i
                stage_load(dr["wdT"][h * 128:(h + 1) * 128, :].rearrange("p (c n) -> p c n", c=CD),
                           wdb[:, i * 2 * D:(i + 1) * 2 * D], ("d", g, i), wd_sp, ww)

        def split_chunk(c, src_ap, hi_ap, lo_ap, ready_mark):
            p.op("scalar", lambda eng: eng.copy(hi_ap, src_ap),
                 waits=[(vv, ready_mark)], inc=(aa, 1))
            him = aa.count
            p.op("vector", lambda eng: eng.tensor_sub(lo_ap, src_ap, hi_ap),
                 waits=[(aa, him)], inc=(vv, 1))
            return [(aa, him), (vv, vv.count)], vv.count

        def h1h(c, tt):
            return h1b[:, c * T + tt * TT: c * T + tt * TT + TT]

        def h1l(c, tt):
            return h1b[:, CD * T + c * T + tt * TT: CD * T + c * T + tt * TT + TT]

        sched_gu(0, [], [])
        sched_wd(0, [])

        ha1 = []
        for tile in range(NT):
            mark, _ = ln(xv(tile), lambda c: lnss[:, 0:TT], EPS_LN,
                         wait0=([(s_in, lds0)] if tile == 0 else []),
                         split=(lambda tile: lambda c, rdy: split_chunk(
                             c, lnss[:, 0:TT], h1h(c, tile), h1l(c, tile), rdy))(tile))
            ha1.append(vv.count)
        ln1_done = vv.count

        G_B = (PS_GATE, PS_UP)
        U_B = (PS_D0, PS_D1)
        D_B = (PS_S0, PS_S1, PS_B0, PS_B1)

        asc_bufs = (asc[:, 0:TT], lnss[:, 0:TT])
        asc_free = {0: [], 1: []}
        act_free = []
        act_ready = {}

        for h in range(CH):
            g, i2 = h // 2, h % 2
            wgv = wgb[:, 0:2 * D]
            wuv = wub[:, 0:2 * D]
            gum = {}
            for wv, banks, nm in ((wgv, G_B, "g"), (wuv, U_B, "u")):
                for tt in range(NT):
                    bank = banks[tt]
                    i_ = 0
                    for k in range(CD):
                        for lhs, rhs in ((wv[:, k * 128:(k + 1) * 128], h1h(k, tt)),
                                         (wv[:, k * 128:(k + 1) * 128], h1l(k, tt)),
                                         (wv[:, D + k * 128:D + (k + 1) * 128], h1h(k, tt))):
                            p.op("tensor", lambda eng, lhs=lhs, rhs=rhs, bank=bank, i_=i_:
                                 eng.matmul(ps[bank][:, 0:TT], lhs, rhs,
                                            start=(i_ == 0), stop=(i_ == 3 * CD - 1)),
                                 waits=([(vv, gu_sp[(nm, h)][1]), (vv, ha1[tt])]
                                        + self._free.get(bank, [])) if i_ == 0 else (),
                                 inc=(mm, 1) if i_ == 3 * CD - 1 else None)
                            i_ += 1
                    gum[(nm, tt)] = mm.count
            gu_marks[h] = mm.count
            if h + 1 < CH:
                sched_gu(h + 1, [(mm, gum[("g", NT - 1)])], [(mm, gu_marks[h])])
            for tt in range(NT):
                ascv = asc_bufs[tt]
                p.op("scalar", lambda eng, ascv=ascv, tt=tt: eng.activation(
                    ascv, ps[G_B[tt]][:, 0:TT], AF.Silu),
                    waits=[(mm, gum[("g", tt)])] + asc_free[tt], inc=(aa, 1))
                self._free[G_B[tt]] = [(aa, aa.count)]
                p.op("vector", lambda eng, ascv=ascv, tt=tt: eng.tensor_mul(
                    ascv, ascv, ps[U_B[tt]][:, 0:TT]),
                    waits=[(mm, gum[("u", tt)]), (aa, aa.count)], inc=(vv, 1))
                self._free[U_B[tt]] = [(vv, vv.count)]
                ach = actb[:, i2 * T + tt * TT: i2 * T + tt * TT + TT]
                acl = actb[:, 2 * T + i2 * T + tt * TT: 2 * T + i2 * T + tt * TT + TT]
                p.op("scalar", lambda eng, ach=ach, ascv=ascv: eng.copy(ach, ascv),
                     waits=[(vv, vv.count)] + act_free, inc=(aa, 1))
                p.op("vector", lambda eng, ach=ach, acl=acl, ascv=ascv: eng.tensor_sub(
                    acl, ascv, ach), waits=[(aa, aa.count)], inc=(vv, 1))
                asc_free[tt] = [(vv, vv.count)]
                act_ready[(i2, tt)] = vv.count
            if i2 == 1:
                for o in range(CD):
                    for tt in range(NT):
                        bank = D_B[(o % 2) * 2 + tt]
                        wdn = list(self._free.get(bank, []))
                        if o == 0 and tt == 0:
                            wdn += [(vv, act_ready[(0, 0)]), (vv, act_ready[(0, 1)]),
                                    (vv, act_ready[(1, 0)]), (vv, act_ready[(1, 1)]),
                                    (vv, wd_sp[("d", g, 0)][1]),
                                    (vv, wd_sp[("d", g, 1)][1])]
                        i_ = 0
                        for i3 in range(2):
                            wdv = wdb[:, i3 * 2 * D:(i3 + 1) * 2 * D]
                            a_h = actb[:, i3 * T + tt * TT: i3 * T + tt * TT + TT]
                            a_l = actb[:, 2 * T + i3 * T + tt * TT: 2 * T + i3 * T + tt * TT + TT]
                            for lhs, rhs in ((wdv[:, o * 128:(o + 1) * 128], a_h),
                                             (wdv[:, o * 128:(o + 1) * 128], a_l),
                                             (wdv[:, D + o * 128:D + (o + 1) * 128], a_h)):
                                p.op("tensor", lambda eng, lhs=lhs, rhs=rhs, i_=i_, bank=bank:
                                     eng.matmul(ps[bank][:, 0:TT], lhs, rhs,
                                                start=(i_ == 0), stop=(i_ == 5)),
                                     waits=wdn if i_ == 0 else (),
                                     inc=(mm, 1) if i_ == 5 else None)
                                i_ += 1
                        dn_mm = mm.count
                        xvv = xT[:, o * T + tt * TT: o * T + tt * TT + TT]
                        stw = [(mm, dn_mm)]
                        if g == 0:
                            stw.append((vv, ln1_done))
                        p.op("vector", lambda eng, xvv=xvv, bank=bank:
                             eng.scalar_tensor_tensor(
                                 xvv, ps[bank][:, 0:TT], 0.5, xvv,
                                 ALU.mult, ALU.add),
                             waits=stw, inc=(vv, 1))
                        self._free[bank] = [(vv, vv.count)]
                wd_marks[g] = mm.count
                act_free = [(mm, mm.count)]
                if g + 1 < CH // 2:
                    sched_wd(g + 1, [(mm, wd_marks[g])])
        x1_done = vv.count
        p.dma_ch("gpsimd", dr["x1_out"].rearrange("(c p) t -> p c t", p=128),
                 xT[:, :].rearrange("p (c t) -> p c t", c=CD),
                 s_x1, waits=[(vv, x1_done)])
        x1dma.append(s_x1.count)

        def xh(c, t):
            return xTb[:, 2 * (c * T + t * TT): 2 * (c * T + t * TT) + TT]

        def xl(c, t):
            return xTb[:, 2 * (c * T + t * TT) + TT: 2 * (c * T + t * TT) + 2 * TT]

        ha_ready = []
        for tile in range(NT):
            mark, _ = ln(xv(tile), lambda c: lnss[:, 0:TT], EPS_LN,
                         wait0=[(s_x1, x1dma[-1])],
                         split=(lambda tile: lambda c, rdy: split_chunk(
                             c, lnss[:, 0:TT], xh(c, tile), xl(c, tile), rdy))(tile))
            ha_ready.append(vv.count)

        stage_bufs = (wd_s, h1[:, 12288:16384])
        weff_bufs = (h1b[:, 16384:16384 + CD * SLAB],
                     h1b[:, 20480:20480 + CD * SLAB])
        stage_ld = {}
        stage_read = {}
        weff_done = {}
        wb_read = {}

        def load_slab(s):
            w = []
            if s - 2 in stage_read:
                w.append((aa, stage_read[s - 2]))
            if s == 0:
                w.append((vv, ha_ready[-1]))
            sem = s_st[s % 2]
            p.dma_ch("sync",
                     stage_bufs[s % 2][:, 0:CD * SLAB].rearrange("p (c n) -> p c n", c=CD),
                     dr["wqkvT"][:, s * SLAB:(s + 1) * SLAB].rearrange("(c p) n -> p c n", p=128),
                     sem, waits=w)
            stage_ld[s] = (sem, sem.count)

        p.dma_ch("sync", sel16, dr["sel16_in"], s_in,
                 waits=[(s_x1, x1dma[-1])])
        wstw = self._slot_free.get(0, []) + self._slot_free.get(1, [])
        p.dma_ch("sync", cos2, dr["cos_t"], s_in, waits=wstw)
        p.dma_ch("sync", sin2, dr["sin_t"], s_in)
        p.dma_ch("sync", rotc, dr["rotc_in"], s_in)
        p.dma_ch("sync", eye_s, dr["eye_in"], s_in)
        sel16_ld = s_in.count
        load_slab(0)
        load_slab(1)

        for s in range(NSLAB):
            par = s % 2
            stage = stage_bufs[par]
            weff = weff_bufs[par]
            w = [stage_ld[s]]
            if s - 1 in wb_read:
                w.append((vv, wb_read[s - 1]))
            if s == 0:
                w.append((s_x1, x1dma[-1]))
            p.op("scalar", lambda eng, stage=stage: eng.copy(
                wbb[:, :], stage[:, 0:CD * SLAB]), waits=w, inc=(aa, 1))
            p.op("scalar", lambda eng, stage=stage: eng.activation(
                wab[:, :], stage[:, 0:CD * SLAB], AF.Abs), inc=(aa, 1))
            stage_read[s] = aa.count
            if s + 2 < NSLAB:
                load_slab(s + 2)
            if getattr(self, "debug_weff", False):
                p.op("vector", lambda eng, stage=stage: eng.tensor_copy(
                    h1[:, 7936:8192], stage[:, 0:256]), waits=[(aa, aa.count)], inc=(vv, 1))
                p.dma_ch("gpsimd", dr["dbg_wb"][s][:, 0:256], h1[:, 7936:8192],
                         s_dbg, waits=[(vv, vv.count)])
                self._weff_dump[s * 100 + 50] = s_dbg.count
            wab_ready = vv.count
            for c in range(CD):
                p.op("tensor", lambda eng, c=c: eng.matmul(
                    ps[PS_S0][0:16, 0:SLAB], selcol[:, (CD - 1 - c):(2 * CD - 1 - c)],
                    wab[:, c * SLAB:(c + 1) * SLAB],
                    start=(c == 0), stop=(c == CD - 1)),
                    waits=([(aa, stage_read[s])] + self._free.get(PS_S0, []))
                    if c == 0 else (), inc=(mm, 1))
            gsum_mm = mm.count
            p.op("vector", lambda eng: eng.tensor_scalar_mul(
                sclb, ps[PS_S0][0:16, 0:SLAB], 1.0 / 128.0),
                waits=[(mm, gsum_mm)], inc=(vv, 1))
            self._free[PS_S0] = [(vv, vv.count)]
            p.op("vector", lambda eng: eng.tensor_scalar_max(
                sclb, sclb, 1e-8), inc=(vv, 1))
            scl_done = vv.count
            for c in range(CD):
                p.op("tensor", lambda eng, c=c: eng.matmul(
                    ps[PS_B1][:, 0:SLAB], sel16[:, c * 128:(c + 1) * 128],
                    sclb, start=True, stop=True),
                    waits=([(vv, scl_done), (s_in, sel16_ld)] if c == 0 else [])
                    + self._free.get(PS_B1, []), inc=(mm, 1))
                bc = mm.count
                p.op("vector", lambda eng: eng.reciprocal(
                    inv_bc, ps[PS_B1][:, 0:SLAB]), waits=[(mm, bc)], inc=(vv, 1))
                p.op("vector", lambda eng, c=c: eng.tensor_mul(
                    rb, wbb[:, c * SLAB:(c + 1) * SLAB], inv_bc),
                    inc=(vv, 1))
                p.op("vector", lambda eng: eng.tensor_single_scalar(
                    g1, rb, 0.5, ALU.is_gt), inc=(vv, 1))
                p.op("vector", lambda eng: eng.tensor_single_scalar(
                    g2, rb, -0.5, ALU.is_lt), inc=(vv, 1))
                p.op("vector", lambda eng: eng.tensor_sub(g1, g1, g2), inc=(vv, 1))
                p.op("vector", lambda eng: eng.tensor_mul(
                    g2, g1, ps[PS_B1][:, 0:SLAB]), inc=(vv, 1))
                self._free[PS_B1] = [(vv, vv.count)]
                p.op("vector", lambda eng, c=c: eng.tensor_sub(
                    rb, g2, wbb[:, c * SLAB:(c + 1) * SLAB]), inc=(vv, 1))
                wwf = []
                if c == 0 and s - 2 in weff_done:
                    wwf.append((mm, weff_done[s - 2]))
                    if self._weff_dump:
                        wwf.append((s_dbg, max(self._weff_dump.values())))
                p.op("vector", lambda eng, c=c, weff=weff: eng.tensor_add(
                    weff[:, c * SLAB:(c + 1) * SLAB],
                    wbb[:, c * SLAB:(c + 1) * SLAB], rb),
                    waits=wwf, inc=(vv, 1))
            wb_read[s] = vv.count
            weff_ready = vv.count
            if getattr(self, "debug_weff", False):
                p.dma_ch("gpsimd", dr["dbg_weff"][s], weff[:, 0:CD * SLAB],
                         s_dbg, waits=[(vv, weff_ready)])
                self._weff_dump[s] = s_dbg.count

            for tile in range(NT):
                for oc in range(2):
                    bank = PS_GATE if oc == 0 else PS_UP
                    og = s * SLAB // 128 + oc
                    i_ = 0
                    for k in range(CD):
                        wt = weff[:, k * SLAB + oc * 128: k * SLAB + oc * 128 + 128]
                        for rhs in (xh(k, tile), xl(k, tile)):
                            p.op("tensor",
                                 lambda eng, wt=wt, rhs=rhs, bank=bank, i_=i_:
                                 eng.matmul(
                                     ps[bank][:, 0:TT], wt, rhs,
                                     start=(i_ == 0), stop=(i_ == 2 * CD - 1)),
                                 waits=[(vv, weff_ready), (vv, vv.count), (aa, aa.count),
                                        (vv, ha_ready[tile])]
                                 if i_ == 0 else (), inc=(mm, 1) if i_ == 2 * CD - 1 else None)
                            i_ += 1
                    qkv_mm = mm.count
                    self._head(p, dr, og, tile, bank, qkv_mm, locals())
            weff_done[s] = mm.count

        p.op("gpsimd", lambda eng: eng.engine_nop(), waits=[(s_out, s_out.count), (s_x1, s_x1.count)])

    def _head(self, p, dr, og, tile, bank, qkv_mm, env):
        T, TT, NH, NKV = self.T, self.TT, self.NH, self.NKV
        ps = env["ps"]
        PS_D0, PS_D1, PS_S1, PS_B1 = 2, 3, 5, 7
        s_out, mm, vv, aa = (env[k] for k in ("s_out", "mm", "vv", "aa"))
        qraw, ro, sq = env["qraw"], env["ro"], env["sq"]
        cos2, sin2, rotc = env["cos2"], env["sin2"], env["rotc"]
        gain_s, eye_s = env["gain_s"], env["eye_s"]
        ones_col, ones_row = env["ones_col"], env["ones_row"]
        A, B, C = env["A"], env["B"], env["C"]
        s_in, lds0 = env["s_in"], env["lds0"]
        cosv = cos2[:, tile * TT:(tile + 1) * TT]
        sinv = sin2[:, tile * TT:(tile + 1) * TT]
        rstd_from_B = env["rstd_from_B"]

        p.op("vector", lambda eng: eng.tensor_copy(qraw, ps[bank][:, 0:TT]),
             waits=[(mm, qkv_mm)], inc=(vv, 1))
        evac = vv.count

        if og < NH:
            h, kind = og, "q"
        elif og < NH + NKV:
            h, kind = og - NH, "k"
        else:
            h, kind = og - NH - NKV, "v"

        if kind == "v":
            for st in range(TT // 128):
                p.op("tensor", lambda eng, st=st: eng.transpose(
                    ps[PS_D0][:, st * 128:(st + 1) * 128],
                    qraw[:, st * 128:(st + 1) * 128], eye_s[:, :]),
                    waits=([(vv, evac)] + self._free.get(PS_D0, []))
                    if st == 0 else (),
                    inc=(mm, 1) if st == TT // 128 - 1 else None)
            tmark = mm.count
            p.op("vector", lambda eng: eng.tensor_copy(ro, ps[PS_D0][:, 0:TT]),
                 waits=[(mm, tmark)], inc=(vv, 1))
            self._free[PS_D0] = [(vv, vv.count)]
            rmark = vv.count
            for st in range(TT // 128):
                p.dma_ch("gpsimd",
                         dr["v_out"][h][tile * TT + st * 128: tile * TT + (st + 1) * 128, :],
                         ro[:, st * 128:(st + 1) * 128],
                         s_out, waits=[(vv, rmark)] if st == 0 else ())
            return

        wsq = [(mm, self._sq_marks[0])] if 0 in self._sq_marks else []
        p.op("scalar", lambda eng: eng.activation(
            sq[:, 0:TT], qraw, AF.Square),
            waits=wsq + [(vv, evac)], inc=(aa, 1))
        p.op("tensor", lambda eng: eng.matmul(
            ps[PS_S1][0:1, 0:TT], ones_col, sq[:, 0:TT],
            start=True, stop=True),
            waits=[(aa, aa.count)] + self._free.get(PS_S1, []), inc=(mm, 1))
        self._sq_marks[0] = mm.count
        smm = mm.count
        p.op("tensor", lambda eng: eng.matmul(
            ps[PS_D1][:, 0:TT], rotc[:, :], qraw, start=True, stop=True),
            waits=[(s_in, lds0), (vv, evac)] + self._free.get(PS_D1, []), inc=(mm, 1))
        rxm = mm.count
        p.op("vector", lambda eng: eng.tensor_scalar_mul(
            B, ps[PS_S1][0:1, 0:TT], 1.0 / 128.0),
            waits=[(mm, smm)], inc=(vv, 1))
        self._free[PS_S1] = [(vv, vv.count)]
        rstd_from_B(EPS_RMS, TT)
        if kind == "q":
            p.op("vector", lambda eng: eng.tensor_scalar_mul(
                C, C, RSQRT_HD), inc=(vv, 1))
            p.op("vector", lambda eng, h=h: eng.tensor_scalar_mul(
                C, C, gain_s[0:1, h:h + 1]), inc=(vv, 1))
        rstd_done = vv.count
        p.op("tensor", lambda eng: eng.matmul(
            ps[PS_B1][:, 0:TT], ones_row, C, start=True, stop=True),
            waits=[(vv, rstd_done)] + self._free.get(PS_B1, []), inc=(mm, 1))
        bmark = mm.count
        p.op("vector", lambda eng: eng.tensor_mul(ro, qraw, cosv), inc=(vv, 1))
        p.op("vector", lambda eng: eng.tensor_mul(qraw, ps[PS_D1][:, 0:TT], sinv),
             waits=[(mm, rxm)], inc=(vv, 1))
        self._free[PS_D1] = [(vv, vv.count)]
        p.op("vector", lambda eng: eng.tensor_add(ro, ro, qraw), inc=(vv, 1))
        p.op("vector", lambda eng: eng.tensor_mul(ro, ro, ps[PS_B1][:, 0:TT]),
             waits=[(mm, bmark)], inc=(vv, 1))
        self._free[PS_B1] = [(vv, vv.count)]
        omark = vv.count
        out = dr["q_out"] if kind == "q" else dr["k_out"]
        p.dma_ch("gpsimd", out[h][:, tile * TT:(tile + 1) * TT], ro,
                 s_out, waits=[(vv, omark)])



class L2:

    def __init__(self, S=8192, NHL=2):
        self.S = S
        self.NHL = NHL

    def build(self):
        S, NHL = self.S, self.NHL
        NQB = S // 512
        NC = S // 128
        nc = bass.Bass("TRN2", target_bir_lowering=False, debug=False,
                       detect_race_conditions=False)
        dr = {}
        dr["q2"] = nc.dram_tensor("q2", [NHL, 128, S], F32, kind="ExternalInput").ap()
        dr["k2"] = nc.dram_tensor("k2", [128, S], F32, kind="ExternalInput").ap()
        dr["v2"] = nc.dram_tensor("v2", [S, 128], F32, kind="ExternalInput").ap()
        dr["y2"] = nc.dram_tensor("y2", [NHL, 128, S], F32, kind="ExternalOutput").ap()

        with contextlib.ExitStack() as es:
            p = Prog(nc, es)
            q2 = p.sbuf("q2s", [128, NHL * S], F32)
            k2 = p.sbuf("k2s", [128, S], F32)
            v2 = p.sbuf("v2s", [128, S], F32)
            q2b, k2b, v2b = q2.bitcast(BF16), k2.bitcast(BF16), v2.bitcast(BF16)
            ee = p.sbuf("ee", [128, 2 * 512], F32)
            eeb = p.sbuf("eeb", [128, 2 * 1024], BF16)
            bc = p.sbuf("bc", [128, 512], F32)
            av = p.sbuf("av", [128, 2 * 512], F32)
            rr = p.sbuf("rr", [1, 512], F32)
            consts = p.sbuf("c2", [128, 134], F32)
            bnc = p.sbuf("bnc", [128, 512], F32)
            ps = [p.psum(f"ps{i}", [128, 512], F32) for i in range(8)]
            PS_AV0, PS_AV1, PS_R0, PS_R1, PS_S0, PS_S1, PS_B = range(7)

            s_in = p.sem("s_in")
            s_out = p.sem("s_out")
            mm = p.sem("mms")
            vv = p.sem("vvs")
            aa = p.sem("aas")
            gg = p.sem("ggs")

            p.dma_ch("sync", q2[:, :].rearrange("p (h s) -> p h s", h=NHL),
                     dr["q2"].rearrange("h p s -> p h s"), s_in)
            p.dma_ch("sync", k2[:, :], dr["k2"], s_in)
            p.dma_ch("sync", v2[:, :].rearrange("p (c n) -> p c n", c=NC),
                     dr["v2"].rearrange("(c p) n -> p c n", p=128), s_in)
            lds = s_in.count
            p.op("vector", lambda eng: eng.memset(consts[:, 0:1], 1.0), inc=(vv, 1))
            p.op("vector", lambda eng: eng.memset(consts[0:1, 1:129], 1.0), inc=(vv, 1))
            p.op("vector", lambda eng: eng.memset(
                consts[:, 129:130].bitcast(BF16), 1.0), inc=(vv, 1))
            cready = vv.count
            ones_col = consts[:, 0:1]
            ones_row = consts[0:1, 1:129]
            ones_b = consts[:, 129:130].bitcast(BF16)[:, 0:1]

            def ksp(c):
                return (k2b[:, 2 * c * 128:2 * c * 128 + 128],
                        k2b[:, 2 * c * 128 + 128:2 * c * 128 + 256])

            def vsp(c):
                return (v2b[:, 2 * c * 128:2 * c * 128 + 128],
                        v2b[:, 2 * c * 128 + 128:2 * c * 128 + 256])

            def qsp(h, qb):
                o = 2 * (h * self.S + qb * 512)
                return (q2b[:, o:o + 512], q2b[:, o + 512:o + 1024])

            def split_inplace(f32view, hi, lo, w0):
                w = f32view.shape[1]
                p.op("vector", lambda eng, w=w: eng.tensor_copy(bnc[:, 0:w], f32view),
                     waits=w0 + [(aa, aa.count)], inc=(vv, 1))
                p.op("scalar", lambda eng, w=w: eng.copy(hi, bnc[:, 0:w]),
                     waits=[(vv, vv.count)], inc=(aa, 1))
                p.op("vector", lambda eng, w=w: eng.tensor_sub(lo, bnc[:, 0:w], hi),
                     waits=[(aa, aa.count)], inc=(vv, 1))

            for c in range(NC):
                kh, kl = ksp(c)
                split_inplace(k2[:, c * 128:(c + 1) * 128], kh, kl,
                              [(s_in, lds)] if c == 0 else [])
                vh, vl = vsp(c)
                split_inplace(v2[:, c * 128:(c + 1) * 128], vh, vl, [])
            for h in range(NHL):
                for qb in range(NQB):
                    qh, ql = qsp(h, qb)
                    split_inplace(q2[:, h * self.S + qb * 512:h * self.S + (qb + 1) * 512],
                                  qh, ql, [])
            splits_done = vv.count

            ee_free = {}
            sc_free = {}
            grp_free = {}
            for gi, (qb, h) in enumerate([(qb, h) for qb in range(NQB) for h in range(NHL)]):
                gpar = gi % 2
                AVB = PS_AV0 if gpar == 0 else PS_AV1
                RB = PS_R0 if gpar == 0 else PS_R1
                njc = 4 * qb + 4
                qh, ql = qsp(h, qb)
                for j in range(njc):
                    sb = PS_S0 if j % 2 == 0 else PS_S1
                    kh, kl = ksp(j)
                    vh, vl = vsp(j)
                    w0 = [(vv, splits_done)] if gi == 0 and j < 2 else []
                    i_ = 0
                    for lhs, rhs in ((kh, qh), (kh, ql), (kl, qh)):
                        p.op("tensor", lambda eng, lhs=lhs, rhs=rhs, sb=sb, i_=i_: eng.matmul(
                            ps[sb][:, 0:512], lhs, rhs,
                            start=(i_ == 0), stop=(i_ == 2)),
                            waits=(w0 + sc_free.get(sb, [])) if i_ == 0 else (),
                            inc=(mm, 1) if i_ == 2 else None)
                        i_ += 1
                    smm = mm.count
                    par = j % 2
                    ev = ee[:, par * 512:(par + 1) * 512]
                    eh = eeb[:, par * 1024:par * 1024 + 512]
                    el = eeb[:, par * 1024 + 512:par * 1024 + 1024]
                    p.op("scalar", lambda eng, ev=ev, sb=sb: eng.activation(
                        ev, ps[sb][:, 0:512], AF.Exp),
                        waits=[(mm, smm)] + ee_free.get(par, []), inc=(aa, 1))
                    emark = [(aa, aa.count)]
                    d = j - 4 * qb
                    if d >= 0:
                        p.op("gpsimd", lambda eng, ev=ev, d=d: eng.affine_select(
                            ev, ev, [[1, 512]], ALU.is_ge, 0.0,
                            base=-128 * d, channel_multiplier=-1),
                            waits=emark, inc=(gg, 1))
                        emark = [(gg, gg.count)]
                    p.op("scalar", lambda eng, eh=eh, ev=ev: eng.copy(eh, ev),
                         waits=emark, inc=(aa, 1))
                    p.op("vector", lambda eng, el=el, ev=ev, eh=eh: eng.tensor_sub(
                        el, ev, eh), waits=[(aa, aa.count)], inc=(vv, 1))
                    esp = [(vv, vv.count)]
                    p.op("tensor", lambda eng, eh=eh, j=j, njc=njc, RB=RB: eng.matmul(
                        ps[RB][0:1, 0:512], ones_b, eh,
                        start=(j == 0), stop=False),
                        waits=esp + (grp_free.get(gpar, []) if j == 0 else []),
                        inc=(mm, 1))
                    p.op("tensor", lambda eng, el=el, j=j, njc=njc, RB=RB: eng.matmul(
                        ps[RB][0:1, 0:512], ones_b, el,
                        start=False, stop=(j == njc - 1)), inc=(mm, 1))
                    i_ = 0
                    for lhs, rhs in ((vh, eh), (vh, el), (vl, eh)):
                        p.op("tensor", lambda eng, lhs=lhs, rhs=rhs, j=j, njc=njc,
                             AVB=AVB, i_=i_: eng.matmul(
                                 ps[AVB][:, 0:512], lhs, rhs,
                                 start=(j == 0 and i_ == 0),
                                 stop=(j == njc - 1 and i_ == 2)), inc=(mm, 1))
                        i_ += 1
                    sc_free[sb] = [(aa, aa.count)]
                    ee_free[par] = [(mm, mm.count)]
                avm = mm.count
                avv = av[:, gpar * 512:(gpar + 1) * 512]
                p.op("vector", lambda eng, RB=RB: eng.reciprocal(
                    rr[:, :], ps[RB][0:1, 0:512]),
                    waits=[(mm, avm)] + grp_free.get(1 - gpar, [])[0:0], inc=(vv, 1))
                p.op("tensor", lambda eng: eng.matmul(
                    ps[PS_B][:, 0:512], ones_row, rr[:, :], start=True, stop=True),
                    waits=[(vv, vv.count)], inc=(mm, 1))
                p.op("vector", lambda eng: eng.tensor_copy(bc[:, :], ps[PS_B][:, 0:512]),
                     waits=[(mm, mm.count)], inc=(vv, 1))
                p.op("vector", lambda eng, avv=avv, AVB=AVB: eng.tensor_mul(
                    avv, ps[AVB][:, 0:512], bc[:, :]), inc=(vv, 1))
                nmark = vv.count
                p.dma_ch("gpsimd", dr["y2"][h][:, qb * 512:(qb + 1) * 512], avv,
                         s_out, waits=[(vv, nmark)])
                grp_free[gpar] = [(vv, nmark), (s_out, s_out.count)]
            p.op("gpsimd", lambda eng: eng.engine_nop(), waits=[(s_out, s_out.count)])
            p.build()
        return nc



class L3(L1):

    def build(self):
        T, TT, D, H = self.T, self.TT, self.D, self.H
        CD = self.CD
        nc = bass.Bass("TRN2", target_bir_lowering=False, debug=False,
                       detect_race_conditions=False)
        dr = {}
        dr["x_fm"] = nc.dram_tensor("x_fm", [D, T], F32, kind="ExternalInput").ap()
        dr["y_fm"] = nc.dram_tensor("y_fm", [D, T], F32, kind="ExternalInput").ap()
        dr["wgT"] = nc.dram_tensor("wgT", [D, H], F32, kind="ExternalInput").ap()
        dr["wuT"] = nc.dram_tensor("wuT", [D, H], F32, kind="ExternalInput").ap()
        dr["wdT"] = nc.dram_tensor("wdT", [H, D], F32, kind="ExternalInput").ap()
        dr["wpT"] = nc.dram_tensor("wpT", [D, D], F32, kind="ExternalInput").ap()
        dr["sel16_in"] = nc.dram_tensor("sel16_in", [16, 16 * 128], BF16, kind="ExternalInput").ap()
        dr["selc_in"] = nc.dram_tensor("selc_in", [128, 2 * CD - 1], BF16, kind="ExternalInput").ap()
        dr["x3_out"] = nc.dram_tensor("x3_out", [D, T], F32, kind="ExternalOutput").ap()
        if getattr(self, "dbg_weff3", False):
            dr["dbg_w"] = nc.dram_tensor("dbg_w", [D // 256, 128, 16 * 256], BF16,
                                         kind="ExternalOutput").ap()
        with contextlib.ExitStack() as es:
            p = Prog(nc, es)
            self._body3(nc, p, dr)
            p.build()
        return nc

    def _body3(self, nc, p, dr):
        T, TT, D, H = self.T, self.TT, self.D, self.H
        CD, CH, NT = self.CD, self.CH, self.NT
        SLAB = 256
        NSLAB = D // SLAB

        xT = p.sbuf("xT", [128, CD * T], F32)
        h1 = p.sbuf("h1", [128, CD * T], F32)
        act = p.sbuf("act", [128, 2 * T], F32)
        wg_s = p.sbuf("wg_s", [128, D], F32)
        wu_s = p.sbuf("wu_s", [128, D], F32)
        wd_s = p.sbuf("wd_s", [128, 2 * D], F32)
        wst = p.sbuf("wst", [128, 2 * D], F32)
        lnss = p.sbuf("lnss", [128, TT], F32)
        asc = p.sbuf("asc", [128, TT], F32)
        sq = p.sbuf("sq", [128, TT], F32)
        stats = p.sbuf("stats", [128, 2 * TT], F32)
        consts = p.sbuf("consts", [128, 144], F32)
        selcol = p.sbuf("selcol", [128, 2 * CD - 1], BF16)
        qscr = p.sbuf("qscr", [128, 3072], F32)
        h1b = h1.bitcast(BF16)
        actb = act.bitcast(BF16)
        wgb = wg_s.bitcast(BF16)
        wub = wu_s.bitcast(BF16)
        wdb = wd_s.bitcast(BF16)
        weff_bufs = (wgb[:, 0:CD * SLAB], wub[:, 0:CD * SLAB])
        wbb = qscr[:, 0:2048].bitcast(BF16)
        rb = qscr[:, 2048:2176].bitcast(BF16)
        g1 = qscr[:, 2176:2432]
        g2 = qscr[:, 2432:2688]
        sclb = qscr[0:16, 2688:2816].bitcast(BF16)
        inv_bc = qscr[:, 2816:3072]
        sqb = sq.bitcast(BF16)
        wab_bufs = (sqb[:, 0:SLAB], sqb[:, SLAB:2 * SLAB])
        sel16 = stats[0:16, 0:1024].bitcast(BF16)

        ps = [p.psum(f"ps{i}", [128, 512], F32) for i in range(8)]
        PS_GATE, PS_UP, PS_D0, PS_D1, PS_S0, PS_S1, PS_B0, PS_B1 = range(8)

        s_in = p.sem("s_in")
        s_st = [p.sem("s_st0"), p.sem("s_st1")]
        s_y = p.sem("s_y")
        s_x1 = p.sem("s_x1")
        mm = p.sem("mms")
        vv = p.sem("vvs")
        aa = p.sem("aas")
        s_wl = [p.sem("s_wl0"), p.sem("s_wl1")]

        self._sq_marks = {}
        self._free = {}
        self._slot_free = {}
        self._slot_seq = 0
        self._asc_free = 0
        self._act_free = {}

        p.dma_ch("sync", xT[:, :].rearrange("p (c t) -> p c t", c=CD),
                 dr["x_fm"].rearrange("(c p) t -> p c t", p=128), s_in)
        p.dma_ch("sync", selcol[:, :], dr["selc_in"], s_in)
        lds0 = s_in.count
        p.op("vector", lambda eng: eng.memset(consts[:, 0:1], 1.0), inc=(vv, 1))
        p.op("vector", lambda eng: eng.memset(consts[0:1, 1:129], 1.0), inc=(vv, 1))
        consts_ready = vv.count
        ones_col = consts[:, 0:1]
        ones_row = consts[0:1, 1:129]
        A = stats[0:1, 0:TT]
        B = stats[0:1, TT:2 * TT]
        C = stats[0:1, 0:TT]
        NSC = lnss[0:1, 0:TT]

        def rstd_from_B(extra_eps, width):
            p.op("vector", lambda eng: eng.tensor_scalar_add(
                B[:, 0:width], B[:, 0:width], extra_eps), inc=(vv, 1))
            p.op("vector", lambda eng: eng.reciprocal(
                C[:, 0:width], B[:, 0:width]), inc=(vv, 1))
            p.op("scalar", lambda eng: eng.activation(
                C[:, 0:width], C[:, 0:width], AF.Sqrt),
                waits=[(vv, vv.count)], inc=(aa, 1))
            p.op("scalar", lambda eng: eng.activation(
                NSC[:, 0:width], C[:, 0:width], AF.Square), inc=(aa, 1))
            p.op("vector", lambda eng: eng.tensor_mul(
                NSC[:, 0:width], NSC[:, 0:width], B[:, 0:width]),
                waits=[(aa, aa.count)], inc=(vv, 1))
            p.op("vector", lambda eng: eng.tensor_scalar(
                NSC[:, 0:width], NSC[:, 0:width], -0.5, 1.5, ALU.mult, ALU.add),
                inc=(vv, 1))
            p.op("vector", lambda eng: eng.tensor_mul(
                C[:, 0:width], C[:, 0:width], NSC[:, 0:width]), inc=(vv, 1))
            return vv.count

        def ln(src_view, dst_view, eps, wait0=(), n_chunks=CD, width=TT, split=None):
            w0 = list(wait0) + [(vv, consts_ready)]
            ndim = n_chunks * 128
            for c in range(n_chunks):
                wsq = [(mm, self._sq_marks[0])] if 0 in self._sq_marks else []
                p.op("scalar", lambda eng, c=c: eng.activation(
                    sq[:, 0:width], src_view(c), AF.Square),
                    waits=wsq + (w0 if c == 0 else []), inc=(aa, 1))
                p.op("tensor", lambda eng, c=c: eng.matmul(
                    ps[PS_S0][0:1, 0:width], ones_col, src_view(c),
                    start=(c == 0), stop=(c == n_chunks - 1)),
                    waits=(w0 + self._free.get(PS_S0, [])) if c == 0 else (),
                    inc=(mm, 1))
                p.op("tensor", lambda eng, c=c: eng.matmul(
                    ps[PS_S1][0:1, 0:width], ones_col, sq[:, 0:width],
                    start=(c == 0), stop=(c == n_chunks - 1)),
                    waits=[(aa, aa.count)] + (self._free.get(PS_S1, []) if c == 0 else []),
                    inc=(mm, 1))
                self._sq_marks[0] = mm.count
            stats_mm = mm.count
            w = [(mm, stats_mm)]
            p.op("vector", lambda eng: eng.tensor_scalar_mul(
                A[:, 0:width], ps[PS_S0][0:1, 0:width], 1.0 / ndim),
                waits=w, inc=(vv, 1))
            self._free[PS_S0] = [(vv, vv.count)]
            p.op("vector", lambda eng: eng.tensor_scalar_mul(
                B[:, 0:width], ps[PS_S1][0:1, 0:width], 1.0 / ndim),
                waits=w, inc=(vv, 1))
            self._free[PS_S1] = [(vv, vv.count)]
            mean_done = vv.count
            p.op("tensor", lambda eng: eng.matmul(
                ps[PS_B0][:, 0:width], ones_row, A[:, 0:width],
                start=True, stop=True),
                waits=[(vv, mean_done)] + self._free.get(PS_B0, []), inc=(mm, 1))
            mbc = mm.count
            p.op("scalar", lambda eng: eng.activation(
                NSC[:, 0:width], A[:, 0:width], AF.Square),
                waits=[(vv, mean_done)], inc=(aa, 1))
            p.op("vector", lambda eng: eng.tensor_sub(
                B[:, 0:width], B[:, 0:width], NSC[:, 0:width]),
                waits=[(aa, aa.count), (mm, mbc)], inc=(vv, 1))
            rstd_done = rstd_from_B(eps, width)
            p.op("tensor", lambda eng: eng.matmul(
                ps[PS_B1][:, 0:width], ones_row, C[:, 0:width],
                start=True, stop=True),
                waits=[(vv, rstd_done)] + self._free.get(PS_B1, []), inc=(mm, 1))
            bca = mm.count
            extra_w = []
            for c in range(n_chunks):
                p.op("vector", lambda eng, c=c: eng.tensor_sub(
                    dst_view(c), src_view(c), ps[PS_B0][:, 0:width]),
                    waits=([(mm, bca)] if c == 0 else []) + extra_w, inc=(vv, 1))
                p.op("vector", lambda eng, c=c: eng.tensor_mul(
                    dst_view(c), dst_view(c), ps[PS_B1][:, 0:width]), inc=(vv, 1))
                if split is not None:
                    extra_w, _ = split(c, vv.count)
            self._free[PS_B0] = [(vv, vv.count)]
            self._free[PS_B1] = [(vv, vv.count)]
            return vv.count, bca

        def xv(tile):
            return lambda c: xT[:, c * T + tile * TT: c * T + (tile + 1) * TT]

        def split_chunk(c, src_ap, hi_ap, lo_ap, ready_mark):
            p.op("scalar", lambda eng: eng.copy(hi_ap, src_ap),
                 waits=[(vv, ready_mark)], inc=(aa, 1))
            him = aa.count
            p.op("vector", lambda eng: eng.tensor_sub(lo_ap, src_ap, hi_ap),
                 waits=[(aa, him)], inc=(vv, 1))
            return [(aa, him), (vv, vv.count)], vv.count

        def h1h(c, tt):
            return h1b[:, c * T + tt * TT: c * T + tt * TT + TT]

        def h1l(c, tt):
            return h1b[:, CD * T + c * T + tt * TT: CD * T + c * T + tt * TT + TT]

        def ynh(c, tt):
            return h1b[:, 2 * (c * T + tt * TT): 2 * (c * T + tt * TT) + TT]

        def ynl(c, tt):
            return h1b[:, 2 * (c * T + tt * TT) + TT: 2 * (c * T + tt * TT) + 2 * TT]

        def yv(tile):
            return lambda c: h1[:, c * T + tile * TT: c * T + (tile + 1) * TT]

        def stage_load(dram_slab, pair_view, key, marks, split_waits):
            slot = self._slot_seq % 2
            self._slot_seq += 1
            sem = s_wl[slot]
            stage = wst[:, slot * D:(slot + 1) * D]
            p.dma_ch("sync", stage.rearrange("p (c n) -> p c n", c=CD), dram_slab,
                     sem, waits=self._slot_free.get(slot, []))
            ld = (sem, sem.count)
            p.op("scalar", lambda eng, stage=stage, pv=pair_view: eng.copy(
                pv[:, 0:D], stage), waits=[ld] + list(split_waits), inc=(aa, 1))
            him = aa.count
            p.op("vector", lambda eng, stage=stage, pv=pair_view: eng.tensor_sub(
                pv[:, D:2 * D], stage, pv[:, 0:D]),
                waits=[(aa, him)], inc=(vv, 1))
            self._slot_free[slot] = [(vv, vv.count)]
            marks[key] = (him, vv.count)

        gu_marks = {}
        wd_marks = {}
        gu_sp = {}
        wd_sp = {}

        def sched_gu(h, wgw, wuw):
            stage_load(dr["wgT"][:, h * 128:(h + 1) * 128].rearrange("(c p) n -> p c n", p=128),
                       wgb[:, 0:2 * D], ("g", h), gu_sp, wgw)
            stage_load(dr["wuT"][:, h * 128:(h + 1) * 128].rearrange("(c p) n -> p c n", p=128),
                       wub[:, 0:2 * D], ("u", h), gu_sp, wuw)

        def sched_wd(g, ww):
            for i in range(2):
                h = 2 * g + i
                stage_load(dr["wdT"][h * 128:(h + 1) * 128, :].rearrange("p (c n) -> p c n", c=CD),
                           wdb[:, i * 2 * D:(i + 1) * 2 * D], ("d", g, i), wd_sp, ww)

        p.dma_ch("sync", h1[:, :].rearrange("p (c t) -> p c t", c=CD),
                 dr["y_fm"].rearrange("(c p) t -> p c t", p=128), s_y)
        yld = s_y.count
        for tile in range(NT):
            for c in range(CD):
                wsq = [(mm, self._sq_marks[0])] if 0 in self._sq_marks else []
                p.op("scalar", lambda eng, c=c, tile=tile: eng.activation(
                    sq[:, 0:TT], yv(tile)(c), AF.Square),
                    waits=wsq + ([(s_y, yld), (vv, consts_ready)] if (c == 0 and tile == 0) else []),
                    inc=(aa, 1))
                p.op("tensor", lambda eng, c=c: eng.matmul(
                    ps[PS_S1][0:1, 0:TT], ones_col, sq[:, 0:TT],
                    start=(c == 0), stop=(c == CD - 1)),
                    waits=[(aa, aa.count)] + (self._free.get(PS_S1, []) if c == 0 else []),
                    inc=(mm, 1))
                self._sq_marks[0] = mm.count
            smm = mm.count
            p.op("vector", lambda eng: eng.tensor_scalar_mul(
                B, ps[PS_S1][0:1, 0:TT], 1.0 / D), waits=[(mm, smm)], inc=(vv, 1))
            self._free[PS_S1] = [(vv, vv.count)]
            rstd_done = rstd_from_B(EPS_RMS, TT)
            p.op("tensor", lambda eng: eng.matmul(
                ps[PS_B1][:, 0:TT], ones_row, C, start=True, stop=True),
                waits=[(vv, rstd_done)] + self._free.get(PS_B1, []), inc=(mm, 1))
            bca = mm.count
            extra_w = []
            for c in range(CD):
                p.op("vector", lambda eng, c=c, tile=tile: eng.tensor_mul(
                    lnss[:, 0:TT], yv(tile)(c), ps[PS_B1][:, 0:TT]),
                    waits=([(mm, bca)] if c == 0 else []) + extra_w, inc=(vv, 1))
                extra_w, _ = split_chunk(c, lnss[:, 0:TT], ynh(c, tile), ynl(c, tile),
                                         vv.count)
            self._free[PS_B1] = [(vv, vv.count)]
        yn_ready = vv.count
        p.dma_ch("sync", sel16, dr["sel16_in"], s_in, waits=[(vv, yn_ready)])
        sel16_ld = s_in.count

        pstage_bufs = (wd_s, wst)
        stage_ld = {}
        stage_read = {}
        weff_done = {}
        wb_read = {}

        def load_slab(s):
            w = []
            if s - 2 in stage_read:
                w.append((aa, stage_read[s - 2]))
            elif s < 2:
                w.append((s_in, lds0))
            sem = s_st[s % 2]
            p.dma_ch("sync",
                     pstage_bufs[s % 2][:, 0:CD * SLAB].rearrange("p (c n) -> p c n", c=CD),
                     dr["wpT"][:, s * SLAB:(s + 1) * SLAB].rearrange("(c p) n -> p c n", p=128),
                     sem, waits=w)
            stage_ld[s] = (sem, sem.count)

        load_slab(0)
        load_slab(1)
        P_B = (PS_GATE, PS_UP, PS_D0, PS_D1)
        for s in range(NSLAB):
            par = s % 2
            stage = pstage_bufs[par]
            weff = weff_bufs[par]
            w = [stage_ld[s]]
            if s - 1 in wb_read:
                w.append((vv, wb_read[s - 1]))
            p.op("scalar", lambda eng, stage=stage: eng.copy(
                wbb[:, :], stage[:, 0:CD * SLAB]), waits=w, inc=(aa, 1))
            gs_marks = []
            for c in range(CD):
                wabv = wab_bufs[c % 2]
                aw = []
                if c == 0 and s == 0:
                    aw.append((mm, self._sq_marks[0]))
                if len(gs_marks) >= 2:
                    aw.append((mm, gs_marks[-2]))
                p.op("scalar", lambda eng, c=c, stage=stage, wabv=wabv: eng.activation(
                    wabv, stage[:, c * SLAB:(c + 1) * SLAB], AF.Abs),
                    waits=aw, inc=(aa, 1))
                abm = aa.count
                p.op("tensor", lambda eng, c=c, wabv=wabv: eng.matmul(
                    ps[PS_S0][0:16, 0:SLAB], selcol[:, (CD - 1 - c):(2 * CD - 1 - c)],
                    wabv, start=(c == 0), stop=(c == CD - 1)),
                    waits=[(aa, abm)] + (self._free.get(PS_S0, []) if c == 0 else []),
                    inc=(mm, 1))
                gs_marks.append(mm.count)
            gsum_mm = mm.count
            self._sq_marks[0] = gsum_mm
            stage_read[s] = aa.count
            if s + 2 < NSLAB:
                load_slab(s + 2)
            p.op("vector", lambda eng: eng.tensor_scalar_mul(
                sclb, ps[PS_S0][0:16, 0:SLAB], 1.0 / 128.0),
                waits=[(mm, gsum_mm)], inc=(vv, 1))
            self._free[PS_S0] = [(vv, vv.count)]
            p.op("vector", lambda eng: eng.tensor_scalar_max(
                sclb, sclb, 1e-8), inc=(vv, 1))
            scl_done = vv.count
            for c in range(CD):
                p.op("tensor", lambda eng, c=c: eng.matmul(
                    ps[PS_B1][:, 0:SLAB], sel16[:, c * 128:(c + 1) * 128],
                    sclb, start=True, stop=True),
                    waits=([(vv, scl_done), (s_in, sel16_ld)] if c == 0 else [])
                    + self._free.get(PS_B1, []), inc=(mm, 1))
                bc = mm.count
                p.op("vector", lambda eng: eng.reciprocal(
                    inv_bc, ps[PS_B1][:, 0:SLAB]), waits=[(mm, bc)], inc=(vv, 1))
                p.op("vector", lambda eng, c=c: eng.tensor_mul(
                    rb, wbb[:, c * SLAB:(c + 1) * SLAB], inv_bc), inc=(vv, 1))
                p.op("vector", lambda eng: eng.tensor_single_scalar(
                    g1, rb, 0.5, ALU.is_gt), inc=(vv, 1))
                p.op("vector", lambda eng: eng.tensor_single_scalar(
                    g2, rb, -0.5, ALU.is_lt), inc=(vv, 1))
                p.op("vector", lambda eng: eng.tensor_sub(g1, g1, g2), inc=(vv, 1))
                p.op("vector", lambda eng: eng.tensor_mul(
                    g2, g1, ps[PS_B1][:, 0:SLAB]), inc=(vv, 1))
                self._free[PS_B1] = [(vv, vv.count)]
                p.op("vector", lambda eng, c=c: eng.tensor_sub(
                    rb, g2, wbb[:, c * SLAB:(c + 1) * SLAB]), inc=(vv, 1))
                wwf = [(mm, weff_done[s - 2])] if (c == 0 and s - 2 in weff_done) else []
                p.op("vector", lambda eng, c=c, weff=weff: eng.tensor_add(
                    weff[:, c * SLAB:(c + 1) * SLAB],
                    wbb[:, c * SLAB:(c + 1) * SLAB], rb),
                    waits=wwf, inc=(vv, 1))
            wb_read[s] = vv.count
            weff_ready = vv.count
            for oc in range(2):
                og = s * 2 + oc
                for tile in range(NT):
                    bank = P_B[oc * 2 + tile]
                    i_ = 0
                    for k in range(CD):
                        wt = weff[:, k * SLAB + oc * 128: k * SLAB + oc * 128 + 128]
                        for rhs in (ynh(k, tile), ynl(k, tile)):
                            p.op("tensor",
                                 lambda eng, wt=wt, rhs=rhs, bank=bank, i_=i_:
                                 eng.matmul(ps[bank][:, 0:TT], wt, rhs,
                                            start=(i_ == 0), stop=(i_ == 2 * CD - 1)),
                                 waits=([(vv, weff_ready), (vv, yn_ready)]
                                        + self._free.get(bank, [])) if i_ == 0 else (),
                                 inc=(mm, 1) if i_ == 2 * CD - 1 else None)
                            i_ += 1
                    pmm = mm.count
                    p.op("vector", lambda eng, og=og, tile=tile, bank=bank:
                         eng.tensor_add(xv(tile)(og), xv(tile)(og), ps[bank][:, 0:TT]),
                         waits=[(mm, pmm), (s_in, lds0)], inc=(vv, 1))
                    self._free[bank] = [(vv, vv.count)]
            weff_done[s] = mm.count
        proj_mm = mm.count
        proj_done = vv.count
        last_sr = [(aa, stage_read[NSLAB - 1]), (aa, stage_read[NSLAB - 2])]
        self._slot_free = {0: list(last_sr), 1: list(last_sr)}

        sched_gu(0, [(mm, proj_mm)], [(mm, proj_mm)])
        sched_wd(0, [(mm, proj_mm), (aa, stage_read[NSLAB - 2])])

        ha1 = []
        for tile in range(NT):
            mark, _ = ln(xv(tile), lambda c: lnss[:, 0:TT], EPS_LN,
                         wait0=([(vv, proj_done)] if tile == 0 else []),
                         split=(lambda tile: lambda c, rdy: split_chunk(
                             c, lnss[:, 0:TT], h1h(c, tile), h1l(c, tile), rdy))(tile))
            ha1.append(vv.count)
        ln1_done = vv.count

        G_B = (PS_GATE, PS_UP)
        U_B = (PS_D0, PS_D1)
        D_B = (PS_S0, PS_S1, PS_B0, PS_B1)

        asc_bufs = (asc[:, 0:TT], lnss[:, 0:TT])
        asc_free = {0: [], 1: []}
        act_free = []
        act_ready = {}

        for h in range(CH):
            g, i2 = h // 2, h % 2
            wgv = wgb[:, 0:2 * D]
            wuv = wub[:, 0:2 * D]
            gum = {}
            for wv, banks, nm in ((wgv, G_B, "g"), (wuv, U_B, "u")):
                for tt in range(NT):
                    bank = banks[tt]
                    i_ = 0
                    for k in range(CD):
                        for lhs, rhs in ((wv[:, k * 128:(k + 1) * 128], h1h(k, tt)),
                                         (wv[:, k * 128:(k + 1) * 128], h1l(k, tt)),
                                         (wv[:, D + k * 128:D + (k + 1) * 128], h1h(k, tt))):
                            p.op("tensor", lambda eng, lhs=lhs, rhs=rhs, bank=bank, i_=i_:
                                 eng.matmul(ps[bank][:, 0:TT], lhs, rhs,
                                            start=(i_ == 0), stop=(i_ == 3 * CD - 1)),
                                 waits=([(vv, gu_sp[(nm, h)][1]), (vv, ha1[tt])]
                                        + self._free.get(bank, [])) if i_ == 0 else (),
                                 inc=(mm, 1) if i_ == 3 * CD - 1 else None)
                            i_ += 1
                    gum[(nm, tt)] = mm.count
            gu_marks[h] = mm.count
            if h + 1 < CH:
                sched_gu(h + 1, [(mm, gum[("g", NT - 1)])], [(mm, gu_marks[h])])
            for tt in range(NT):
                ascv = asc_bufs[tt]
                p.op("scalar", lambda eng, ascv=ascv, tt=tt: eng.activation(
                    ascv, ps[G_B[tt]][:, 0:TT], AF.Silu),
                    waits=[(mm, gum[("g", tt)])] + asc_free[tt], inc=(aa, 1))
                self._free[G_B[tt]] = [(aa, aa.count)]
                p.op("vector", lambda eng, ascv=ascv, tt=tt: eng.tensor_mul(
                    ascv, ascv, ps[U_B[tt]][:, 0:TT]),
                    waits=[(mm, gum[("u", tt)]), (aa, aa.count)], inc=(vv, 1))
                self._free[U_B[tt]] = [(vv, vv.count)]
                ach = actb[:, i2 * T + tt * TT: i2 * T + tt * TT + TT]
                acl = actb[:, 2 * T + i2 * T + tt * TT: 2 * T + i2 * T + tt * TT + TT]
                p.op("scalar", lambda eng, ach=ach, ascv=ascv: eng.copy(ach, ascv),
                     waits=[(vv, vv.count)] + act_free, inc=(aa, 1))
                p.op("vector", lambda eng, ach=ach, acl=acl, ascv=ascv: eng.tensor_sub(
                    acl, ascv, ach), waits=[(aa, aa.count)], inc=(vv, 1))
                asc_free[tt] = [(vv, vv.count)]
                act_ready[(i2, tt)] = vv.count
            if i2 == 1:
                for o in range(CD):
                    for tt in range(NT):
                        bank = D_B[(o % 2) * 2 + tt]
                        wdn = list(self._free.get(bank, []))
                        if o == 0 and tt == 0:
                            wdn += [(vv, act_ready[(0, 0)]), (vv, act_ready[(0, 1)]),
                                    (vv, act_ready[(1, 0)]), (vv, act_ready[(1, 1)]),
                                    (vv, wd_sp[("d", g, 0)][1]),
                                    (vv, wd_sp[("d", g, 1)][1])]
                        i_ = 0
                        for i3 in range(2):
                            wdv = wdb[:, i3 * 2 * D:(i3 + 1) * 2 * D]
                            a_h = actb[:, i3 * T + tt * TT: i3 * T + tt * TT + TT]
                            a_l = actb[:, 2 * T + i3 * T + tt * TT: 2 * T + i3 * T + tt * TT + TT]
                            for lhs, rhs in ((wdv[:, o * 128:(o + 1) * 128], a_h),
                                             (wdv[:, o * 128:(o + 1) * 128], a_l),
                                             (wdv[:, D + o * 128:D + (o + 1) * 128], a_h)):
                                p.op("tensor", lambda eng, lhs=lhs, rhs=rhs, i_=i_, bank=bank:
                                     eng.matmul(ps[bank][:, 0:TT], lhs, rhs,
                                                start=(i_ == 0), stop=(i_ == 5)),
                                     waits=wdn if i_ == 0 else (),
                                     inc=(mm, 1) if i_ == 5 else None)
                                i_ += 1
                        dn_mm = mm.count
                        xvv = xT[:, o * T + tt * TT: o * T + tt * TT + TT]
                        stw = [(mm, dn_mm)]
                        if g == 0:
                            stw.append((vv, ln1_done))
                        p.op("vector", lambda eng, xvv=xvv, bank=bank:
                             eng.scalar_tensor_tensor(
                                 xvv, ps[bank][:, 0:TT], 0.5, xvv,
                                 ALU.mult, ALU.add),
                             waits=stw, inc=(vv, 1))
                        self._free[bank] = [(vv, vv.count)]
                wd_marks[g] = mm.count
                act_free = [(mm, mm.count)]
                if g + 1 < CH // 2:
                    sched_wd(g + 1, [(mm, wd_marks[g])])
        x3_done = vv.count
        p.dma_ch("gpsimd", dr["x3_out"].rearrange("(c p) t -> p c t", p=128),
                 xT[:, :].rearrange("p (c t) -> p c t", c=CD),
                 s_x1, waits=[(vv, x3_done)])
        p.op("gpsimd", lambda eng: eng.engine_nop(), waits=[(s_x1, s_x1.count)])



NCORE = 8
SEQ = 8192
DM = 2048
TCORE = SEQ // NCORE


def _consts():
    eye = np.eye(128, dtype=np.float32)
    import ml_dtypes
    sel16 = np.zeros((16, 16 * 128), dtype=ml_dtypes.bfloat16)
    for c in range(16):
        sel16[c, c * 128:(c + 1) * 128] = 1.0
    CD = DM // 128
    selc = np.zeros((128, 2 * CD - 1), dtype=ml_dtypes.bfloat16)
    for pp in range(128):
        selc[pp, CD - 1] = 1.0
    R = np.zeros((128, 128), dtype=np.float32)
    for i in range(64):
        R[i, 64 + i] = 1.0
        R[64 + i, i] = -1.0
    rotc = np.ascontiguousarray(R.T)
    return eye, sel16, selc, rotc


_ROPE_CACHE = {}


def _rope_tables_full():
    if "t" not in _ROPE_CACHE:
        import jax
        import jax.numpy as jnp
        cpu = jax.devices("cpu")[0]
        with jax.default_device(cpu):
            inv = 1.0 / 10000.0 ** (jnp.arange(0, 128, 2, dtype=jnp.float32) / 128)
            t = jnp.arange(SEQ, dtype=jnp.float32)
            fr = jnp.outer(t, inv)
            c = np.asarray(jnp.cos(fr)).T.astype(np.float32)
            s = np.asarray(jnp.sin(fr)).T.astype(np.float32)
        _ROPE_CACHE["t"] = (c, s)
    return _ROPE_CACHE["t"]


def _rope_tables(pos):
    c, s = _rope_tables_full()
    cc = np.ascontiguousarray(c[:, pos])
    ss = np.ascontiguousarray(s[:, pos])
    return np.concatenate([cc, cc]), np.concatenate([ss, ss])


def kernel(L0_attn_x, L0_attn_w_qkv, L0_attn_w_proj, L0_attn_q_gain,
           L0_ffn_w_gate, L0_ffn_w_up, L0_ffn_w_down):
    from concourse.bass_utils import run_bass_kernel_spmd

    x = np.ascontiguousarray(np.asarray(L0_attn_x, dtype=np.float32)[0].T)
    wgT = np.ascontiguousarray(np.asarray(L0_ffn_w_gate, np.float32).T)
    wuT = np.ascontiguousarray(np.asarray(L0_ffn_w_up, np.float32).T)
    wdT = np.ascontiguousarray(np.asarray(L0_ffn_w_down, np.float32).T)
    wqkvT = np.ascontiguousarray(np.asarray(L0_attn_w_qkv, np.float32).T)
    wpT = np.ascontiguousarray(np.asarray(L0_attn_w_proj, np.float32).T)
    gain = np.asarray(L0_attn_q_gain, np.float32)[None, :]
    eye, sel16, selc, rotc = _consts()

    im1 = []
    for c in range(NCORE):
        pos = np.arange(c * TCORE, (c + 1) * TCORE)
        cos, sin = _rope_tables(pos)
        im1.append({
            "x_fm": np.ascontiguousarray(x[:, c * TCORE:(c + 1) * TCORE]),
            "wgT": wgT, "wuT": wuT, "wdT": wdT, "wqkvT": wqkvT,
            "cos_t": cos, "sin_t": sin, "gain_in": gain,
            "eye_in": eye, "sel16_in": sel16, "selc_in": selc, "rotc_in": rotc,
        })
    nc1 = L1().build()
    r1 = run_bass_kernel_spmd(nc1, im1, core_ids=list(range(NCORE))).results

    q_all = np.concatenate([r["q_out"] for r in r1], axis=2)
    k_all = np.concatenate([r["k_out"] for r in r1], axis=2)
    v_all = np.concatenate([r["v_out"] for r in r1], axis=1)

    im2 = []
    for c in range(NCORE):
        kv = c // 2
        im2.append({
            "q2": np.ascontiguousarray(q_all[2 * c:2 * c + 2]),
            "k2": np.ascontiguousarray(k_all[kv]),
            "v2": np.ascontiguousarray(v_all[kv]),
        })
    nc2 = L2().build()
    r2 = run_bass_kernel_spmd(nc2, im2, core_ids=list(range(NCORE))).results
    y_all = np.concatenate([r["y2"] for r in r2], axis=0)
    y_fm = y_all.reshape(DM, SEQ)

    im3 = []
    for c in range(NCORE):
        im3.append({
            "x_fm": r1[c]["x1_out"],
            "y_fm": np.ascontiguousarray(y_fm[:, c * TCORE:(c + 1) * TCORE]),
            "wgT": wgT, "wuT": wuT, "wdT": wdT, "wpT": wpT,
            "sel16_in": sel16, "selc_in": selc,
        })
    nc3 = L3().build()
    r3 = run_bass_kernel_spmd(nc3, im3, core_ids=list(range(NCORE))).results
    out = np.concatenate([r["x3_out"].T for r in r3], axis=0)
    return np.ascontiguousarray(out[None, :, :])


# version 07
# speedup vs baseline: 1.6385x; 1.3540x over previous
import contextlib
import numpy as np
import concourse.bass as bass
import concourse.mybir as mybir

F32 = mybir.dt.float32
BF16 = mybir.dt.bfloat16
I32 = mybir.dt.int32
AF = mybir.ActivationFunctionType
ALU = mybir.AluOpType
AX = mybir.AxisListType

ENGINES = ("sync", "tensor", "vector", "scalar", "gpsimd")


class Sem:
    __slots__ = ("name", "count", "h")

    def __init__(self, name, h):
        self.name = name
        self.count = 0
        self.h = h


class Prog:

    def __init__(self, nc, es):
        self.nc = nc
        self.es = es
        self.ops = []

    def sem(self, name):
        return Sem(name, self.es.enter_context(self.nc.semaphore(name)))

    def sbuf(self, name, shape, dtype):
        return self.es.enter_context(self.nc.sbuf_tensor(name, shape, dtype))

    def psum(self, name, shape, dtype=F32):
        return self.es.enter_context(self.nc.psum_tensor(name, shape, dtype))

    def op(self, engine, fn, waits=(), inc=None, incs=None):
        assert engine in ENGINES, engine
        for s, v in waits:
            assert v <= s.count, (s.name, v, s.count, "wait on future sem value")
        ilist = list(incs) if incs else ([] if inc is None else [inc])
        for s, n in ilist:
            s.count += n
        self.ops.append((engine, tuple(waits), fn, ilist))

    def dma(self, engine, out, in_, waits=(), inc=None):
        def fn(eng, out=out, in_=in_):
            return eng.dma_start(out=out, in_=in_)
        self.op(engine, fn, waits=waits, inc=inc)

    def dma_ch(self, engine, out, in_, sem, waits=()):
        def fn(eng, out=out, in_=in_):
            return eng.dma_start(out=out, in_=in_)
        self.op(engine, fn, waits=waits, inc=(sem, 16))

    def build(self):
        with self.nc.Block() as block:
            for name in ENGINES:
                ops = [o for o in self.ops if o[0] == name]
                if not ops:
                    continue

                def make(ops):
                    def emit(eng):
                        for _, waits, fn, ilist in ops:
                            for s, v in waits:
                                eng.wait_ge(s.h, v)
                            ins = fn(eng)
                            for s, n in ilist:
                                ins = ins.then_inc(s.h, n)
                    return emit

                getattr(block, name)(make(ops))


def ceil_div(a, b):
    return -(-a // b)



RSQRT_HD = 0.08838834764831845
EPS_LN = 1e-5
EPS_RMS = 1.1920929e-07


class L1:
    def __init__(self, T=1024, TT=512, D=2048, H=8192, NH=16, NKV=4):
        self.__dict__.update(T=T, TT=TT, D=D, H=H, NH=NH, NKV=NKV)
        self.CD, self.CH, self.NT = D // 128, H // 128, T // TT
        self.O3 = (NH + 2 * NKV) * 128

    def build(self):
        T, TT, D, H, NH, NKV = self.T, self.TT, self.D, self.H, self.NH, self.NKV
        CD, O3 = self.CD, self.O3
        nc = bass.Bass("TRN2", target_bir_lowering=False, debug=False,
                       detect_race_conditions=False)

        dr = {}
        dr["x_fm"] = nc.dram_tensor("x_fm", [D, T], F32, kind="ExternalInput").ap()
        dr["wgT"] = nc.dram_tensor("wgT", [D, H], F32, kind="ExternalInput").ap()
        dr["wuT"] = nc.dram_tensor("wuT", [D, H], F32, kind="ExternalInput").ap()
        dr["wdT"] = nc.dram_tensor("wdT", [H, D], F32, kind="ExternalInput").ap()
        dr["wqkvT"] = nc.dram_tensor("wqkvT", [D, O3], F32, kind="ExternalInput").ap()
        dr["cos_t"] = nc.dram_tensor("cos_t", [128, T], F32, kind="ExternalInput").ap()
        dr["sin_t"] = nc.dram_tensor("sin_t", [128, T], F32, kind="ExternalInput").ap()
        dr["rotc_in"] = nc.dram_tensor("rotc_in", [128, 128], F32, kind="ExternalInput").ap()
        dr["gain_in"] = nc.dram_tensor("gain_in", [1, NH], F32, kind="ExternalInput").ap()
        dr["eye_in"] = nc.dram_tensor("eye_in", [128, 128], F32, kind="ExternalInput").ap()
        dr["sel16_in"] = nc.dram_tensor("sel16_in", [16, 16 * 128], BF16, kind="ExternalInput").ap()
        dr["selc_in"] = nc.dram_tensor("selc_in", [128, 2 * CD - 1], BF16, kind="ExternalInput").ap()

        dr["x1_out"] = nc.dram_tensor("x1_out", [D, T], F32, kind="ExternalOutput").ap()
        dr["q_out"] = nc.dram_tensor("q_out", [NH, 128, T], F32, kind="ExternalOutput").ap()
        dr["k_out"] = nc.dram_tensor("k_out", [NKV, 128, T], F32, kind="ExternalOutput").ap()
        dr["v_out"] = nc.dram_tensor("v_out", [NKV, T, 128], F32, kind="ExternalOutput").ap()
        if getattr(self, "debug_weff", False):
            dr["dbg_weff"] = nc.dram_tensor("dbg_weff", [self.O3 // 256, 128, 16 * 256], F32,
                                            kind="ExternalOutput").ap()
            dr["dbg_wb"] = nc.dram_tensor("dbg_wb", [self.O3 // 256, 128, 16 * 256], F32,
                                          kind="ExternalOutput").ap()

        with contextlib.ExitStack() as es:
            p = Prog(nc, es)
            self._body(nc, p, dr)
            p.build()
        return nc

    def _body(self, nc, p, dr):
        T, TT, D, H, NH, NKV = self.T, self.TT, self.D, self.H, self.NH, self.NKV
        CD, CH, NT, O3 = self.CD, self.CH, self.NT, self.O3
        SLAB = 256
        NSLAB = O3 // SLAB

        xT = p.sbuf("xT", [128, CD * T], F32)
        h1 = p.sbuf("h1", [128, CD * T], F32)
        act = p.sbuf("act", [128, 2 * T], F32)
        wg_s = p.sbuf("wg_s", [128, D], F32)
        wu_s = p.sbuf("wu_s", [128, D], F32)
        wd_s = p.sbuf("wd_s", [128, 2 * D], F32)
        lnss = p.sbuf("lnss", [128, TT], F32)
        asc = p.sbuf("asc", [128, TT], F32)
        wst = p.sbuf("wst", [128, 2 * D], F32)
        h1b = h1.bitcast(BF16)
        actb = act.bitcast(BF16)
        wgb = wg_s.bitcast(BF16)
        wub = wu_s.bitcast(BF16)
        wdb = wd_s.bitcast(BF16)
        xTb = xT.bitcast(BF16)

        sq = p.sbuf("sq", [128, TT], F32)
        cos2 = wst[:, 0:T]
        sin2 = wst[:, T:2 * T]
        rotc = wst[:, 2 * T:2 * T + 128]
        eye_s = wst[:, 2 * T + 128:2 * T + 256]
        stats = p.sbuf("stats", [128, 2 * TT], F32)
        consts = p.sbuf("consts", [128, 144], F32)
        gain_s = p.sbuf("gain_s", [1, NH], F32)
        selcol = p.sbuf("selcol", [128, 2 * CD - 1], BF16)

        wbb = h1[:, 0:CD * SLAB // 2].bitcast(BF16)
        wab = h1[:, 2048:4096].bitcast(BF16)
        rb = h1[:, 4096:4224].bitcast(BF16)
        g1 = h1[:, 4224:4480]
        g2 = h1[:, 4480:4736]
        qraw = h1[:, 4736:5248]
        ro = h1[:, 5248:5760]
        sclb = h1[0:16, 6784:6912].bitcast(BF16)
        inv_bc = h1[:, 6912:7168]
        sel16 = h1[0:16, 5760:6784].bitcast(BF16)

        ps = [p.psum(f"ps{i}", [128, 512], F32) for i in range(8)]
        PS_GATE, PS_UP, PS_D0, PS_D1, PS_S0, PS_S1, PS_B0, PS_B1 = range(8)

        s_in = p.sem("s_in")
        s_gu = [p.sem("s_gu0"), p.sem("s_gu1")]
        s_wd = [p.sem("s_wd0"), p.sem("s_wd1")]
        s_st = [p.sem("s_st0"), p.sem("s_st1")]
        s_x1 = p.sem("s_x1")
        s_out = p.sem("s_out")
        s_dbg = p.sem("s_dbg")
        mm = p.sem("mms")
        vv = p.sem("vvs")
        aa = p.sem("aas")

        self._sqpar = 0
        self._sq_marks = {}
        self._free = {}
        self._weff_dump = {}

        p.dma_ch("sync", xT[:, :].rearrange("p (c t) -> p c t", c=CD),
                 dr["x_fm"].rearrange("(c p) t -> p c t", p=128), s_in)
        p.dma_ch("sync", gain_s[:, :], dr["gain_in"], s_in)
        p.dma_ch("sync", selcol[:, :], dr["selc_in"], s_in)
        lds0 = s_in.count
        p.op("vector", lambda eng: eng.memset(consts[:, 0:1], 1.0), inc=(vv, 1))
        p.op("vector", lambda eng: eng.memset(consts[0:1, 1:129], 1.0), inc=(vv, 1))
        consts_ready = vv.count
        ones_col = consts[:, 0:1]
        ones_row = consts[0:1, 1:129]

        A = stats[0:1, 0:TT]
        B = stats[0:1, TT:2 * TT]
        C = stats[0:1, 0:TT]
        NSC = lnss[0:1, 0:TT]

        def rstd_from_B(extra_eps, width):
            p.op("vector", lambda eng: eng.tensor_scalar_add(
                B[:, 0:width], B[:, 0:width], extra_eps), inc=(vv, 1))
            p.op("vector", lambda eng: eng.reciprocal(
                C[:, 0:width], B[:, 0:width]), inc=(vv, 1))
            p.op("scalar", lambda eng: eng.activation(
                C[:, 0:width], C[:, 0:width], AF.Sqrt),
                waits=[(vv, vv.count)], inc=(aa, 1))
            p.op("scalar", lambda eng: eng.activation(
                NSC[:, 0:width], C[:, 0:width], AF.Square), inc=(aa, 1))
            p.op("vector", lambda eng: eng.tensor_mul(
                NSC[:, 0:width], NSC[:, 0:width], B[:, 0:width]),
                waits=[(aa, aa.count)], inc=(vv, 1))
            p.op("vector", lambda eng: eng.tensor_scalar(
                NSC[:, 0:width], NSC[:, 0:width], -0.5, 1.5, ALU.mult, ALU.add),
                inc=(vv, 1))
            p.op("vector", lambda eng: eng.tensor_mul(
                C[:, 0:width], C[:, 0:width], NSC[:, 0:width]), inc=(vv, 1))
            return vv.count

        def ln(src_view, dst_view, eps, wait0=(), n_chunks=CD, width=TT, split=None):
            ndim = n_chunks * 128
            w0 = list(wait0) + [(vv, consts_ready)]
            for c in range(n_chunks):
                wsq = [(mm, self._sq_marks[0])] if 0 in self._sq_marks else []
                p.op("scalar", lambda eng, c=c: eng.activation(
                    sq[:, 0:width], src_view(c), AF.Square),
                    waits=wsq + (w0 if c == 0 else []), inc=(aa, 1))
                p.op("tensor", lambda eng, c=c: eng.matmul(
                    ps[PS_S0][0:1, 0:width], ones_col, src_view(c),
                    start=(c == 0), stop=(c == n_chunks - 1)),
                    waits=(w0 + self._free.get(PS_S0, [])) if c == 0 else (),
                    inc=(mm, 1))
                p.op("tensor", lambda eng, c=c: eng.matmul(
                    ps[PS_S1][0:1, 0:width], ones_col, sq[:, 0:width],
                    start=(c == 0), stop=(c == n_chunks - 1)),
                    waits=[(aa, aa.count)] + (self._free.get(PS_S1, []) if c == 0 else []),
                    inc=(mm, 1))
                self._sq_marks[0] = mm.count
            stats_mm = mm.count
            w = [(mm, stats_mm)]
            p.op("vector", lambda eng: eng.tensor_scalar_mul(
                A[:, 0:width], ps[PS_S0][0:1, 0:width], 1.0 / ndim),
                waits=w, inc=(vv, 1))
            self._free[PS_S0] = [(vv, vv.count)]
            p.op("vector", lambda eng: eng.tensor_scalar_mul(
                B[:, 0:width], ps[PS_S1][0:1, 0:width], 1.0 / ndim),
                waits=w, inc=(vv, 1))
            self._free[PS_S1] = [(vv, vv.count)]
            mean_done = vv.count
            p.op("tensor", lambda eng: eng.matmul(
                ps[PS_B0][:, 0:width], ones_row, A[:, 0:width],
                start=True, stop=True),
                waits=[(vv, mean_done)] + self._free.get(PS_B0, []), inc=(mm, 1))
            mbc = mm.count
            p.op("scalar", lambda eng: eng.activation(
                NSC[:, 0:width], A[:, 0:width], AF.Square),
                waits=[(vv, mean_done)], inc=(aa, 1))
            p.op("vector", lambda eng: eng.tensor_sub(
                B[:, 0:width], B[:, 0:width], NSC[:, 0:width]),
                waits=[(aa, aa.count), (mm, mbc)], inc=(vv, 1))
            rstd_done = rstd_from_B(eps, width)
            p.op("tensor", lambda eng: eng.matmul(
                ps[PS_B1][:, 0:width], ones_row, C[:, 0:width],
                start=True, stop=True),
                waits=[(vv, rstd_done)] + self._free.get(PS_B1, []), inc=(mm, 1))
            bca = mm.count
            extra_w = []
            for c in range(n_chunks):
                p.op("vector", lambda eng, c=c: eng.tensor_sub(
                    dst_view(c), src_view(c), ps[PS_B0][:, 0:width]),
                    waits=([(mm, bca)] if c == 0 else []) + extra_w, inc=(vv, 1))
                p.op("vector", lambda eng, c=c: eng.tensor_mul(
                    dst_view(c), dst_view(c), ps[PS_B1][:, 0:width]), inc=(vv, 1))
                if split is not None:
                    extra_w, _ = split(c, vv.count)
            self._free[PS_B0] = [(vv, vv.count)]
            self._free[PS_B1] = [(vv, vv.count)]
            return vv.count, bca

        def xv(tile):
            return lambda c: xT[:, c * T + tile * TT: c * T + (tile + 1) * TT]

        gu_marks = {}
        wd_marks = {}
        x1dma = []
        s_wl = [p.sem("s_wl0"), p.sem("s_wl1")]
        self._slot_free = {}
        self._slot_seq = 0

        def stage_load(dram_slab, pair_view, gseq_key, marks, split_waits):
            slot = self._slot_seq % 2
            self._slot_seq += 1
            sem = s_wl[slot]
            stage = wst[:, slot * D:(slot + 1) * D]
            p.dma_ch("sync", stage.rearrange("p (c n) -> p c n", c=CD), dram_slab,
                     sem, waits=self._slot_free.get(slot, []))
            ld = (sem, sem.count)
            p.op("scalar", lambda eng, stage=stage, pv=pair_view: eng.copy(
                pv[:, 0:D], stage), waits=[ld] + list(split_waits), inc=(aa, 1))
            him = aa.count
            p.op("vector", lambda eng, stage=stage, pv=pair_view: eng.tensor_sub(
                pv[:, D:2 * D], stage, pv[:, 0:D]),
                waits=[(aa, him)], inc=(vv, 1))
            lom = vv.count
            self._slot_free[slot] = [(vv, lom)]
            marks[gseq_key] = (him, lom)

        gu_sp = {}
        wd_sp = {}

        def sched_gu(h, wgw, wuw):
            stage_load(dr["wgT"][:, h * 128:(h + 1) * 128].rearrange("(c p) n -> p c n", p=128),
                       wgb[:, 0:2 * D], ("g", h), gu_sp, wgw)
            stage_load(dr["wuT"][:, h * 128:(h + 1) * 128].rearrange("(c p) n -> p c n", p=128),
                       wub[:, 0:2 * D], ("u", h), gu_sp, wuw)

        def sched_wd(g, ww):
            for i in range(2):
                h = 2 * g + i
                stage_load(dr["wdT"][h * 128:(h + 1) * 128, :].rearrange("p (c n) -> p c n", c=CD),
                           wdb[:, i * 2 * D:(i + 1) * 2 * D], ("d", g, i), wd_sp, ww)

        def split_chunk(c, src_ap, hi_ap, lo_ap, ready_mark):
            p.op("scalar", lambda eng: eng.copy(hi_ap, src_ap),
                 waits=[(vv, ready_mark)], inc=(aa, 1))
            him = aa.count
            p.op("vector", lambda eng: eng.tensor_sub(lo_ap, src_ap, hi_ap),
                 waits=[(aa, him)], inc=(vv, 1))
            return [(aa, him), (vv, vv.count)], vv.count

        def h1h(c, tt):
            return h1b[:, c * T + tt * TT: c * T + tt * TT + TT]

        def h1l(c, tt):
            return h1b[:, CD * T + c * T + tt * TT: CD * T + c * T + tt * TT + TT]

        sched_gu(0, [], [])
        sched_wd(0, [])

        ha1 = []
        for tile in range(NT):
            mark, _ = ln(xv(tile), lambda c: lnss[:, 0:TT], EPS_LN,
                         wait0=([(s_in, lds0)] if tile == 0 else []),
                         split=(lambda tile: lambda c, rdy: split_chunk(
                             c, lnss[:, 0:TT], h1h(c, tile), h1l(c, tile), rdy))(tile))
            ha1.append(vv.count)
        ln1_done = vv.count

        G_B = (PS_GATE, PS_UP)
        U_B = (PS_D0, PS_D1)
        D_B = (PS_S0, PS_S1, PS_B0, PS_B1)

        asc_bufs = (asc[:, 0:TT], lnss[:, 0:TT])
        asc_free = {0: [], 1: []}
        act_free = []
        act_ready = {}

        for h in range(CH):
            g, i2 = h // 2, h % 2
            wgv = wgb[:, 0:2 * D]
            wuv = wub[:, 0:2 * D]
            gum = {}
            for wv, banks, nm in ((wgv, G_B, "g"), (wuv, U_B, "u")):
                for tt in range(NT):
                    bank = banks[tt]
                    i_ = 0
                    for k in range(CD):
                        for lhs, rhs in ((wv[:, k * 128:(k + 1) * 128], h1h(k, tt)),
                                         (wv[:, k * 128:(k + 1) * 128], h1l(k, tt)),
                                         (wv[:, D + k * 128:D + (k + 1) * 128], h1h(k, tt))):
                            p.op("tensor", lambda eng, lhs=lhs, rhs=rhs, bank=bank, i_=i_:
                                 eng.matmul(ps[bank][:, 0:TT], lhs, rhs,
                                            start=(i_ == 0), stop=(i_ == 3 * CD - 1)),
                                 waits=([(vv, gu_sp[(nm, h)][1]), (vv, ha1[tt])]
                                        + self._free.get(bank, [])) if i_ == 0 else (),
                                 inc=(mm, 1) if i_ == 3 * CD - 1 else None)
                            i_ += 1
                    gum[(nm, tt)] = mm.count
            gu_marks[h] = mm.count
            if h + 1 < CH:
                sched_gu(h + 1, [(mm, gum[("g", NT - 1)])], [(mm, gu_marks[h])])
            for tt in range(NT):
                ascv = asc_bufs[tt]
                p.op("scalar", lambda eng, ascv=ascv, tt=tt: eng.activation(
                    ascv, ps[G_B[tt]][:, 0:TT], AF.Silu),
                    waits=[(mm, gum[("g", tt)])] + asc_free[tt], inc=(aa, 1))
                self._free[G_B[tt]] = [(aa, aa.count)]
                p.op("vector", lambda eng, ascv=ascv, tt=tt: eng.tensor_mul(
                    ascv, ascv, ps[U_B[tt]][:, 0:TT]),
                    waits=[(mm, gum[("u", tt)]), (aa, aa.count)], inc=(vv, 1))
                self._free[U_B[tt]] = [(vv, vv.count)]
                ach = actb[:, i2 * T + tt * TT: i2 * T + tt * TT + TT]
                acl = actb[:, 2 * T + i2 * T + tt * TT: 2 * T + i2 * T + tt * TT + TT]
                p.op("scalar", lambda eng, ach=ach, ascv=ascv: eng.copy(ach, ascv),
                     waits=[(vv, vv.count)] + act_free, inc=(aa, 1))
                p.op("vector", lambda eng, ach=ach, acl=acl, ascv=ascv: eng.tensor_sub(
                    acl, ascv, ach), waits=[(aa, aa.count)], inc=(vv, 1))
                asc_free[tt] = [(vv, vv.count)]
                act_ready[(i2, tt)] = vv.count
            if i2 == 1:
                for o in range(CD):
                    for tt in range(NT):
                        bank = D_B[(o % 2) * 2 + tt]
                        wdn = list(self._free.get(bank, []))
                        if o == 0 and tt == 0:
                            wdn += [(vv, act_ready[(0, 0)]), (vv, act_ready[(0, 1)]),
                                    (vv, act_ready[(1, 0)]), (vv, act_ready[(1, 1)]),
                                    (vv, wd_sp[("d", g, 0)][1]),
                                    (vv, wd_sp[("d", g, 1)][1])]
                        i_ = 0
                        for i3 in range(2):
                            wdv = wdb[:, i3 * 2 * D:(i3 + 1) * 2 * D]
                            a_h = actb[:, i3 * T + tt * TT: i3 * T + tt * TT + TT]
                            a_l = actb[:, 2 * T + i3 * T + tt * TT: 2 * T + i3 * T + tt * TT + TT]
                            for lhs, rhs in ((wdv[:, o * 128:(o + 1) * 128], a_h),
                                             (wdv[:, o * 128:(o + 1) * 128], a_l),
                                             (wdv[:, D + o * 128:D + (o + 1) * 128], a_h)):
                                p.op("tensor", lambda eng, lhs=lhs, rhs=rhs, i_=i_, bank=bank:
                                     eng.matmul(ps[bank][:, 0:TT], lhs, rhs,
                                                start=(i_ == 0), stop=(i_ == 5)),
                                     waits=wdn if i_ == 0 else (),
                                     inc=(mm, 1) if i_ == 5 else None)
                                i_ += 1
                        dn_mm = mm.count
                        xvv = xT[:, o * T + tt * TT: o * T + tt * TT + TT]
                        stw = [(mm, dn_mm)]
                        if g == 0:
                            stw.append((vv, ln1_done))
                        p.op("vector", lambda eng, xvv=xvv, bank=bank:
                             eng.scalar_tensor_tensor(
                                 xvv, ps[bank][:, 0:TT], 0.5, xvv,
                                 ALU.mult, ALU.add),
                             waits=stw, inc=(vv, 1))
                        self._free[bank] = [(vv, vv.count)]
                wd_marks[g] = mm.count
                act_free = [(mm, mm.count)]
                if g + 1 < CH // 2:
                    sched_wd(g + 1, [(mm, wd_marks[g])])
        x1_done = vv.count
        p.dma_ch("gpsimd", dr["x1_out"].rearrange("(c p) t -> p c t", p=128),
                 xT[:, :].rearrange("p (c t) -> p c t", c=CD),
                 s_x1, waits=[(vv, x1_done)])
        x1dma.append(s_x1.count)

        def xh(c, t):
            return xTb[:, 2 * (c * T + t * TT): 2 * (c * T + t * TT) + TT]

        def xl(c, t):
            return xTb[:, 2 * (c * T + t * TT) + TT: 2 * (c * T + t * TT) + 2 * TT]

        ha_ready = []
        for tile in range(NT):
            mark, _ = ln(xv(tile), lambda c: lnss[:, 0:TT], EPS_LN,
                         wait0=[(s_x1, x1dma[-1])],
                         split=(lambda tile: lambda c, rdy: split_chunk(
                             c, lnss[:, 0:TT], xh(c, tile), xl(c, tile), rdy))(tile))
            ha_ready.append(vv.count)

        stage_bufs = (wd_s, h1[:, 12288:16384])
        weff_bufs = (h1b[:, 16384:16384 + CD * SLAB],
                     h1b[:, 20480:20480 + CD * SLAB])
        stage_ld = {}
        stage_read = {}
        weff_done = {}
        wb_read = {}

        def load_slab(s):
            w = []
            if s - 2 in stage_read:
                w.append((aa, stage_read[s - 2]))
            if s == 0:
                w.append((vv, ha_ready[-1]))
            sem = s_st[s % 2]
            p.dma_ch("sync",
                     stage_bufs[s % 2][:, 0:CD * SLAB].rearrange("p (c n) -> p c n", c=CD),
                     dr["wqkvT"][:, s * SLAB:(s + 1) * SLAB].rearrange("(c p) n -> p c n", p=128),
                     sem, waits=w)
            stage_ld[s] = (sem, sem.count)

        p.dma_ch("sync", sel16, dr["sel16_in"], s_in,
                 waits=[(s_x1, x1dma[-1])])
        wstw = self._slot_free.get(0, []) + self._slot_free.get(1, [])
        p.dma_ch("sync", cos2, dr["cos_t"], s_in, waits=wstw)
        p.dma_ch("sync", sin2, dr["sin_t"], s_in)
        p.dma_ch("sync", rotc, dr["rotc_in"], s_in)
        p.dma_ch("sync", eye_s, dr["eye_in"], s_in)
        sel16_ld = s_in.count
        load_slab(0)
        load_slab(1)

        for s in range(NSLAB):
            par = s % 2
            stage = stage_bufs[par]
            weff = weff_bufs[par]
            w = [stage_ld[s]]
            if s - 1 in wb_read:
                w.append((vv, wb_read[s - 1]))
            if s == 0:
                w.append((s_x1, x1dma[-1]))
            p.op("scalar", lambda eng, stage=stage: eng.copy(
                wbb[:, :], stage[:, 0:CD * SLAB]), waits=w, inc=(aa, 1))
            p.op("scalar", lambda eng, stage=stage: eng.activation(
                wab[:, :], stage[:, 0:CD * SLAB], AF.Abs), inc=(aa, 1))
            stage_read[s] = aa.count
            if s + 2 < NSLAB:
                load_slab(s + 2)
            if getattr(self, "debug_weff", False):
                p.op("vector", lambda eng, stage=stage: eng.tensor_copy(
                    h1[:, 7936:8192], stage[:, 0:256]), waits=[(aa, aa.count)], inc=(vv, 1))
                p.dma_ch("gpsimd", dr["dbg_wb"][s][:, 0:256], h1[:, 7936:8192],
                         s_dbg, waits=[(vv, vv.count)])
                self._weff_dump[s * 100 + 50] = s_dbg.count
            wab_ready = vv.count
            for c in range(CD):
                p.op("tensor", lambda eng, c=c: eng.matmul(
                    ps[PS_S0][0:16, 0:SLAB], selcol[:, (CD - 1 - c):(2 * CD - 1 - c)],
                    wab[:, c * SLAB:(c + 1) * SLAB],
                    start=(c == 0), stop=(c == CD - 1)),
                    waits=([(aa, stage_read[s])] + self._free.get(PS_S0, []))
                    if c == 0 else (), inc=(mm, 1))
            gsum_mm = mm.count
            p.op("vector", lambda eng: eng.tensor_scalar_mul(
                sclb, ps[PS_S0][0:16, 0:SLAB], 1.0 / 128.0),
                waits=[(mm, gsum_mm)], inc=(vv, 1))
            self._free[PS_S0] = [(vv, vv.count)]
            p.op("vector", lambda eng: eng.tensor_scalar_max(
                sclb, sclb, 1e-8), inc=(vv, 1))
            scl_done = vv.count
            for c in range(CD):
                p.op("tensor", lambda eng, c=c: eng.matmul(
                    ps[PS_B1][:, 0:SLAB], sel16[:, c * 128:(c + 1) * 128],
                    sclb, start=True, stop=True),
                    waits=([(vv, scl_done), (s_in, sel16_ld)] if c == 0 else [])
                    + self._free.get(PS_B1, []), inc=(mm, 1))
                bc = mm.count
                p.op("vector", lambda eng: eng.reciprocal(
                    inv_bc, ps[PS_B1][:, 0:SLAB]), waits=[(mm, bc)], inc=(vv, 1))
                p.op("vector", lambda eng, c=c: eng.tensor_mul(
                    rb, wbb[:, c * SLAB:(c + 1) * SLAB], inv_bc),
                    inc=(vv, 1))
                p.op("vector", lambda eng: eng.tensor_single_scalar(
                    g1, rb, 0.5, ALU.is_gt), inc=(vv, 1))
                p.op("vector", lambda eng: eng.tensor_single_scalar(
                    g2, rb, -0.5, ALU.is_lt), inc=(vv, 1))
                p.op("vector", lambda eng: eng.tensor_sub(g1, g1, g2), inc=(vv, 1))
                p.op("vector", lambda eng: eng.tensor_mul(
                    g2, g1, ps[PS_B1][:, 0:SLAB]), inc=(vv, 1))
                self._free[PS_B1] = [(vv, vv.count)]
                p.op("vector", lambda eng, c=c: eng.tensor_sub(
                    rb, g2, wbb[:, c * SLAB:(c + 1) * SLAB]), inc=(vv, 1))
                wwf = []
                if c == 0 and s - 2 in weff_done:
                    wwf.append((mm, weff_done[s - 2]))
                    if self._weff_dump:
                        wwf.append((s_dbg, max(self._weff_dump.values())))
                p.op("vector", lambda eng, c=c, weff=weff: eng.tensor_add(
                    weff[:, c * SLAB:(c + 1) * SLAB],
                    wbb[:, c * SLAB:(c + 1) * SLAB], rb),
                    waits=wwf, inc=(vv, 1))
            wb_read[s] = vv.count
            weff_ready = vv.count
            if getattr(self, "debug_weff", False):
                p.dma_ch("gpsimd", dr["dbg_weff"][s], weff[:, 0:CD * SLAB],
                         s_dbg, waits=[(vv, weff_ready)])
                self._weff_dump[s] = s_dbg.count

            for tile in range(NT):
                for oc in range(2):
                    bank = PS_GATE if oc == 0 else PS_UP
                    og = s * SLAB // 128 + oc
                    i_ = 0
                    for k in range(CD):
                        wt = weff[:, k * SLAB + oc * 128: k * SLAB + oc * 128 + 128]
                        for rhs in (xh(k, tile), xl(k, tile)):
                            p.op("tensor",
                                 lambda eng, wt=wt, rhs=rhs, bank=bank, i_=i_:
                                 eng.matmul(
                                     ps[bank][:, 0:TT], wt, rhs,
                                     start=(i_ == 0), stop=(i_ == 2 * CD - 1)),
                                 waits=[(vv, weff_ready), (vv, vv.count), (aa, aa.count),
                                        (vv, ha_ready[tile])]
                                 if i_ == 0 else (), inc=(mm, 1) if i_ == 2 * CD - 1 else None)
                            i_ += 1
                    qkv_mm = mm.count
                    self._head(p, dr, og, tile, bank, qkv_mm, locals())
            weff_done[s] = mm.count

        p.op("gpsimd", lambda eng: eng.engine_nop(), waits=[(s_out, s_out.count), (s_x1, s_x1.count)])

    def _head(self, p, dr, og, tile, bank, qkv_mm, env):
        T, TT, NH, NKV = self.T, self.TT, self.NH, self.NKV
        ps = env["ps"]
        PS_D0, PS_D1, PS_S1, PS_B1 = 2, 3, 5, 7
        s_out, mm, vv, aa = (env[k] for k in ("s_out", "mm", "vv", "aa"))
        qraw, ro, sq = env["qraw"], env["ro"], env["sq"]
        cos2, sin2, rotc = env["cos2"], env["sin2"], env["rotc"]
        gain_s, eye_s = env["gain_s"], env["eye_s"]
        ones_col, ones_row = env["ones_col"], env["ones_row"]
        A, B, C = env["A"], env["B"], env["C"]
        s_in, lds0 = env["s_in"], env["lds0"]
        cosv = cos2[:, tile * TT:(tile + 1) * TT]
        sinv = sin2[:, tile * TT:(tile + 1) * TT]
        rstd_from_B = env["rstd_from_B"]

        p.op("vector", lambda eng: eng.tensor_copy(qraw, ps[bank][:, 0:TT]),
             waits=[(mm, qkv_mm)], inc=(vv, 1))
        evac = vv.count

        if og < NH:
            h, kind = og, "q"
        elif og < NH + NKV:
            h, kind = og - NH, "k"
        else:
            h, kind = og - NH - NKV, "v"

        if kind == "v":
            for st in range(TT // 128):
                p.op("tensor", lambda eng, st=st: eng.transpose(
                    ps[PS_D0][:, st * 128:(st + 1) * 128],
                    qraw[:, st * 128:(st + 1) * 128], eye_s[:, :]),
                    waits=([(vv, evac)] + self._free.get(PS_D0, []))
                    if st == 0 else (),
                    inc=(mm, 1) if st == TT // 128 - 1 else None)
            tmark = mm.count
            p.op("vector", lambda eng: eng.tensor_copy(ro, ps[PS_D0][:, 0:TT]),
                 waits=[(mm, tmark)], inc=(vv, 1))
            self._free[PS_D0] = [(vv, vv.count)]
            rmark = vv.count
            for st in range(TT // 128):
                p.dma_ch("gpsimd",
                         dr["v_out"][h][tile * TT + st * 128: tile * TT + (st + 1) * 128, :],
                         ro[:, st * 128:(st + 1) * 128],
                         s_out, waits=[(vv, rmark)] if st == 0 else ())
            return

        wsq = [(mm, self._sq_marks[0])] if 0 in self._sq_marks else []
        p.op("scalar", lambda eng: eng.activation(
            sq[:, 0:TT], qraw, AF.Square),
            waits=wsq + [(vv, evac)], inc=(aa, 1))
        p.op("tensor", lambda eng: eng.matmul(
            ps[PS_S1][0:1, 0:TT], ones_col, sq[:, 0:TT],
            start=True, stop=True),
            waits=[(aa, aa.count)] + self._free.get(PS_S1, []), inc=(mm, 1))
        self._sq_marks[0] = mm.count
        smm = mm.count
        p.op("tensor", lambda eng: eng.matmul(
            ps[PS_D1][:, 0:TT], rotc[:, :], qraw, start=True, stop=True),
            waits=[(s_in, lds0), (vv, evac)] + self._free.get(PS_D1, []), inc=(mm, 1))
        rxm = mm.count
        p.op("vector", lambda eng: eng.tensor_scalar_mul(
            B, ps[PS_S1][0:1, 0:TT], 1.0 / 128.0),
            waits=[(mm, smm)], inc=(vv, 1))
        self._free[PS_S1] = [(vv, vv.count)]
        rstd_from_B(EPS_RMS, TT)
        if kind == "q":
            p.op("vector", lambda eng: eng.tensor_scalar_mul(
                C, C, RSQRT_HD), inc=(vv, 1))
            p.op("vector", lambda eng, h=h: eng.tensor_scalar_mul(
                C, C, gain_s[0:1, h:h + 1]), inc=(vv, 1))
        rstd_done = vv.count
        p.op("tensor", lambda eng: eng.matmul(
            ps[PS_B1][:, 0:TT], ones_row, C, start=True, stop=True),
            waits=[(vv, rstd_done)] + self._free.get(PS_B1, []), inc=(mm, 1))
        bmark = mm.count
        p.op("vector", lambda eng: eng.tensor_mul(ro, qraw, cosv), inc=(vv, 1))
        p.op("vector", lambda eng: eng.tensor_mul(qraw, ps[PS_D1][:, 0:TT], sinv),
             waits=[(mm, rxm)], inc=(vv, 1))
        self._free[PS_D1] = [(vv, vv.count)]
        p.op("vector", lambda eng: eng.tensor_add(ro, ro, qraw), inc=(vv, 1))
        p.op("vector", lambda eng: eng.tensor_mul(ro, ro, ps[PS_B1][:, 0:TT]),
             waits=[(mm, bmark)], inc=(vv, 1))
        self._free[PS_B1] = [(vv, vv.count)]
        omark = vv.count
        out = dr["q_out"] if kind == "q" else dr["k_out"]
        p.dma_ch("gpsimd", out[h][:, tile * TT:(tile + 1) * TT], ro,
                 s_out, waits=[(vv, omark)])



class L2:

    def __init__(self, S=8192, NHL=2):
        self.S = S
        self.NHL = NHL

    def build(self):
        S, NHL = self.S, self.NHL
        NQB = S // 512
        NC = S // 128
        nc = bass.Bass("TRN2", target_bir_lowering=False, debug=False,
                       detect_race_conditions=False)
        dr = {}
        dr["q2"] = nc.dram_tensor("q2", [NHL, 128, S], F32, kind="ExternalInput").ap()
        dr["k2"] = nc.dram_tensor("k2", [128, S], F32, kind="ExternalInput").ap()
        dr["v2"] = nc.dram_tensor("v2", [S, 128], F32, kind="ExternalInput").ap()
        dr["y2"] = nc.dram_tensor("y2", [NHL, 128, S], F32, kind="ExternalOutput").ap()

        with contextlib.ExitStack() as es:
            p = Prog(nc, es)
            q2 = p.sbuf("q2s", [128, NHL * S], F32)
            k2 = p.sbuf("k2s", [128, S], F32)
            v2 = p.sbuf("v2s", [128, S], F32)
            q2b, k2b, v2b = q2.bitcast(BF16), k2.bitcast(BF16), v2.bitcast(BF16)
            ee = p.sbuf("ee", [128, 3 * 512], F32)
            eeb = p.sbuf("eeb", [128, 3 * 1024], BF16)
            bc = p.sbuf("bc", [128, 512], F32)
            av = p.sbuf("av", [128, 2 * 512], F32)
            rr = p.sbuf("rr", [1, 512], F32)
            consts = p.sbuf("c2", [128, 134], F32)
            bnc = p.sbuf("bnc", [128, 512], F32)
            ps = [p.psum(f"ps{i}", [128, 512], F32) for i in range(8)]
            PS_AV0, PS_AV1, PS_R0, PS_R1, PS_S0, PS_S1, PS_B, PS_S2 = range(8)
            PS_S = (PS_S0, PS_S1, PS_S2)

            s_in = p.sem("s_in")
            s_kv = p.sem("s_kv")
            s_out = p.sem("s_out")
            mm = p.sem("mms")
            vv = p.sem("vvs")
            aa = p.sem("aas")
            gg = p.sem("ggs")

            s_v2 = p.sem("s_v2")
            s_q1 = p.sem("s_q1")
            p.dma_ch("gpsimd", k2[:, :], dr["k2"], s_kv)
            p.dma_ch("sync", q2[:, 0:S], dr["q2"][0], s_in)
            p.dma_ch("gpsimd", v2[:, :].rearrange("p (c n) -> p c n", c=NC),
                     dr["v2"].rearrange("(c p) n -> p c n", p=128), s_v2)
            p.dma_ch("sync", q2[:, S:2 * S], dr["q2"][1], s_q1)
            lds = s_in.count
            p.op("vector", lambda eng: eng.memset(consts[:, 0:1], 1.0), inc=(vv, 1))
            p.op("vector", lambda eng: eng.memset(consts[0:1, 1:129], 1.0), inc=(vv, 1))
            p.op("vector", lambda eng: eng.memset(
                consts[:, 129:130].bitcast(BF16), 1.0), inc=(vv, 1))
            cready = vv.count
            ones_col = consts[:, 0:1]
            ones_row = consts[0:1, 1:129]
            ones_b = consts[:, 129:130].bitcast(BF16)[:, 0:1]

            def ksp(c):
                return (k2b[:, 2 * c * 128:2 * c * 128 + 128],
                        k2b[:, 2 * c * 128 + 128:2 * c * 128 + 256])

            def vsp(c):
                return (v2b[:, 2 * c * 128:2 * c * 128 + 128],
                        v2b[:, 2 * c * 128 + 128:2 * c * 128 + 256])

            def qsp(h, qb):
                o = 2 * (h * self.S + qb * 512)
                return (q2b[:, o:o + 512], q2b[:, o + 512:o + 1024])

            def split_inplace(f32view, hi, lo, w0):
                w = f32view.shape[1]
                p.op("vector", lambda eng, w=w: eng.tensor_copy(bnc[:, 0:w], f32view),
                     waits=w0 + [(aa, aa.count)], inc=(vv, 1))
                p.op("scalar", lambda eng, w=w: eng.copy(hi, bnc[:, 0:w]),
                     waits=[(vv, vv.count)], inc=(aa, 1))
                p.op("vector", lambda eng, w=w: eng.tensor_sub(lo, bnc[:, 0:w], hi),
                     waits=[(aa, aa.count)], inc=(vv, 1))

            kspm = {}
            vspm = {}
            qspm = {}

            def split_items(qb):
                items = []
                for c in range(4 * qb, 4 * qb + 4):
                    def it(c=c, qb=qb):
                        kh, kl = ksp(c)
                        split_inplace(k2[:, c * 128:(c + 1) * 128], kh, kl,
                                      [(s_kv, 16)] if c == 0 else [])
                        kspm[c] = vv.count
                    items.append(it)
                for c in range(4 * qb, 4 * qb + 4):
                    def it(c=c, qb=qb):
                        vh, vl = vsp(c)
                        split_inplace(v2[:, c * 128:(c + 1) * 128], vh, vl,
                                      [(s_v2, 16)] if c == 0 else [])
                        vspm[c] = vv.count
                    items.append(it)
                for h in range(NHL):
                    def it(h=h, qb=qb):
                        qh, ql = qsp(h, qb)
                        split_inplace(
                            q2[:, h * self.S + qb * 512:h * self.S + (qb + 1) * 512],
                            qh, ql,
                            [(s_in, 16)] if (qb == 0 and h == 0)
                            else ([(s_q1, 16)] if (qb == 0 and h == 1) else []))
                        qspm[(h, qb)] = vv.count
                    items.append(it)
                return items

            for it in split_items(0) + split_items(1):
                it()
            pending_splits = []
            for qb in range(2, NQB):
                pending_splits.extend(split_items(qb))

            ee_free = {}
            sc_free = {}
            grp_free = {}
            for gi, (qb, h) in enumerate([(qb, h) for qb in range(NQB) for h in range(NHL)]):
                gpar = gi % 2
                AVB = PS_AV0 if gpar == 0 else PS_AV1
                RB = PS_R0 if gpar == 0 else PS_R1
                njc = 4 * qb + 4
                qh, ql = qsp(h, qb)
                smarks = {}

                def emit_score(j, qb=qb, h=h, qh=qh, ql=ql, smarks=smarks):
                    sb = PS_S[j % 3]
                    kh, kl = ksp(j)
                    w0 = [(vv, kspm[j]), (vv, qspm[(h, qb)])] + sc_free.get(sb, [])
                    i_ = 0
                    for lhs, rhs in ((kh, qh), (kh, ql), (kl, qh)):
                        p.op("tensor", lambda eng, lhs=lhs, rhs=rhs, sb=sb, i_=i_:
                             eng.matmul(ps[sb][:, 0:512], lhs, rhs,
                                        start=(i_ == 0), stop=(i_ == 2)),
                             waits=w0 if i_ == 0 else (),
                             inc=(mm, 1) if i_ == 2 else None)
                        i_ += 1
                    smarks[j] = mm.count

                for j in range(min(3, njc)):
                    emit_score(j)
                for j in range(njc):
                    sb = PS_S[j % 3]
                    vh, vl = vsp(j)
                    par = j % 3
                    ev = ee[:, par * 512:(par + 1) * 512]
                    eh = eeb[:, par * 1024:par * 1024 + 512]
                    el = eeb[:, par * 1024 + 512:par * 1024 + 1024]
                    p.op("scalar", lambda eng, ev=ev, sb=sb: eng.activation(
                        ev, ps[sb][:, 0:512], AF.Exp),
                        waits=[(mm, smarks[j])] + ee_free.get(par, []), inc=(aa, 1))
                    sc_free[sb] = [(aa, aa.count)]
                    emark = [(aa, aa.count)]
                    d = j - 4 * qb
                    if d >= 0:
                        p.op("gpsimd", lambda eng, ev=ev, d=d: eng.affine_select(
                            ev, ev, [[1, 512]], ALU.is_ge, 0.0,
                            base=-128 * d, channel_multiplier=-1),
                            waits=emark, inc=(gg, 1))
                        emark = [(gg, gg.count)]
                    p.op("scalar", lambda eng, eh=eh, ev=ev: eng.copy(eh, ev),
                         waits=emark, inc=(aa, 1))
                    p.op("vector", lambda eng, el=el, ev=ev, eh=eh: eng.tensor_sub(
                        el, ev, eh), waits=[(aa, aa.count)], inc=(vv, 1))
                    esp = [(vv, vv.count)]
                    p.op("tensor", lambda eng, eh=eh, j=j, njc=njc, RB=RB: eng.matmul(
                        ps[RB][0:1, 0:512], ones_b, eh,
                        start=(j == 0), stop=False),
                        waits=esp + (grp_free.get(gpar, []) if j == 0 else []),
                        inc=(mm, 1))
                    p.op("tensor", lambda eng, el=el, j=j, njc=njc, RB=RB: eng.matmul(
                        ps[RB][0:1, 0:512], ones_b, el,
                        start=False, stop=(j == njc - 1)), inc=(mm, 1))
                    i_ = 0
                    for lhs, rhs in ((vh, eh), (vh, el), (vl, eh)):
                        p.op("tensor", lambda eng, lhs=lhs, rhs=rhs, j=j, njc=njc,
                             AVB=AVB, i_=i_: eng.matmul(
                                 ps[AVB][:, 0:512], lhs, rhs,
                                 start=(j == 0 and i_ == 0),
                                 stop=(j == njc - 1 and i_ == 2)),
                             waits=[(vv, vspm[j])] if i_ == 0 else (),
                             inc=(mm, 1))
                        i_ += 1
                    ee_free[par] = [(mm, mm.count)]
                    if j + 3 <= njc - 1:
                        emit_score(j + 3)
                    if pending_splits:
                        pending_splits.pop(0)()
                avm = mm.count
                avv = av[:, gpar * 512:(gpar + 1) * 512]
                p.op("vector", lambda eng, RB=RB: eng.reciprocal(
                    rr[:, :], ps[RB][0:1, 0:512]),
                    waits=[(mm, avm)] + grp_free.get(1 - gpar, [])[0:0], inc=(vv, 1))
                p.op("tensor", lambda eng: eng.matmul(
                    ps[PS_B][:, 0:512], ones_row, rr[:, :], start=True, stop=True),
                    waits=[(vv, vv.count)], inc=(mm, 1))
                p.op("vector", lambda eng: eng.tensor_copy(bc[:, :], ps[PS_B][:, 0:512]),
                     waits=[(mm, mm.count)], inc=(vv, 1))
                p.op("vector", lambda eng, avv=avv, AVB=AVB: eng.tensor_mul(
                    avv, ps[AVB][:, 0:512], bc[:, :]), inc=(vv, 1))
                nmark = vv.count
                p.dma_ch("gpsimd", dr["y2"][h][:, qb * 512:(qb + 1) * 512], avv,
                         s_out, waits=[(vv, nmark)])
                grp_free[gpar] = [(vv, nmark), (s_out, s_out.count)]
            p.op("gpsimd", lambda eng: eng.engine_nop(), waits=[(s_out, s_out.count)])
            p.build()
        return nc



class L3(L1):

    def build(self):
        T, TT, D, H = self.T, self.TT, self.D, self.H
        CD = self.CD
        nc = bass.Bass("TRN2", target_bir_lowering=False, debug=False,
                       detect_race_conditions=False)
        dr = {}
        dr["x_fm"] = nc.dram_tensor("x_fm", [D, T], F32, kind="ExternalInput").ap()
        dr["y_fm"] = nc.dram_tensor("y_fm", [D, T], F32, kind="ExternalInput").ap()
        dr["wgT"] = nc.dram_tensor("wgT", [D, H], F32, kind="ExternalInput").ap()
        dr["wuT"] = nc.dram_tensor("wuT", [D, H], F32, kind="ExternalInput").ap()
        dr["wdT"] = nc.dram_tensor("wdT", [H, D], F32, kind="ExternalInput").ap()
        dr["wpT"] = nc.dram_tensor("wpT", [D, D], F32, kind="ExternalInput").ap()
        dr["sel16_in"] = nc.dram_tensor("sel16_in", [16, 16 * 128], BF16, kind="ExternalInput").ap()
        dr["selc_in"] = nc.dram_tensor("selc_in", [128, 2 * CD - 1], BF16, kind="ExternalInput").ap()
        dr["x3_out"] = nc.dram_tensor("x3_out", [D, T], F32, kind="ExternalOutput").ap()
        if getattr(self, "dbg_weff3", False):
            dr["dbg_w"] = nc.dram_tensor("dbg_w", [D // 256, 128, 16 * 256], BF16,
                                         kind="ExternalOutput").ap()
        with contextlib.ExitStack() as es:
            p = Prog(nc, es)
            self._body3(nc, p, dr)
            p.build()
        return nc

    def _body3(self, nc, p, dr):
        T, TT, D, H = self.T, self.TT, self.D, self.H
        CD, CH, NT = self.CD, self.CH, self.NT
        SLAB = 256
        NSLAB = D // SLAB

        xT = p.sbuf("xT", [128, CD * T], F32)
        h1 = p.sbuf("h1", [128, CD * T], F32)
        act = p.sbuf("act", [128, 2 * T], F32)
        wg_s = p.sbuf("wg_s", [128, D], F32)
        wu_s = p.sbuf("wu_s", [128, D], F32)
        wd_s = p.sbuf("wd_s", [128, 2 * D], F32)
        wst = p.sbuf("wst", [128, 2 * D], F32)
        lnss = p.sbuf("lnss", [128, TT], F32)
        asc = p.sbuf("asc", [128, TT], F32)
        sq = p.sbuf("sq", [128, TT], F32)
        stats = p.sbuf("stats", [128, 2 * TT], F32)
        consts = p.sbuf("consts", [128, 144], F32)
        selcol = p.sbuf("selcol", [128, 2 * CD - 1], BF16)
        qscr = p.sbuf("qscr", [128, 3072], F32)
        h1b = h1.bitcast(BF16)
        actb = act.bitcast(BF16)
        wgb = wg_s.bitcast(BF16)
        wub = wu_s.bitcast(BF16)
        wdb = wd_s.bitcast(BF16)
        weff_bufs = (wgb[:, 0:CD * SLAB], wub[:, 0:CD * SLAB])
        wbb = qscr[:, 0:2048].bitcast(BF16)
        rb = qscr[:, 2048:2176].bitcast(BF16)
        g1 = qscr[:, 2176:2432]
        g2 = qscr[:, 2432:2688]
        sclb = qscr[0:16, 2688:2816].bitcast(BF16)
        inv_bc = qscr[:, 2816:3072]
        sqb = sq.bitcast(BF16)
        wab_bufs = (sqb[:, 0:SLAB], sqb[:, SLAB:2 * SLAB])
        sel16 = stats[0:16, 0:1024].bitcast(BF16)

        ps = [p.psum(f"ps{i}", [128, 512], F32) for i in range(8)]
        PS_GATE, PS_UP, PS_D0, PS_D1, PS_S0, PS_S1, PS_B0, PS_B1 = range(8)

        s_in = p.sem("s_in")
        s_st = [p.sem("s_st0"), p.sem("s_st1")]
        s_y = p.sem("s_y")
        s_x1 = p.sem("s_x1")
        mm = p.sem("mms")
        vv = p.sem("vvs")
        aa = p.sem("aas")
        s_wl = [p.sem("s_wl0"), p.sem("s_wl1")]

        self._sq_marks = {}
        self._free = {}
        self._slot_free = {}
        self._slot_seq = 0
        self._asc_free = 0
        self._act_free = {}

        p.dma_ch("sync", xT[:, :].rearrange("p (c t) -> p c t", c=CD),
                 dr["x_fm"].rearrange("(c p) t -> p c t", p=128), s_in)
        p.dma_ch("sync", selcol[:, :], dr["selc_in"], s_in)
        lds0 = s_in.count
        p.op("vector", lambda eng: eng.memset(consts[:, 0:1], 1.0), inc=(vv, 1))
        p.op("vector", lambda eng: eng.memset(consts[0:1, 1:129], 1.0), inc=(vv, 1))
        consts_ready = vv.count
        ones_col = consts[:, 0:1]
        ones_row = consts[0:1, 1:129]
        A = stats[0:1, 0:TT]
        B = stats[0:1, TT:2 * TT]
        C = stats[0:1, 0:TT]
        NSC = lnss[0:1, 0:TT]

        def rstd_from_B(extra_eps, width):
            p.op("vector", lambda eng: eng.tensor_scalar_add(
                B[:, 0:width], B[:, 0:width], extra_eps), inc=(vv, 1))
            p.op("vector", lambda eng: eng.reciprocal(
                C[:, 0:width], B[:, 0:width]), inc=(vv, 1))
            p.op("scalar", lambda eng: eng.activation(
                C[:, 0:width], C[:, 0:width], AF.Sqrt),
                waits=[(vv, vv.count)], inc=(aa, 1))
            p.op("scalar", lambda eng: eng.activation(
                NSC[:, 0:width], C[:, 0:width], AF.Square), inc=(aa, 1))
            p.op("vector", lambda eng: eng.tensor_mul(
                NSC[:, 0:width], NSC[:, 0:width], B[:, 0:width]),
                waits=[(aa, aa.count)], inc=(vv, 1))
            p.op("vector", lambda eng: eng.tensor_scalar(
                NSC[:, 0:width], NSC[:, 0:width], -0.5, 1.5, ALU.mult, ALU.add),
                inc=(vv, 1))
            p.op("vector", lambda eng: eng.tensor_mul(
                C[:, 0:width], C[:, 0:width], NSC[:, 0:width]), inc=(vv, 1))
            return vv.count

        def ln(src_view, dst_view, eps, wait0=(), n_chunks=CD, width=TT, split=None):
            w0 = list(wait0) + [(vv, consts_ready)]
            ndim = n_chunks * 128
            for c in range(n_chunks):
                wsq = [(mm, self._sq_marks[0])] if 0 in self._sq_marks else []
                p.op("scalar", lambda eng, c=c: eng.activation(
                    sq[:, 0:width], src_view(c), AF.Square),
                    waits=wsq + (w0 if c == 0 else []), inc=(aa, 1))
                p.op("tensor", lambda eng, c=c: eng.matmul(
                    ps[PS_S0][0:1, 0:width], ones_col, src_view(c),
                    start=(c == 0), stop=(c == n_chunks - 1)),
                    waits=(w0 + self._free.get(PS_S0, [])) if c == 0 else (),
                    inc=(mm, 1))
                p.op("tensor", lambda eng, c=c: eng.matmul(
                    ps[PS_S1][0:1, 0:width], ones_col, sq[:, 0:width],
                    start=(c == 0), stop=(c == n_chunks - 1)),
                    waits=[(aa, aa.count)] + (self._free.get(PS_S1, []) if c == 0 else []),
                    inc=(mm, 1))
                self._sq_marks[0] = mm.count
            stats_mm = mm.count
            w = [(mm, stats_mm)]
            p.op("vector", lambda eng: eng.tensor_scalar_mul(
                A[:, 0:width], ps[PS_S0][0:1, 0:width], 1.0 / ndim),
                waits=w, inc=(vv, 1))
            self._free[PS_S0] = [(vv, vv.count)]
            p.op("vector", lambda eng: eng.tensor_scalar_mul(
                B[:, 0:width], ps[PS_S1][0:1, 0:width], 1.0 / ndim),
                waits=w, inc=(vv, 1))
            self._free[PS_S1] = [(vv, vv.count)]
            mean_done = vv.count
            p.op("tensor", lambda eng: eng.matmul(
                ps[PS_B0][:, 0:width], ones_row, A[:, 0:width],
                start=True, stop=True),
                waits=[(vv, mean_done)] + self._free.get(PS_B0, []), inc=(mm, 1))
            mbc = mm.count
            p.op("scalar", lambda eng: eng.activation(
                NSC[:, 0:width], A[:, 0:width], AF.Square),
                waits=[(vv, mean_done)], inc=(aa, 1))
            p.op("vector", lambda eng: eng.tensor_sub(
                B[:, 0:width], B[:, 0:width], NSC[:, 0:width]),
                waits=[(aa, aa.count), (mm, mbc)], inc=(vv, 1))
            rstd_done = rstd_from_B(eps, width)
            p.op("tensor", lambda eng: eng.matmul(
                ps[PS_B1][:, 0:width], ones_row, C[:, 0:width],
                start=True, stop=True),
                waits=[(vv, rstd_done)] + self._free.get(PS_B1, []), inc=(mm, 1))
            bca = mm.count
            extra_w = []
            for c in range(n_chunks):
                p.op("vector", lambda eng, c=c: eng.tensor_sub(
                    dst_view(c), src_view(c), ps[PS_B0][:, 0:width]),
                    waits=([(mm, bca)] if c == 0 else []) + extra_w, inc=(vv, 1))
                p.op("vector", lambda eng, c=c: eng.tensor_mul(
                    dst_view(c), dst_view(c), ps[PS_B1][:, 0:width]), inc=(vv, 1))
                if split is not None:
                    extra_w, _ = split(c, vv.count)
            self._free[PS_B0] = [(vv, vv.count)]
            self._free[PS_B1] = [(vv, vv.count)]
            return vv.count, bca

        def xv(tile):
            return lambda c: xT[:, c * T + tile * TT: c * T + (tile + 1) * TT]

        def split_chunk(c, src_ap, hi_ap, lo_ap, ready_mark):
            p.op("scalar", lambda eng: eng.copy(hi_ap, src_ap),
                 waits=[(vv, ready_mark)], inc=(aa, 1))
            him = aa.count
            p.op("vector", lambda eng: eng.tensor_sub(lo_ap, src_ap, hi_ap),
                 waits=[(aa, him)], inc=(vv, 1))
            return [(aa, him), (vv, vv.count)], vv.count

        def h1h(c, tt):
            return h1b[:, c * T + tt * TT: c * T + tt * TT + TT]

        def h1l(c, tt):
            return h1b[:, CD * T + c * T + tt * TT: CD * T + c * T + tt * TT + TT]

        def ynh(c, tt):
            return h1b[:, 2 * (c * T + tt * TT): 2 * (c * T + tt * TT) + TT]

        def ynl(c, tt):
            return h1b[:, 2 * (c * T + tt * TT) + TT: 2 * (c * T + tt * TT) + 2 * TT]

        def yv(tile):
            return lambda c: h1[:, c * T + tile * TT: c * T + (tile + 1) * TT]

        def stage_load(dram_slab, pair_view, key, marks, split_waits):
            slot = self._slot_seq % 2
            self._slot_seq += 1
            sem = s_wl[slot]
            stage = wst[:, slot * D:(slot + 1) * D]
            p.dma_ch("sync", stage.rearrange("p (c n) -> p c n", c=CD), dram_slab,
                     sem, waits=self._slot_free.get(slot, []))
            ld = (sem, sem.count)
            p.op("scalar", lambda eng, stage=stage, pv=pair_view: eng.copy(
                pv[:, 0:D], stage), waits=[ld] + list(split_waits), inc=(aa, 1))
            him = aa.count
            p.op("vector", lambda eng, stage=stage, pv=pair_view: eng.tensor_sub(
                pv[:, D:2 * D], stage, pv[:, 0:D]),
                waits=[(aa, him)], inc=(vv, 1))
            self._slot_free[slot] = [(vv, vv.count)]
            marks[key] = (him, vv.count)

        gu_marks = {}
        wd_marks = {}
        gu_sp = {}
        wd_sp = {}

        def sched_gu(h, wgw, wuw):
            stage_load(dr["wgT"][:, h * 128:(h + 1) * 128].rearrange("(c p) n -> p c n", p=128),
                       wgb[:, 0:2 * D], ("g", h), gu_sp, wgw)
            stage_load(dr["wuT"][:, h * 128:(h + 1) * 128].rearrange("(c p) n -> p c n", p=128),
                       wub[:, 0:2 * D], ("u", h), gu_sp, wuw)

        def sched_wd(g, ww):
            for i in range(2):
                h = 2 * g + i
                stage_load(dr["wdT"][h * 128:(h + 1) * 128, :].rearrange("p (c n) -> p c n", c=CD),
                           wdb[:, i * 2 * D:(i + 1) * 2 * D], ("d", g, i), wd_sp, ww)

        p.dma_ch("sync", h1[:, :].rearrange("p (c t) -> p c t", c=CD),
                 dr["y_fm"].rearrange("(c p) t -> p c t", p=128), s_y)
        yld = s_y.count
        for tile in range(NT):
            for c in range(CD):
                wsq = [(mm, self._sq_marks[0])] if 0 in self._sq_marks else []
                p.op("scalar", lambda eng, c=c, tile=tile: eng.activation(
                    sq[:, 0:TT], yv(tile)(c), AF.Square),
                    waits=wsq + ([(s_y, yld), (vv, consts_ready)] if (c == 0 and tile == 0) else []),
                    inc=(aa, 1))
                p.op("tensor", lambda eng, c=c: eng.matmul(
                    ps[PS_S1][0:1, 0:TT], ones_col, sq[:, 0:TT],
                    start=(c == 0), stop=(c == CD - 1)),
                    waits=[(aa, aa.count)] + (self._free.get(PS_S1, []) if c == 0 else []),
                    inc=(mm, 1))
                self._sq_marks[0] = mm.count
            smm = mm.count
            p.op("vector", lambda eng: eng.tensor_scalar_mul(
                B, ps[PS_S1][0:1, 0:TT], 1.0 / D), waits=[(mm, smm)], inc=(vv, 1))
            self._free[PS_S1] = [(vv, vv.count)]
            rstd_done = rstd_from_B(EPS_RMS, TT)
            p.op("tensor", lambda eng: eng.matmul(
                ps[PS_B1][:, 0:TT], ones_row, C, start=True, stop=True),
                waits=[(vv, rstd_done)] + self._free.get(PS_B1, []), inc=(mm, 1))
            bca = mm.count
            extra_w = []
            for c in range(CD):
                p.op("vector", lambda eng, c=c, tile=tile: eng.tensor_mul(
                    lnss[:, 0:TT], yv(tile)(c), ps[PS_B1][:, 0:TT]),
                    waits=([(mm, bca)] if c == 0 else []) + extra_w, inc=(vv, 1))
                extra_w, _ = split_chunk(c, lnss[:, 0:TT], ynh(c, tile), ynl(c, tile),
                                         vv.count)
            self._free[PS_B1] = [(vv, vv.count)]
        yn_ready = vv.count
        p.dma_ch("sync", sel16, dr["sel16_in"], s_in, waits=[(vv, yn_ready)])
        sel16_ld = s_in.count

        pstage_bufs = (wd_s, wst)
        stage_ld = {}
        stage_read = {}
        weff_done = {}
        wb_read = {}

        def load_slab(s):
            w = []
            if s - 2 in stage_read:
                w.append((aa, stage_read[s - 2]))
            elif s < 2:
                w.append((s_in, lds0))
            sem = s_st[s % 2]
            p.dma_ch("sync",
                     pstage_bufs[s % 2][:, 0:CD * SLAB].rearrange("p (c n) -> p c n", c=CD),
                     dr["wpT"][:, s * SLAB:(s + 1) * SLAB].rearrange("(c p) n -> p c n", p=128),
                     sem, waits=w)
            stage_ld[s] = (sem, sem.count)

        load_slab(0)
        load_slab(1)
        P_B = (PS_GATE, PS_UP, PS_D0, PS_D1)
        for s in range(NSLAB):
            par = s % 2
            stage = pstage_bufs[par]
            weff = weff_bufs[par]
            w = [stage_ld[s]]
            if s - 1 in wb_read:
                w.append((vv, wb_read[s - 1]))
            p.op("scalar", lambda eng, stage=stage: eng.copy(
                wbb[:, :], stage[:, 0:CD * SLAB]), waits=w, inc=(aa, 1))
            gs_marks = []
            for c in range(CD):
                wabv = wab_bufs[c % 2]
                aw = []
                if c == 0 and s == 0:
                    aw.append((mm, self._sq_marks[0]))
                if len(gs_marks) >= 2:
                    aw.append((mm, gs_marks[-2]))
                p.op("scalar", lambda eng, c=c, stage=stage, wabv=wabv: eng.activation(
                    wabv, stage[:, c * SLAB:(c + 1) * SLAB], AF.Abs),
                    waits=aw, inc=(aa, 1))
                abm = aa.count
                p.op("tensor", lambda eng, c=c, wabv=wabv: eng.matmul(
                    ps[PS_S0][0:16, 0:SLAB], selcol[:, (CD - 1 - c):(2 * CD - 1 - c)],
                    wabv, start=(c == 0), stop=(c == CD - 1)),
                    waits=[(aa, abm)] + (self._free.get(PS_S0, []) if c == 0 else []),
                    inc=(mm, 1))
                gs_marks.append(mm.count)
            gsum_mm = mm.count
            self._sq_marks[0] = gsum_mm
            stage_read[s] = aa.count
            if s + 2 < NSLAB:
                load_slab(s + 2)
            p.op("vector", lambda eng: eng.tensor_scalar_mul(
                sclb, ps[PS_S0][0:16, 0:SLAB], 1.0 / 128.0),
                waits=[(mm, gsum_mm)], inc=(vv, 1))
            self._free[PS_S0] = [(vv, vv.count)]
            p.op("vector", lambda eng: eng.tensor_scalar_max(
                sclb, sclb, 1e-8), inc=(vv, 1))
            scl_done = vv.count
            for c in range(CD):
                p.op("tensor", lambda eng, c=c: eng.matmul(
                    ps[PS_B1][:, 0:SLAB], sel16[:, c * 128:(c + 1) * 128],
                    sclb, start=True, stop=True),
                    waits=([(vv, scl_done), (s_in, sel16_ld)] if c == 0 else [])
                    + self._free.get(PS_B1, []), inc=(mm, 1))
                bc = mm.count
                p.op("vector", lambda eng: eng.reciprocal(
                    inv_bc, ps[PS_B1][:, 0:SLAB]), waits=[(mm, bc)], inc=(vv, 1))
                p.op("vector", lambda eng, c=c: eng.tensor_mul(
                    rb, wbb[:, c * SLAB:(c + 1) * SLAB], inv_bc), inc=(vv, 1))
                p.op("vector", lambda eng: eng.tensor_single_scalar(
                    g1, rb, 0.5, ALU.is_gt), inc=(vv, 1))
                p.op("vector", lambda eng: eng.tensor_single_scalar(
                    g2, rb, -0.5, ALU.is_lt), inc=(vv, 1))
                p.op("vector", lambda eng: eng.tensor_sub(g1, g1, g2), inc=(vv, 1))
                p.op("vector", lambda eng: eng.tensor_mul(
                    g2, g1, ps[PS_B1][:, 0:SLAB]), inc=(vv, 1))
                self._free[PS_B1] = [(vv, vv.count)]
                p.op("vector", lambda eng, c=c: eng.tensor_sub(
                    rb, g2, wbb[:, c * SLAB:(c + 1) * SLAB]), inc=(vv, 1))
                wwf = [(mm, weff_done[s - 2])] if (c == 0 and s - 2 in weff_done) else []
                p.op("vector", lambda eng, c=c, weff=weff: eng.tensor_add(
                    weff[:, c * SLAB:(c + 1) * SLAB],
                    wbb[:, c * SLAB:(c + 1) * SLAB], rb),
                    waits=wwf, inc=(vv, 1))
            wb_read[s] = vv.count
            weff_ready = vv.count
            for oc in range(2):
                og = s * 2 + oc
                for tile in range(NT):
                    bank = P_B[oc * 2 + tile]
                    i_ = 0
                    for k in range(CD):
                        wt = weff[:, k * SLAB + oc * 128: k * SLAB + oc * 128 + 128]
                        for rhs in (ynh(k, tile), ynl(k, tile)):
                            p.op("tensor",
                                 lambda eng, wt=wt, rhs=rhs, bank=bank, i_=i_:
                                 eng.matmul(ps[bank][:, 0:TT], wt, rhs,
                                            start=(i_ == 0), stop=(i_ == 2 * CD - 1)),
                                 waits=([(vv, weff_ready), (vv, yn_ready)]
                                        + self._free.get(bank, [])) if i_ == 0 else (),
                                 inc=(mm, 1) if i_ == 2 * CD - 1 else None)
                            i_ += 1
                    pmm = mm.count
                    p.op("vector", lambda eng, og=og, tile=tile, bank=bank:
                         eng.tensor_add(xv(tile)(og), xv(tile)(og), ps[bank][:, 0:TT]),
                         waits=[(mm, pmm), (s_in, lds0)], inc=(vv, 1))
                    self._free[bank] = [(vv, vv.count)]
            weff_done[s] = mm.count
        proj_mm = mm.count
        proj_done = vv.count
        last_sr = [(aa, stage_read[NSLAB - 1]), (aa, stage_read[NSLAB - 2])]
        self._slot_free = {0: list(last_sr), 1: list(last_sr)}

        sched_gu(0, [(mm, proj_mm)], [(mm, proj_mm)])
        sched_wd(0, [(mm, proj_mm), (aa, stage_read[NSLAB - 2])])

        ha1 = []
        for tile in range(NT):
            mark, _ = ln(xv(tile), lambda c: lnss[:, 0:TT], EPS_LN,
                         wait0=([(vv, proj_done)] if tile == 0 else []),
                         split=(lambda tile: lambda c, rdy: split_chunk(
                             c, lnss[:, 0:TT], h1h(c, tile), h1l(c, tile), rdy))(tile))
            ha1.append(vv.count)
        ln1_done = vv.count

        G_B = (PS_GATE, PS_UP)
        U_B = (PS_D0, PS_D1)
        D_B = (PS_S0, PS_S1, PS_B0, PS_B1)

        asc_bufs = (asc[:, 0:TT], lnss[:, 0:TT])
        asc_free = {0: [], 1: []}
        act_free = []
        act_ready = {}

        for h in range(CH):
            g, i2 = h // 2, h % 2
            wgv = wgb[:, 0:2 * D]
            wuv = wub[:, 0:2 * D]
            gum = {}
            for wv, banks, nm in ((wgv, G_B, "g"), (wuv, U_B, "u")):
                for tt in range(NT):
                    bank = banks[tt]
                    i_ = 0
                    for k in range(CD):
                        for lhs, rhs in ((wv[:, k * 128:(k + 1) * 128], h1h(k, tt)),
                                         (wv[:, k * 128:(k + 1) * 128], h1l(k, tt)),
                                         (wv[:, D + k * 128:D + (k + 1) * 128], h1h(k, tt))):
                            p.op("tensor", lambda eng, lhs=lhs, rhs=rhs, bank=bank, i_=i_:
                                 eng.matmul(ps[bank][:, 0:TT], lhs, rhs,
                                            start=(i_ == 0), stop=(i_ == 3 * CD - 1)),
                                 waits=([(vv, gu_sp[(nm, h)][1]), (vv, ha1[tt])]
                                        + self._free.get(bank, [])) if i_ == 0 else (),
                                 inc=(mm, 1) if i_ == 3 * CD - 1 else None)
                            i_ += 1
                    gum[(nm, tt)] = mm.count
            gu_marks[h] = mm.count
            if h + 1 < CH:
                sched_gu(h + 1, [(mm, gum[("g", NT - 1)])], [(mm, gu_marks[h])])
            for tt in range(NT):
                ascv = asc_bufs[tt]
                p.op("scalar", lambda eng, ascv=ascv, tt=tt: eng.activation(
                    ascv, ps[G_B[tt]][:, 0:TT], AF.Silu),
                    waits=[(mm, gum[("g", tt)])] + asc_free[tt], inc=(aa, 1))
                self._free[G_B[tt]] = [(aa, aa.count)]
                p.op("vector", lambda eng, ascv=ascv, tt=tt: eng.tensor_mul(
                    ascv, ascv, ps[U_B[tt]][:, 0:TT]),
                    waits=[(mm, gum[("u", tt)]), (aa, aa.count)], inc=(vv, 1))
                self._free[U_B[tt]] = [(vv, vv.count)]
                ach = actb[:, i2 * T + tt * TT: i2 * T + tt * TT + TT]
                acl = actb[:, 2 * T + i2 * T + tt * TT: 2 * T + i2 * T + tt * TT + TT]
                p.op("scalar", lambda eng, ach=ach, ascv=ascv: eng.copy(ach, ascv),
                     waits=[(vv, vv.count)] + act_free, inc=(aa, 1))
                p.op("vector", lambda eng, ach=ach, acl=acl, ascv=ascv: eng.tensor_sub(
                    acl, ascv, ach), waits=[(aa, aa.count)], inc=(vv, 1))
                asc_free[tt] = [(vv, vv.count)]
                act_ready[(i2, tt)] = vv.count
            if i2 == 1:
                for o in range(CD):
                    for tt in range(NT):
                        bank = D_B[(o % 2) * 2 + tt]
                        wdn = list(self._free.get(bank, []))
                        if o == 0 and tt == 0:
                            wdn += [(vv, act_ready[(0, 0)]), (vv, act_ready[(0, 1)]),
                                    (vv, act_ready[(1, 0)]), (vv, act_ready[(1, 1)]),
                                    (vv, wd_sp[("d", g, 0)][1]),
                                    (vv, wd_sp[("d", g, 1)][1])]
                        i_ = 0
                        for i3 in range(2):
                            wdv = wdb[:, i3 * 2 * D:(i3 + 1) * 2 * D]
                            a_h = actb[:, i3 * T + tt * TT: i3 * T + tt * TT + TT]
                            a_l = actb[:, 2 * T + i3 * T + tt * TT: 2 * T + i3 * T + tt * TT + TT]
                            for lhs, rhs in ((wdv[:, o * 128:(o + 1) * 128], a_h),
                                             (wdv[:, o * 128:(o + 1) * 128], a_l),
                                             (wdv[:, D + o * 128:D + (o + 1) * 128], a_h)):
                                p.op("tensor", lambda eng, lhs=lhs, rhs=rhs, i_=i_, bank=bank:
                                     eng.matmul(ps[bank][:, 0:TT], lhs, rhs,
                                                start=(i_ == 0), stop=(i_ == 5)),
                                     waits=wdn if i_ == 0 else (),
                                     inc=(mm, 1) if i_ == 5 else None)
                                i_ += 1
                        dn_mm = mm.count
                        xvv = xT[:, o * T + tt * TT: o * T + tt * TT + TT]
                        stw = [(mm, dn_mm)]
                        if g == 0:
                            stw.append((vv, ln1_done))
                        p.op("vector", lambda eng, xvv=xvv, bank=bank:
                             eng.scalar_tensor_tensor(
                                 xvv, ps[bank][:, 0:TT], 0.5, xvv,
                                 ALU.mult, ALU.add),
                             waits=stw, inc=(vv, 1))
                        self._free[bank] = [(vv, vv.count)]
                wd_marks[g] = mm.count
                act_free = [(mm, mm.count)]
                if g + 1 < CH // 2:
                    sched_wd(g + 1, [(mm, wd_marks[g])])
        x3_done = vv.count
        p.dma_ch("gpsimd", dr["x3_out"].rearrange("(c p) t -> p c t", p=128),
                 xT[:, :].rearrange("p (c t) -> p c t", c=CD),
                 s_x1, waits=[(vv, x3_done)])
        p.op("gpsimd", lambda eng: eng.engine_nop(), waits=[(s_x1, s_x1.count)])



NCORE = 8
SEQ = 8192
DM = 2048
TCORE = SEQ // NCORE


def _consts():
    eye = np.eye(128, dtype=np.float32)
    import ml_dtypes
    sel16 = np.zeros((16, 16 * 128), dtype=ml_dtypes.bfloat16)
    for c in range(16):
        sel16[c, c * 128:(c + 1) * 128] = 1.0
    CD = DM // 128
    selc = np.zeros((128, 2 * CD - 1), dtype=ml_dtypes.bfloat16)
    for pp in range(128):
        selc[pp, CD - 1] = 1.0
    R = np.zeros((128, 128), dtype=np.float32)
    for i in range(64):
        R[i, 64 + i] = 1.0
        R[64 + i, i] = -1.0
    rotc = np.ascontiguousarray(R.T)
    return eye, sel16, selc, rotc


_ROPE_CACHE = {}


def _rope_tables_full():
    if "t" not in _ROPE_CACHE:
        import jax
        import jax.numpy as jnp
        cpu = jax.devices("cpu")[0]
        with jax.default_device(cpu):
            inv = 1.0 / 10000.0 ** (jnp.arange(0, 128, 2, dtype=jnp.float32) / 128)
            t = jnp.arange(SEQ, dtype=jnp.float32)
            fr = jnp.outer(t, inv)
            c = np.asarray(jnp.cos(fr)).T.astype(np.float32)
            s = np.asarray(jnp.sin(fr)).T.astype(np.float32)
        _ROPE_CACHE["t"] = (c, s)
    return _ROPE_CACHE["t"]


def _rope_tables(pos):
    c, s = _rope_tables_full()
    cc = np.ascontiguousarray(c[:, pos])
    ss = np.ascontiguousarray(s[:, pos])
    return np.concatenate([cc, cc]), np.concatenate([ss, ss])


def kernel(L0_attn_x, L0_attn_w_qkv, L0_attn_w_proj, L0_attn_q_gain,
           L0_ffn_w_gate, L0_ffn_w_up, L0_ffn_w_down):
    from concourse.bass_utils import run_bass_kernel_spmd

    x = np.ascontiguousarray(np.asarray(L0_attn_x, dtype=np.float32)[0].T)
    wgT = np.ascontiguousarray(np.asarray(L0_ffn_w_gate, np.float32).T)
    wuT = np.ascontiguousarray(np.asarray(L0_ffn_w_up, np.float32).T)
    wdT = np.ascontiguousarray(np.asarray(L0_ffn_w_down, np.float32).T)
    wqkvT = np.ascontiguousarray(np.asarray(L0_attn_w_qkv, np.float32).T)
    wpT = np.ascontiguousarray(np.asarray(L0_attn_w_proj, np.float32).T)
    gain = np.asarray(L0_attn_q_gain, np.float32)[None, :]
    eye, sel16, selc, rotc = _consts()

    im1 = []
    for c in range(NCORE):
        pos = np.arange(c * TCORE, (c + 1) * TCORE)
        cos, sin = _rope_tables(pos)
        im1.append({
            "x_fm": np.ascontiguousarray(x[:, c * TCORE:(c + 1) * TCORE]),
            "wgT": wgT, "wuT": wuT, "wdT": wdT, "wqkvT": wqkvT,
            "cos_t": cos, "sin_t": sin, "gain_in": gain,
            "eye_in": eye, "sel16_in": sel16, "selc_in": selc, "rotc_in": rotc,
        })
    nc1 = L1().build()
    r1 = run_bass_kernel_spmd(nc1, im1, core_ids=list(range(NCORE))).results

    q_all = np.concatenate([r["q_out"] for r in r1], axis=2)
    k_all = np.concatenate([r["k_out"] for r in r1], axis=2)
    v_all = np.concatenate([r["v_out"] for r in r1], axis=1)

    im2 = []
    for c in range(NCORE):
        kv = c // 2
        im2.append({
            "q2": np.ascontiguousarray(q_all[2 * c:2 * c + 2]),
            "k2": np.ascontiguousarray(k_all[kv]),
            "v2": np.ascontiguousarray(v_all[kv]),
        })
    nc2 = L2().build()
    r2 = run_bass_kernel_spmd(nc2, im2, core_ids=list(range(NCORE))).results
    y_all = np.concatenate([r["y2"] for r in r2], axis=0)
    y_fm = y_all.reshape(DM, SEQ)

    im3 = []
    for c in range(NCORE):
        im3.append({
            "x_fm": r1[c]["x1_out"],
            "y_fm": np.ascontiguousarray(y_fm[:, c * TCORE:(c + 1) * TCORE]),
            "wgT": wgT, "wuT": wuT, "wdT": wdT, "wpT": wpT,
            "sel16_in": sel16, "selc_in": selc,
        })
    nc3 = L3().build()
    r3 = run_bass_kernel_spmd(nc3, im3, core_ids=list(range(NCORE))).results
    out = np.concatenate([r["x3_out"].T for r in r3], axis=0)
    return np.ascontiguousarray(out[None, :, :])
